# version 22
# baseline (speedup 1.0000x reference)
import numpy as np
import ml_dtypes
from contextlib import ExitStack
import concourse.bass as bass
import concourse.mybir as mybir
from concourse.bass_utils import run_bass_kernel_spmd
from concourse.alu_op_type import AluOpType as ALU

F32 = mybir.dt.float32
BF16 = mybir.dt.bfloat16
U8 = mybir.dt.uint8
AF = mybir.ActivationFunctionType

B, T, D = 2, 4096, 2048
H, HKV, DDS, DDG, DV = 16, 4, 32, 32, 64
MBK, LW = 64, 1024
REMOTE = T - LW
NPOOL = REMOTE // MBK
NDUM = 16
KEYS = NPOOL + NDUM + LW
KEYSP = 1152
NKT = 9
KTSZ = [128] * 8 + [64]
KTOF = [128 * i for i in range(NKT)]
TQ = 1024
NQT = 8
KD = 16
CH = 256
NCH = T // CH
NM_UP = 64
ROPE_BASE = 10000.0
INTERLEAVE = True
WJ = [1088, 64, 64, 64, 64, 64, 576, 1088] if INTERLEAVE else [KEYS] * 8


def kt_list(w):
    out = [128] * (w // 128)
    if w % 128:
        out.append(w % 128)
    return out
EPS = 1e-5
LN32H = -0.5 * float(np.log(32.0))

ENGINES = ("sync", "gpsimd", "tensor", "vector", "scalar")


class Prog:

    def __init__(self, nc, slots=()):
        self.nc = nc
        self.records = {e: [] for e in ENGINES}
        self.counts = {e: 0 for e in ENGINES}
        self.events = {}
        self.sems = {}
        self.slots = list(slots)
        self.slot_evs = {sl: [] for sl in slots}
        self.highwater = {}
        self.elide = False
        self._n = 0

    def make_sems(self, stack):
        for e in ENGINES:
            self.sems[e] = stack.enter_context(self.nc.semaphore(f"sb_{e}"))
        for sl in self.slots:
            self.sems[sl] = stack.enter_context(self.nc.semaphore(f"dq_{sl}"))
            self.counts[sl] = 0

    def op(self, eng, fn, wait=(), ev=None, dma=False, slot=None):
        recs = self.records[eng]
        if isinstance(wait, str):
            wait = [wait]
        need = {}
        for w in wait:
            if w is None:
                continue
            we, wv = self.events[w]
            if need.get(we, -1) < wv:
                need[we] = wv
        hw = self.highwater.setdefault(eng, {})
        for we, wv in need.items():
            if self.elide and hw.get(we, -1) >= wv:
                continue
            if hw.get(we, -1) < wv:
                hw[we] = wv
            recs.append(("wait", we, wv))
        if ev is None:
            ev = f"_e{self._n}"
            self._n += 1
        if dma:
            assert slot is not None, "dma ops need a slot"
            self.counts[slot] += 16
            self.events[ev] = (slot, self.counts[slot])
            self.slot_evs[slot].append(ev)
            recs.append(("op", fn, (slot, 16)))
        else:
            self.counts[eng] += 1
            self.events[ev] = (eng, self.counts[eng])
            recs.append(("op", fn, (eng, 1)))
        return ev

    def finalize_slot(self, slot):
        total = self.counts[slot]
        for ev in self.slot_evs[slot]:
            self.events[ev] = (slot, total)

    def barrier(self):
        heads = [(nm, c) for nm, c in self.counts.items() if c > 0]
        for e in ENGINES:
            hw = self.highwater.setdefault(e, {})
            for we, wv in heads:
                if we != e and hw.get(we, -1) < wv:
                    hw[we] = wv
                    self.records[e].append(("wait", we, wv))

    def final_wait(self, eng, evs):
        need = {}
        for w in evs:
            we, wv = self.events[w]
            if need.get(we, -1) < wv:
                need[we] = wv
        for we, wv in need.items():
            self.records[eng].append(("wait", we, wv))

    def emit(self, block):
        for eng in ENGINES:
            if not self.records[eng]:
                continue

            def make(eng):
                def body(e):
                    for rec in self.records[eng]:
                        if rec[0] == "wait":
                            e.wait_ge(self.sems[rec[1]], rec[2])
                        else:
                            semname, inc = rec[2]
                            rec[1](e).then_inc(self.sems[semname], inc)
                return body

            getattr(block, eng)(make(eng))


class Buf:

    def __init__(self):
        self.writes = []
        self.readers = []

    def w(self):
        return self.readers + self.writes

    def wrote(self, ev):
        self.writes.append(ev)
        self.readers = []

    def r(self):
        return list(self.writes)

    def read(self, ev):
        self.readers.append(ev)



def rope_tables(Tlen):
    inv = 1.0 / ROPE_BASE ** (np.arange(0, DDG, 2, dtype=np.float64) / DDG)
    ang = np.arange(Tlen, dtype=np.float64)[:, None] * inv[None, :]
    return np.cos(ang).astype(np.float32), np.sin(ang).astype(np.float32)


def n_visible(q):
    npool = min(NPOOL, (q - 63) // 64 + 1) if q >= 63 else 0
    return npool, max(0, q - (REMOTE - 1))


def host_shared_weights(W):
    def tileKM(Wm, nk, nm):
        return np.ascontiguousarray(Wm.reshape(nk, 128, nm, 128).transpose(2, 1, 0, 3))

    wq_cat = np.concatenate(
        [np.concatenate([W['Wq_sem'][:, h * DDS:(h + 1) * DDS],
                         W['Wq_geo'][:, h * DDG:(h + 1) * DDG]], 1) for h in range(H)], 1)
    wk_cat = np.concatenate(
        [np.concatenate([W['Wk_sem'][:, j * DDS:(j + 1) * DDS],
                         W['Wk_geo'][:, j * DDG:(j + 1) * DDG]], 1) for j in range(HKV)], 1)
    cos_k, sin_k = rope_tables(T)
    spm_k = np.concatenate([-sin_k, sin_k], 1)
    ktc = np.ascontiguousarray(cos_k.reshape(32, 128, 16).transpose(1, 0, 2).reshape(128, 512))
    kts = np.ascontiguousarray(spm_k.reshape(32, 128, 32).transpose(1, 0, 2).reshape(128, 1024))
    ls = np.asarray(W['logit_scale'], np.float32)
    lsrep = np.ascontiguousarray(
        np.stack([np.repeat(ls[2 * i:2 * i + 2], 64) for i in range(8)], axis=1))
    poolm = np.zeros((128, 2), np.float32)
    poolm[0:64, 0] = 1.0 / MBK
    poolm[64:128, 1] = 1.0 / MBK
    return {
        "wq": np.ascontiguousarray(wq_cat.reshape(KD, 128, 1024)),
        "wk": np.ascontiguousarray(wk_cat.reshape(KD, 128, 256)),
        "wv": np.ascontiguousarray(np.asarray(W['Wv'], np.float32).reshape(KD, 128, 256)),
        "wo": np.ascontiguousarray(np.asarray(W['Wo'], np.float32).reshape(8, 128, 2048)),
        "wg": np.ascontiguousarray(np.asarray(W['gate_w'], np.float32).reshape(KD, 128, 8)),
        "wu": tileKM(np.asarray(W['up_w'], np.float32), KD, NM_UP),
        "wd": tileKM(np.asarray(W['down_w'], np.float32), NM_UP, KD),
        "we": tileKM(np.asarray(W['expert_weight'], np.float32), KD, KD),
        "ident": np.eye(128, dtype=np.float32),
        "lsrep": lsrep,
        "poolm": poolm,
        "ktc": ktc, "kts": kts,
        "gammab": np.ascontiguousarray(
            np.broadcast_to(np.asarray(W['ln_gamma'], np.float32)[None, :], (128, D))),
        "betab": np.ascontiguousarray(
            np.broadcast_to(np.asarray(W['ln_beta'], np.float32)[None, :], (128, D))),
    }


def host_core_inputs(core, x_full, shared):
    b = core // 4
    cc = core % 4
    if INTERLEAVE:
        qpos = np.concatenate([np.arange((cc + 4 * j) * 128, (cc + 4 * j) * 128 + 128)
                               for j in range(NQT)])
    else:
        qpos = np.arange(cc * TQ, cc * TQ + TQ)
    cos_k, sin_k = rope_tables(T)
    spm_k = np.concatenate([-sin_k, sin_k], 1)
    qtc = np.ascontiguousarray(cos_k[qpos].reshape(8, 128, 16).transpose(1, 0, 2).reshape(128, 128))
    qts = np.ascontiguousarray(spm_k[qpos].reshape(8, 128, 32).transpose(1, 0, 2).reshape(128, 256))
    bm = np.full((TQ, KEYS), -1e9, np.float32)
    bm[:, NPOOL:NPOOL + NDUM] = -2e9
    for i, q in enumerate(qpos):
        npool, nloc = n_visible(q)
        bm[i, :npool] = 1e9
        bm[i, NPOOL + NDUM:NPOOL + NDUM + nloc] = 1e9
    d = dict(shared)
    d["xb"] = np.ascontiguousarray(x_full[b])
    d["xq"] = np.ascontiguousarray(x_full[b][qpos])
    d["qtc"] = qtc
    d["qts"] = qts
    d["bmask"] = np.ascontiguousarray(bm.reshape(NQT, 128, KEYS))
    return d, qpos



def build_program(dbg=False):
    nc = bass.Bass("TRN2", target_bir_lowering=False)
    IN = {}
    for name, shape in [
        ("xb", [T, D]), ("xq", [TQ, D]),
        ("wq", [KD, 128, 1024]), ("wk", [KD, 128, 256]), ("wv", [KD, 128, 256]),
        ("wo", [8, 128, 2048]), ("wg", [KD, 128, 8]),
        ("wu", [NM_UP, 128, KD, 128]), ("wd", [KD, 128, NM_UP, 128]),
        ("we", [KD, 128, KD, 128]),
        ("ident", [128, 128]), ("lsrep", [128, 8]), ("poolm", [128, 2]),
        ("ktc", [128, 512]), ("kts", [128, 1024]), ("qtc", [128, 128]), ("qts", [128, 256]),
        ("bmask", [NQT, 128, KEYS]), ("gammab", [128, D]), ("betab", [128, D]),
    ]:
        IN[name] = nc.dram_tensor(name, shape, F32, kind="ExternalInput").ap()
    out = nc.dram_tensor("out", [TQ, D], F32, kind="ExternalOutput").ap()
    x1sp = nc.dram_tensor("x1sp", [TQ, D], F32, kind="Internal").ap()
    xlosp = nc.dram_tensor("xlosp", [NQT, 128, D], BF16, kind="Internal").ap()
    moesp = nc.dram_tensor("moesp", [KD, 128, TQ], F32, kind="Internal").ap()
    DBG = {}
    if dbg:
        for name, shape in [("d_kt", [4, 128, KEYSP]), ("d_v2", [128, NKT * 256]),
                            ("d_qt", [128, NQT * 1024]), ("d_x1", [TQ, D]),
                            ("d_ot", [128, 8 * 1024]), ("d_xn2", [TQ, D]),
                            ("d_scale", [128, NQT]), ("d_y", [TQ, D]),
                            ("d_p", [128, KEYS]), ("d_h", [128, 1024]),
                            ("d_yb", [128, KD * 1024]), ("d_moe", [KD, 128, TQ]),
                            ("d_xs", [128, KD * 1024]), ("d_sd", [128, 8 * 128]),
                            ("d_we", [KD, 128, KD * 128]), ("d_mr", [KD, 128, TQ])]:
            DBG[name] = nc.dram_tensor(name, shape, F32, kind="ExternalOutput").ap()

    SLOTS = [*[f'ini{i}' for i in range(1, 12)], 'dbg', 'xch0', 'xch1', 'wob', 'bm0', 'bm1', 'xq0', 'xq1',
             'x1s0', 'x1s1', 'we0', 'we1', 'ms0', 'ms1', 'wu0', 'wu1',
             'mr0', 'mr1', 'wd0', 'wd1', 'wd2', 'wd3', 'gam', 'bet', 'x1r0', 'x1r1',
             'out0', 'out1', 'xch2', 'xlo0', 'xlo1']
    pr = Prog(nc, slots=SLOTS)
    stack = ExitStack()
    bufs = {}

    def gb(name):
        if name not in bufs:
            bufs[name] = Buf()
        return bufs[name]

    with stack:
        tabs = stack.enter_context(nc.sbuf_tensor("tabs", [128, 1920], F32))
        ktc_v, kts_v = tabs[:, 0:512], tabs[:, 512:1536]
        qtc_v, qts_v = tabs[:, 1536:1664], tabs[:, 1664:1920]
        identb = stack.enter_context(nc.sbuf_tensor("identb", [128, 128], BF16))
        poolb = stack.enter_context(nc.sbuf_tensor("poolb", [128, 2], BF16))
        wgf = stack.enter_context(nc.sbuf_tensor("wgf", [128, KD * 8], F32))
        wgh = stack.enter_context(nc.sbuf_tensor("wgh", [128, KD * 8], BF16))
        wgl = stack.enter_context(nc.sbuf_tensor("wgl", [128, KD * 8], BF16))
        svec = stack.enter_context(nc.sbuf_tensor("svec", [128, 8], F32))
        lsb = stack.enter_context(nc.sbuf_tensor("lsb", [128, 8], F32))
        stat = stack.enter_context(nc.sbuf_tensor("stat", [128, 96], F32))
        LNC = [dict(bn=8, mv=32, rstd=34), dict(bn=40, mv=64, rstd=66)]
        ARN = 203392
        arena = stack.enter_context(nc.sbuf_tensor("arena", [128, ARN], U8))

        def av(off, n, dt):
            sz = 4 if dt == F32 else 2
            assert off % 4 == 0 and off + n * sz <= ARN, (off, n, sz)
            return arena[:, off:off + n * sz].bitcast(dt)

        K = 1024
        KT = [av(114 * K + j * 2 * KEYSP, KEYSP, BF16) for j in range(4)]
        V2 = av(114 * K + 9216, NKT * 256, BF16)
        QT = av(114 * K + 9216 + 4608, NQT * 1024, BF16)
        xch = [av(0 + i * 16 * K, 2 * 2048, F32) for i in range(2)] + [av(166 * K, 2 * 2048, F32)]
        xnb = [av(32 * K + i * 8 * K, 2 * 2048, BF16) for i in range(2)] + [av(182 * K, 2 * 2048, BF16)]
        xnT = [av(48 * K + i * 8 * K, KD * CH, BF16) for i in range(2)] + [av(190 * K, KD * CH, BF16)]
        wqb = av(64 * K, KD * 1024, BF16)
        wkb = av(96 * K, KD * 256, BF16)
        wvb = av(104 * K, KD * 256, BF16)
        kbf = [av(144 * K + i * K, 2 * 256, BF16) for i in range(2)] + [av(112 * K, 2 * 256, BF16)]
        kswp = [av(146 * K + i * 512, 2 * 128, BF16) for i in range(2)] + [av(113 * K, 2 * 128, BF16)]
        xnpT = av(148 * K, KD * 48, BF16)
        qfb = [av(150 * K + i * 4 * K, 2 * 1024, BF16) for i in range(2)]
        qswp = [av(158 * K + i * 2 * K, 2 * 512, BF16) for i in range(2)]
        bmv = [av(0 + i * 5 * K, KEYS, F32) for i in range(2)]
        negS = [av(10 * K + i * 5 * K, KEYS, F32) for i in range(2)]
        Pb = [av(20 * K + i * 3 * K, KEYSP, BF16) for i in range(2)]
        PTs = [av(26 * K + i * 3 * K, KEYSP, BF16) for i in range(2)] + [av(106 * K, KEYSP, BF16)]
        diagb = [av(32 * K + i * 512, 128, BF16) for i in range(4)]
        OT = av(34 * K, 8 * 1024, BF16)
        wob = av(50 * K, 8 * 2048, BF16)
        xqt = [av(82 * K + i * 8 * K, 2048, F32) for i in range(2)]
        x1t = [av(176 * K + i * 8 * K, 2048, F32) for i in range(2)]
        xlot = [av(98 * K + i * 4 * K, 2048, BF16) for i in range(2)]
        xn2 = av(144 * K, NQT * 2048, BF16)
        xn2f = [av(64 * K + i * 8 * K, 2048, F32) for i in range(2)]
        xn2lo = av(0, NQT * 2048, BF16)
        xn2T = av(32 * K, KD * 1024, BF16)
        xn2sT = av(0, KD * 1024, BF16)
        xloT = [av(64 * K + i * 4 * K, KD * 128, BF16) for i in range(2)]
        wdb = [av(32 * K + i * 8 * K, 32 * 128, BF16) for i in range(4)]
        logitsT = av(82 * K, 1024, BF16)
        sdiag = [av(84 * K + j * 512, 128, BF16) for j in range(NQT)]
        hbuf = av(ARN - 131072, NM_UP * 1024, BF16)
        web = [av(90 * K + i * 4 * K, KD * 128, BF16) for i in range(2)]
        wub = [av(0 + i * 4 * K, KD * 128, BF16) for i in range(2)]
        moer_m = [av(64 * K + i * 4 * K, 1024, F32) for i in range(2)]
        moer = [av(64 * K, 1024, F32) for i in range(2)]
        assert 64 * K + 4096 <= ARN - 131072
        ybuf = av(0, KD * 1024, BF16)
        ytok = av(72 * K, NQT * 2048, F32)
        gamv = av(136 * K, 2048, F32)
        betv = av(144 * K, 2048, F32)
        x1r = [av(152 * K + i * 8 * K, 2048, F32) for i in range(2)]
        outt = [av(168 * K + i * 8 * K, 2048, F32) for i in range(2)]

        PSA = stack.enter_context(nc.psum_tensor("PSA", [128, 4096], F32))

        def pv(c0, n, p0=0, p1=128):
            return PSA[p0:p1, c0:c0 + n]

        def pbanks(c0, n):
            return [gb(f"bank{i}") for i in range(c0 // 512, (c0 + n + 511) // 512)]

        pr.make_sems(stack)

        def E(eng, fn, reads=(), writes=(), extra=(), dma=False, slot=None):
            wait = list(extra)
            for bnames in reads:
                wait += gb(bnames).r() if isinstance(bnames, str) else bnames.r()
            for bnames in writes:
                wait += gb(bnames).w() if isinstance(bnames, str) else bnames.w()
            ev = pr.op(eng, fn, wait=wait, dma=dma, slot=slot)
            for bnames in reads:
                (gb(bnames) if isinstance(bnames, str) else bnames).read(ev)
            for bnames in writes:
                (gb(bnames) if isinstance(bnames, str) else bnames).wrote(ev)
            return ev

        def MMG(items, reads, writes):
            wait = []
            for b in reads:
                wait += gb(b).r() if isinstance(b, str) else b.r()
            for b in writes:
                wait += gb(b).w() if isinstance(b, str) else b.w()
            for i, fn in enumerate(items[:-1]):
                pr.op('tensor', fn, wait=wait if i == 0 else ())
            ev = pr.op('tensor', items[-1], wait=wait if len(items) == 1 else ())
            for b in reads:
                (gb(b) if isinstance(b, str) else b).read(ev)
            for b in writes:
                (gb(b) if isinstance(b, str) else b).wrote(ev)
            return ev

        with nc.Block() as block:
            E('vector', lambda e: e.memset(stat[:, 0:1], EPS), writes=['st_eps'])
            E('vector', lambda e: e.memset(stat[:, 1:2], LN32H), writes=['st_ln32'])
            for j in range(4):
                E('vector', lambda e, j=j: e.memset(KT[j][:, :], 0.0), writes=[f'KT{j}'])
            E('vector', lambda e: e.memset(V2[:, :], 0.0), writes=['V2'])
            E('gpsimd', lambda e: e.dma_start(out=identb[:], in_=IN['ident']), dma=True, slot='ini1', writes=['identb'])
            E('gpsimd', lambda e: e.dma_start(out=poolb[:], in_=IN['poolm']), dma=True, slot='ini2', writes=['poolb'])
            E('sync', lambda e: e.dma_start(out=wgf[:], in_=IN['wg'].rearrange("k p c -> p k c")),
              dma=True, slot='ini3', writes=['wgf'])
            E('scalar', lambda e: e.copy(out=wgh[:], in_=wgf[:]), reads=['wgf'], writes=['wgh'])
            E('vector', lambda e: e.tensor_tensor(out=wgl[:], in0=wgf[:], in1=wgh[:],
                                                  op=ALU.subtract),
              reads=['wgf', 'wgh'], writes=['wgl'])
            E('gpsimd', lambda e: e.dma_start(out=wqb.rearrange("p (k c) -> p k c", k=KD),
                                              in_=IN['wq'].rearrange("k p c -> p k c")),
              dma=True, slot='ini4', writes=['wqb'])
            E('gpsimd', lambda e: e.dma_start(out=wkb.rearrange("p (k c) -> p k c", k=KD),
                                              in_=IN['wk'].rearrange("k p c -> p k c")),
              dma=True, slot='ini5', writes=['wkb'])
            E('gpsimd', lambda e: e.dma_start(out=wvb.rearrange("p (k c) -> p k c", k=KD),
                                              in_=IN['wv'].rearrange("k p c -> p k c")),
              dma=True, slot='ini6', writes=['wvb'])
            E('sync', lambda e: e.dma_start(out=ktc_v, in_=IN['ktc']), dma=True, slot='ini7', writes=['ktc'])
            E('sync', lambda e: e.dma_start(out=kts_v, in_=IN['kts']), dma=True, slot='ini8', writes=['kts'])
            E('sync', lambda e: e.dma_start(out=qtc_v, in_=IN['qtc']), dma=True, slot='ini9', writes=['qtc'])
            E('sync', lambda e: e.dma_start(out=qts_v, in_=IN['qts']), dma=True, slot='ini10', writes=['qts'])
            E('sync', lambda e: e.dma_start(out=lsb[:], in_=IN['lsrep']), dma=True, slot='ini11', writes=['lsb'])
            E('scalar', lambda e: e.activation(out=svec[:], in_=lsb[:], func=AF.Exp,
                                               bias=stat[:, 1:2], scale=1.0),
              reads=['lsb', 'st_ln32'], writes=['svec'])

            def emit_ln(xf32_tile, dst_bf, rdbufs, wrbuf, par=0):
                c = LNC[par]
                bn, mvc, rs = c['bn'], c['mv'], c['rstd']
                for s in range(4):
                    E('vector', lambda e, s=s: e.bn_stats(stat[:, bn + 6 * s:bn + 6 * s + 6],
                                                          xf32_tile[:, 512 * s:512 * (s + 1)]),
                      reads=rdbufs, writes=[f'st_bn{par}'])
                E('vector', lambda e: e.bn_aggr(stat[:, mvc:mvc + 2], stat[:, bn:bn + 24]),
                  reads=[f'st_bn{par}'], writes=[f'st_mv{par}'])
                E('scalar', lambda e: e.activation(out=stat[:, rs:rs + 1], in_=stat[:, mvc + 1:mvc + 2],
                                                   func=AF.Ln, bias=stat[:, 0:1], scale=1.0),
                  reads=[f'st_mv{par}', 'st_eps'], writes=[f'st_r0{par}'])
                E('scalar', lambda e: e.activation(out=stat[:, rs:rs + 1], in_=stat[:, rs:rs + 1],
                                                   func=AF.Exp, scale=-0.5),
                  reads=[f'st_r0{par}'], writes=[f'st_rstd{par}'])
                E('vector', lambda e: e.tensor_scalar(out=dst_bf, in0=xf32_tile,
                                                      scalar1=stat[:, mvc:mvc + 1],
                                                      scalar2=stat[:, rs:rs + 1],
                                                      op0=ALU.subtract, op1=ALU.mult),
                  reads=rdbufs + [f'st_mv{par}', f'st_rstd{par}'], writes=[wrbuf])

            def emit_transposes(src_bf16_tile, nk, dst_fn, wrbuf, rdbuf, scale_col=None,
                                rhs=None, rhsbuf='identb'):
                rhs_ap = identb[:] if rhs is None else rhs
                for k0 in range(0, nk, 4):
                    kn = min(4, nk - k0)
                    pbank = 4 + (k0 // 4) % 2
                    items = [
                        (lambda e, k=k0 + i, pbank=pbank, i=i: e.matmul(
                            pv(pbank * 512 + i * 128, 128),
                            src_bf16_tile[:, 128 * k:128 * (k + 1)], rhs_ap,
                            start=True, stop=True))
                        for i in range(kn)]
                    mev = MMG(items, [rdbuf, rhsbuf], [f'bank{pbank}'])
                    if scale_col is None:
                        E('scalar', lambda e, k0=k0, kn=kn, pbank=pbank: e.copy(
                            out=dst_fn(k0, kn), in_=pv(pbank * 512, kn * 128)),
                          reads=[f'bank{pbank}'], writes=[wrbuf])
                    else:
                        for i in range(kn):
                            E('scalar', lambda e, k=k0 + i, i=i, pbank=pbank: e.activation(
                                out=dst_fn(k, 1), in_=pv(pbank * 512 + i * 128, 128),
                                func=AF.Copy, scale=scale_col(k)),
                              reads=[f'bank{pbank}', 'svec'], writes=[wrbuf])

            def emit_rope(eng, tile_bf, nh, swp, cos_ap, spm_ap, rdbuf, swpbuf):
                geo = tile_bf.rearrange("p (h x) -> p h x", h=nh)[:, :, 32:64] \
                    .rearrange("p h (t d) -> p h t d", t=2)
                sw = swp.rearrange("p (h x) -> p h x", h=nh).rearrange("p h (t d) -> p h t d", t=2)
                cosb = cos_ap.unsqueeze(1).unsqueeze(2).broadcast_to([128, nh, 2, 16])
                spmb = spm_ap.rearrange("p (t d) -> p t d", t=2).unsqueeze(1) \
                    .broadcast_to([128, nh, 2, 16])
                E(eng, lambda e: e.tensor_copy(sw[:, :, 0, :], geo[:, :, 1, :]),
                  reads=[rdbuf], writes=[swpbuf])
                E(eng, lambda e: e.tensor_copy(sw[:, :, 1, :], geo[:, :, 0, :]),
                  reads=[rdbuf], writes=[swpbuf])
                E(eng, lambda e: e.tensor_tensor(out=geo, in0=geo, in1=cosb, op=ALU.mult),
                  reads=[rdbuf, 'ktc', 'qtc'], writes=[rdbuf])
                E(eng, lambda e: e.tensor_tensor(out=sw, in0=sw, in1=spmb, op=ALU.mult),
                  reads=[swpbuf, 'kts', 'qts'], writes=[swpbuf])
                E(eng, lambda e: e.tensor_tensor(out=geo, in0=geo, in1=sw, op=ALU.add),
                  reads=[rdbuf, swpbuf], writes=[rdbuf])

            def p1A(i):
                c, ti = i // 2, i % 2
                sl = c % 3
                if ti == 0:
                    E('sync', lambda e, c=c, sl=sl: e.dma_start(
                        out=xch[sl].rearrange("p (t d) -> p t d", t=2),
                        in_=IN['xb'][CH * c:CH * (c + 1)].rearrange("(t p) d -> p t d", p=128)),
                      dma=True, slot=f'xch{sl}', writes=[f'xch{sl}'])

            def p1B(i):
                c, ti = i // 2, i % 2
                sl = c % 3
                par = i % 2
                cc = LNC[par]
                bn, mvc, rs = cc['bn'], cc['mv'], cc['rstd']
                xt = xch[sl].rearrange("p (t d) -> p t d", t=2)[:, ti, :]
                for ss in range(4):
                    E('vector', lambda e, ss=ss, xt=xt, bn=bn: e.bn_stats(
                        stat[:, bn + 6 * ss:bn + 6 * ss + 6], xt[:, 512 * ss:512 * (ss + 1)]),
                      reads=[f'xch{sl}'], writes=[f'st_bn{par}'])
                E('vector', lambda e, bn=bn, mvc=mvc: e.bn_aggr(stat[:, mvc:mvc + 2],
                                                                stat[:, bn:bn + 24]),
                  reads=[f'st_bn{par}'], writes=[f'st_mv{par}'])
                E('scalar', lambda e, mvc=mvc, rs=rs: e.activation(
                    out=stat[:, rs:rs + 1], in_=stat[:, mvc + 1:mvc + 2],
                    func=AF.Ln, bias=stat[:, 0:1], scale=1.0),
                  reads=[f'st_mv{par}', 'st_eps'], writes=[f'st_r0{par}'])
                E('scalar', lambda e, rs=rs: e.activation(out=stat[:, rs:rs + 1],
                                                          in_=stat[:, rs:rs + 1],
                                                          func=AF.Exp, scale=-0.5),
                  reads=[f'st_r0{par}'], writes=[f'st_rstd{par}'])

            def p1C(i):
                c, ti = i // 2, i % 2
                sl = c % 3
                par = i % 2
                cc = LNC[par]
                mvc, rs = cc['mv'], cc['rstd']
                xt = xch[sl].rearrange("p (t d) -> p t d", t=2)[:, ti, :]
                xnv = xnb[sl].rearrange("p (t d) -> p t d", t=2)[:, ti, :]
                E('vector', lambda e, xt=xt, xnv=xnv, mvc=mvc, rs=rs: e.tensor_scalar(
                    out=xnv, in0=xt, scalar1=stat[:, mvc:mvc + 1], scalar2=stat[:, rs:rs + 1],
                    op0=ALU.subtract, op1=ALU.mult),
                  reads=[f'xch{sl}', f'st_mv{par}', f'st_rstd{par}'], writes=[f'xnb{sl}'])
                xnTv = xnT[sl].rearrange("p (k c) -> p k c", k=KD)
                emit_transposes(xnv, KD,
                                lambda k0, kn, ti=ti, xnTv=xnTv: xnTv[:, k0:k0 + kn,
                                                                      128 * ti:128 * (ti + 1)],
                                f'xnT{sl}', f'xnb{sl}')

            def p1D(i):
                c, ti = i // 2, i % 2
                sl = c % 3
                g = i
                xnv = xnb[sl].rearrange("p (t d) -> p t d", t=2)[:, ti, :]
                xnTv = xnT[sl].rearrange("p (k c) -> p k c", k=KD)
                items = [(lambda e, k=k, ti=ti, xnTv=xnTv: e.matmul(
                    pv(2 * 512, 256), xnTv[:, k, 128 * ti:128 * (ti + 1)],
                    wkb.rearrange("p (k c) -> p k c", k=KD)[:, k, :],
                    start=(k == 0), stop=(k == KD - 1))) for k in range(KD)]
                MMG(items, [f'xnT{sl}', 'wkb'], ['bank2'])
                kbv = kbf[sl].rearrange("p (t c) -> p t c", t=2)[:, ti, :]
                E('scalar', lambda e, kbv=kbv: e.copy(out=kbv, in_=pv(2 * 512, 256)),
                  reads=['bank2'], writes=[f'kbf{sl}'])
                emit_rope('gpsimd', kbv, HKV,
                          kswp[sl].rearrange("p (t c) -> p t c", t=2)[:, ti, :],
                          ktc_v[:, 16 * g:16 * (g + 1)], kts_v[:, 32 * g:32 * (g + 1)],
                          f'kbf{sl}', f'kswp{sl}')

            def p1E(i):
                c, ti = i // 2, i % 2
                sl = c % 3
                g = i
                xnv = xnb[sl].rearrange("p (t d) -> p t d", t=2)[:, ti, :]
                xnTv = xnT[sl].rearrange("p (k c) -> p k c", k=KD)
                kbv = kbf[sl].rearrange("p (t c) -> p t c", t=2)[:, ti, :]
                if c < 12:
                    for j2 in range(2):
                        E('tensor', lambda e, kbv=kbv, j2=j2, g=g: e.matmul(
                            pv(3072 + j2 * 128 + 2 * g, 2), kbv[:, 128 * j2:128 * (j2 + 1)],
                            poolb[:], start=True, stop=True),
                          reads=[f'kbf{sl}', 'poolb'], writes=['bank6'])
                    for dm in range(KD):
                        E('tensor', lambda e, xnv=xnv, dm=dm, ti=ti: e.matmul(
                            pv(3584 + dm * 4 + ti * 2, 2), xnv[:, 128 * dm:128 * (dm + 1)],
                            poolb[:], start=True, stop=True),
                          reads=[f'xnb{sl}', 'poolb'], writes=['bank7'])
                    if ti == 1:
                        xnpTv = xnpT.rearrange("p (k c) -> p k c", k=KD)
                        E('scalar', lambda e, c=c, xnpTv=xnpTv: e.copy(
                            out=xnpTv[:, :, 4 * c:4 * (c + 1)],
                            in_=pv(3584, 64).rearrange("p (k c) -> p k c", k=KD)),
                          reads=['bank7'], writes=['xnpT'])
                else:
                    m = 2 * (c - 12) + ti
                    items = [(lambda e, k=k, ti=ti, xnTv=xnTv: e.matmul(
                        pv(3 * 512, 256), xnTv[:, k, 128 * ti:128 * (ti + 1)],
                        wvb.rearrange("p (k c) -> p k c", k=KD)[:, k, :],
                        start=(k == 0), stop=(k == KD - 1))) for k in range(KD)]
                    MMG(items, [f'xnT{sl}', 'wvb'], ['bank3'])
                    V2v = V2.rearrange("p (t c) -> p t c", t=NKT)
                    kta = (64 + 128 * m) // 128
                    E('scalar', lambda e, kta=kta, V2v=V2v: e.copy(out=V2v[64:128, kta, :],
                                                                   in_=pv(3 * 512, 256, 0, 64)),
                      reads=['bank3'], writes=['V2'])
                    E('scalar', lambda e, kta=kta, V2v=V2v: e.copy(out=V2v[0:64, kta + 1, :],
                                                                   in_=pv(3 * 512, 256, 64, 128)),
                      reads=['bank3'], writes=['V2'])
                    for j2 in range(2):
                        E('tensor', lambda e, kbv=kbv, j2=j2: e.matmul(
                            pv(0, 128), kbv[:, 128 * j2:128 * (j2 + 1)], identb[:],
                            start=True, stop=True),
                          reads=[f'kbf{sl}', 'identb'], writes=['bank0'])
                        co = 64 + 128 * m
                        for jj, (p0, p1) in enumerate([(0, 64), (64, 128)]):
                            jk = 2 * j2 + jj
                            E('scalar', lambda e, jk=jk, co=co, p0=p0, p1=p1: e.copy(
                                out=KT[jk][0:64, co:co + 128], in_=pv(0, 128, p0, p1)),
                              reads=['bank0'], writes=[f'KT{jk}'])
                            E('scalar', lambda e, jk=jk, co=co, p0=p0, p1=p1: e.copy(
                                out=KT[jk][64:128, co:co + 128], in_=pv(0, 128, p0, p1)),
                              reads=['bank0'], writes=[f'KT{jk}'])

            NT1 = 2 * NCH
            NT1 = 2 * NCH
            for step in range(NT1 + 5):
                if step < NT1:
                    p1A(step)
                if 0 <= step - 1 < NT1:
                    p1B(step - 1)
                if 0 <= step - 2 < NT1:
                    p1C(step - 2)
                if 0 <= step - 3 < NT1:
                    p1D(step - 3)
                if 0 <= step - 5 < NT1:
                    p1E(step - 5)
            for j2 in range(2):
                for jj, (p0, p1) in enumerate([(0, 64), (64, 128)]):
                    jk = 2 * j2 + jj
                    E('scalar', lambda e, j2=j2, p0=p0, p1=p1, jk=jk: e.copy(
                        out=KT[jk][0:64, 0:48], in_=pv(3072 + j2 * 128, 48, p0, p1)),
                      reads=['bank6'], writes=[f'KT{jk}'])
                    E('scalar', lambda e, j2=j2, p0=p0, p1=p1, jk=jk: e.copy(
                        out=KT[jk][64:128, 0:48], in_=pv(3072 + j2 * 128, 48, p0, p1)),
                      reads=['bank6'], writes=[f'KT{jk}'])
            xnpTv = xnpT.rearrange("p (k c) -> p k c", k=KD)
            items = [(lambda e, k=k: e.matmul(
                pv(3 * 512, 256, 0, 48), xnpTv[:, k, :],
                wvb.rearrange("p (k c) -> p k c", k=KD)[:, k, :],
                start=(k == 0), stop=(k == KD - 1))) for k in range(KD)]
            MMG(items, ['xnpT', 'wvb'], ['bank3'])
            E('scalar', lambda e: e.copy(out=V2.rearrange("p (t c) -> p t c", t=NKT)[0:48, 0, :],
                                         in_=pv(3 * 512, 256, 0, 48)),
              reads=['bank3'], writes=['V2'])

            def q1A(i):
                qc, ti = i // 2, i % 2
                sl = qc % 2
                if ti == 0:
                    E('sync', lambda e, qc=qc, sl=sl: e.dma_start(
                        out=xch[sl].rearrange("p (t d) -> p t d", t=2),
                        in_=IN['xq'][CH * qc:CH * (qc + 1)].rearrange("(t p) d -> p t d", p=128)),
                      dma=True, slot=f'xch{sl}', writes=[f'xch{sl}'])

            def q1B(i):
                qc, ti = i // 2, i % 2
                sl = qc % 2
                par = i % 2
                cc = LNC[par]
                bn, mvc, rs = cc['bn'], cc['mv'], cc['rstd']
                xt = xch[sl].rearrange("p (t d) -> p t d", t=2)[:, ti, :]
                for ss in range(4):
                    E('vector', lambda e, ss=ss, xt=xt, bn=bn: e.bn_stats(
                        stat[:, bn + 6 * ss:bn + 6 * ss + 6], xt[:, 512 * ss:512 * (ss + 1)]),
                      reads=[f'xch{sl}'], writes=[f'st_bn{par}'])
                E('vector', lambda e, bn=bn, mvc=mvc: e.bn_aggr(stat[:, mvc:mvc + 2],
                                                                stat[:, bn:bn + 24]),
                  reads=[f'st_bn{par}'], writes=[f'st_mv{par}'])
                E('scalar', lambda e, mvc=mvc, rs=rs: e.activation(
                    out=stat[:, rs:rs + 1], in_=stat[:, mvc + 1:mvc + 2],
                    func=AF.Ln, bias=stat[:, 0:1], scale=1.0),
                  reads=[f'st_mv{par}', 'st_eps'], writes=[f'st_r0{par}'])
                E('scalar', lambda e, rs=rs: e.activation(out=stat[:, rs:rs + 1],
                                                          in_=stat[:, rs:rs + 1],
                                                          func=AF.Exp, scale=-0.5),
                  reads=[f'st_r0{par}'], writes=[f'st_rstd{par}'])

            def q1C(i):
                qc, ti = i // 2, i % 2
                sl = qc % 2
                par = i % 2
                cc = LNC[par]
                mvc, rs = cc['mv'], cc['rstd']
                xt = xch[sl].rearrange("p (t d) -> p t d", t=2)[:, ti, :]
                xnv = xnb[sl].rearrange("p (t d) -> p t d", t=2)[:, ti, :]
                E('vector', lambda e, xt=xt, xnv=xnv, mvc=mvc, rs=rs: e.tensor_scalar(
                    out=xnv, in0=xt, scalar1=stat[:, mvc:mvc + 1], scalar2=stat[:, rs:rs + 1],
                    op0=ALU.subtract, op1=ALU.mult),
                  reads=[f'xch{sl}', f'st_mv{par}', f'st_rstd{par}'], writes=[f'xnb{sl}'])
                xnTv = xnT[sl].rearrange("p (k c) -> p k c", k=KD)
                emit_transposes(xnv, KD,
                                lambda k0, kn, ti=ti, xnTv=xnTv: xnTv[:, k0:k0 + kn,
                                                                      128 * ti:128 * (ti + 1)],
                                f'xnT{sl}', f'xnb{sl}')

            def q1D(i):
                qc, ti = i // 2, i % 2
                sl = qc % 2
                jt = i
                xnTv = xnT[sl].rearrange("p (k c) -> p k c", k=KD)
                for seg in range(2):
                    items = [(lambda e, k=k, seg=seg, ti=ti, xnTv=xnTv: e.matmul(
                        pv(1024 + 512 * seg, 512), xnTv[:, k, 128 * ti:128 * (ti + 1)],
                        wqb.rearrange("p (k c) -> p k c", k=KD)[:, k, 512 * seg:512 * (seg + 1)],
                        start=(k == 0), stop=(k == KD - 1))) for k in range(KD)]
                    MMG(items, [f'xnT{sl}', 'wqb'], [f'bank{2 + seg}'])
                qv = qfb[sl].rearrange("p (t c) -> p t c", t=2)[:, ti, :]
                E('scalar', lambda e, qv=qv: e.copy(out=qv, in_=pv(1024, 1024)),
                  reads=['bank2', 'bank3'], writes=[f'qfb{sl}'])
                emit_rope('vector', qv, H,
                          qswp[sl].rearrange("p (t c) -> p t c", t=2)[:, ti, :],
                          qtc_v[:, 16 * jt:16 * (jt + 1)], qts_v[:, 32 * jt:32 * (jt + 1)],
                          f'qfb{sl}', f'qswp{sl}')

            def q1E(i):
                qc, ti = i // 2, i % 2
                sl = qc % 2
                jt = i
                qv = qfb[sl].rearrange("p (t c) -> p t c", t=2)[:, ti, :]
                QTv = QT.rearrange("p (m c) -> p m c", m=NQT)
                emit_transposes(qv, 8,
                                lambda k0, kn, jt=jt, QTv=QTv: QTv[:, k0, 128 * jt:128 * (jt + 1)],
                                'QT', f'qfb{sl}',
                                scale_col=lambda m: svec[:, m:m + 1])

            for step in range(8 + 5):
                if step < 8:
                    q1A(step)
                if 0 <= step - 1 < 8:
                    q1B(step - 1)
                if 0 <= step - 2 < 8:
                    q1C(step - 2)
                if 0 <= step - 3 < 8:
                    q1D(step - 3)
                if 0 <= step - 5 < 8:
                    q1E(step - 5)
            if dbg:
                for j in range(4):
                    E('gpsimd', lambda e, j=j: e.dma_start(out=DBG['d_kt'][j], in_=KT[j][:]),
                      reads=[f'KT{j}'], dma=True, slot='dbg')
                E('gpsimd', lambda e: e.dma_start(out=DBG['d_v2'], in_=V2[:]), reads=['V2'], dma=True, slot='dbg')
                E('gpsimd', lambda e: e.dma_start(out=DBG['d_qt'], in_=QT[:]), reads=['QT'], dma=True, slot='dbg')

            pr.barrier()
            QTv = QT.rearrange("p (m c) -> p m c", m=NQT)
            OTv = OT.rearrange("p (m c) -> p m c", m=NQT)
            V2v = V2.rearrange("p (t c) -> p t c", t=NKT)
            E('gpsimd', lambda e: e.dma_start(out=wob.rearrange("p (k c) -> p k c", k=8),
                                              in_=IN['wo'].rearrange("k p c -> p k c")),
              dma=True, slot='wob', writes=['wob'])
            tasks = []
            for j in range(NQT):
                if WJ[j] <= 512:
                    for t in range(8):
                        tasks.append((j, 2 * t, 2))
                else:
                    for h in range(H):
                        tasks.append((j, h, 1))
            NTASK = len(tasks)

            def st1(i):
                j, h0, nh = tasks[i]
                w = WJ[j]
                sA = i % 2
                sbase = sA * 1536
                sbanks = [f'bank{b}' for b in range(sA * 3, sA * 3 + max(1, (w * nh + 511) // 512))]
                if h0 == 0:
                    bs = j % 2
                    E('sync', lambda e, j=j, bs=bs, w=w: e.dma_start(
                        out=bmv[bs][:, 0:w], in_=IN['bmask'][j][:, 0:w]),
                      dma=True, slot=f'bm{bs}', writes=[f'bmv{bs}'])
                items = []
                for hh in range(nh):
                    h = h0 + hh
                    kv = h // 4
                    m = h // 2
                    half = h % 2
                    off = hh * 512
                    for (o, ww) in [(o, min(512, w - o)) for o in range(0, w, 512)]:
                        items.append(lambda e, o=o, ww=ww, m=m, half=half, kv=kv, j=j, \
                                            sbase=sbase, off=off: e.matmul(
                            pv(sbase + off + o, ww),
                            QTv[64 * half:64 * (half + 1), m, 128 * j:128 * (j + 1)],
                            KT[kv][64 * half:64 * (half + 1), o:o + ww],
                            start=True, stop=True))
                kvs = sorted({(h0 + hh) // 4 for hh in range(nh)})
                MMG(items, ['QT'] + [f'KT{kv}' for kv in kvs],
                    sbanks if nh == 1 else [f'bank{sA * 3}', f'bank{sA * 3 + 1}'])

            def st2(i):
                j, h0, nh = tasks[i]
                w = WJ[j]
                wt = w * nh
                sA = i % 2
                sbase = sA * 1536
                hp = i % 2
                bs = j % 2
                sbanks = [f'bank{b}' for b in range(sA * 3, sA * 3 + max(1, (wt + 511) // 512))]
                if nh == 2:
                    sview = pv(sbase, 1024).rearrange("p (h k) -> p h k", h=2)[:, :, 0:w]
                    bview = bmv[bs][:, 0:w].unsqueeze(1).broadcast_to([128, 2, w])
                    nview = negS[hp][:, 0:wt].rearrange("p (h k) -> p h k", h=2)
                    sbanks = [f'bank{sA * 3}', f'bank{sA * 3 + 1}']
                    E('vector', lambda e, sview=sview, bview=bview, nview=nview: e.tensor_tensor(
                        out=nview, in0=sview, in1=bview, op=ALU.min),
                      reads=sbanks + [f'bmv{bs}'], writes=[f'negS{hp}'])
                else:
                    E('vector', lambda e, sbase=sbase, hp=hp, bs=bs, w=w: e.tensor_tensor(
                        out=negS[hp][:, 0:w], in0=pv(sbase, w), in1=bmv[bs][:, 0:w], op=ALU.min),
                      reads=sbanks + [f'bmv{bs}'], writes=[f'negS{hp}'])
                E('vector', lambda e, hp=hp, wt=wt: e.tensor_scalar(
                    out=negS[hp][:, 0:wt], in0=negS[hp][:, 0:wt], scalar1=-1.0, scalar2=3e38,
                    op0=ALU.mult, op1=ALU.min, accum_out=stat[:, 2 + hp:3 + hp]),
                  reads=[f'negS{hp}'], writes=[f'negS{hp}', f'st_negmax{hp}'])
                if nh == 2:
                    E('scalar', lambda e, hp=hp, wt=wt: e.activation(
                        out=Pb[hp][:, 0:wt], in_=negS[hp][:, 0:wt], func=AF.Exp,
                        bias=stat[:, 2 + hp:3 + hp], scale=-1.0),
                      reads=[f'negS{hp}', f'st_negmax{hp}'], writes=[f'Pb{hp}'])
                else:
                    E('scalar', lambda e, hp=hp, wt=wt: e.activation(
                        out=Pb[hp][:, 0:wt], in_=negS[hp][:, 0:wt], func=AF.Exp,
                        bias=stat[:, 2 + hp:3 + hp], scale=-1.0, accum_out=stat[:, 4 + hp:5 + hp]),
                      reads=[f'negS{hp}', f'st_negmax{hp}'], writes=[f'Pb{hp}', f'st_rowsum{hp}'])

            def st3a(i):
                j, h0, nh = tasks[i]
                w = WJ[j]
                hp = i % 2
                pq = i % 3
                kts = kt_list(w)
                if nh == 2:
                    E('vector', lambda e, hp=hp, w=w: e.reduce_sum(
                        stat[:, 90 + 2 * hp:92 + 2 * hp],
                        Pb[hp][:, 0:2 * w].rearrange("p (h k) -> p h k", h=2),
                        axis=mybir.AxisListType.X),
                      reads=[f'Pb{hp}'], writes=[f'st_rowsum{hp}'])
                    E('vector', lambda e, hp=hp: e.reciprocal(stat[:, 35 + 2 * hp:37 + 2 * hp],
                                                              stat[:, 90 + 2 * hp:92 + 2 * hp]),
                      reads=[f'st_rowsum{hp}'], writes=[f'st_recip{hp}'])
                    for hh in range(2):
                        E('vector', lambda e, hp=hp, hh=hh: e.tensor_scalar(
                            out=diagb[2 * hp + hh], in0=identb[:],
                            scalar1=stat[:, 35 + 2 * hp + hh:36 + 2 * hp + hh], scalar2=None,
                            op0=ALU.mult),
                          reads=['identb', f'st_recip{hp}'], writes=[f'diag{2 * hp + hh}'])
                    items = [(lambda e, hh=hh, hp=hp, w=w: e.matmul(
                        pv(6 * 512 + 128 * hh, 128, 0, w),
                        Pb[hp][:, hh * w:hh * w + w], diagb[2 * hp + hh],
                        start=True, stop=True)) for hh in range(2)]
                    MMG(items, [f'Pb{hp}', f'diag{2 * hp}', f'diag{2 * hp + 1}'], ['bank6'])
                    E('scalar', lambda e, pq=pq, w=w: e.copy(
                        out=PTs[pq][0:w, 0:256], in_=pv(6 * 512, 256, 0, w)),
                      reads=['bank6'], writes=[f'PTs{pq}'])
                else:
                    E('vector', lambda e, hp=hp: e.reciprocal(stat[:, 6 + hp:7 + hp],
                                                              stat[:, 4 + hp:5 + hp]),
                      reads=[f'st_rowsum{hp}'], writes=[f'st_recip{hp}'])
                    E('vector', lambda e, hp=hp: e.tensor_scalar(
                        out=diagb[2 * hp], in0=identb[:], scalar1=stat[:, 6 + hp:7 + hp],
                        scalar2=None, op0=ALU.mult),
                      reads=['identb', f'st_recip{hp}'], writes=[f'diag{2 * hp}'])
                    for k0 in range(0, len(kts), 4):
                        kn = min(4, len(kts) - k0)
                        items = [(lambda e, kt=k0 + ii, ii=ii, hp=hp, kts=kts: e.matmul(
                            pv(6 * 512 + 128 * ii, 128, 0, kts[kt]),
                            Pb[hp][:, 128 * kt:128 * kt + kts[kt]], diagb[2 * hp],
                            start=True, stop=True)) for ii in range(kn)]
                        MMG(items, [f'Pb{hp}', f'diag{2 * hp}'], ['bank6'])
                        E('scalar', lambda e, k0=k0, kn=kn, pq=pq: e.copy(
                            out=PTs[pq][:, 128 * k0:128 * (k0 + kn)], in_=pv(6 * 512, 128 * kn)),
                          reads=['bank6'], writes=[f'PTs{pq}'])
                if dbg and j == 0 and h0 == 0:
                    E('gpsimd', lambda e, hp=hp: e.dma_start(out=DBG['d_p'][:, 0:WJ[0]],
                                                             in_=Pb[hp][:, 0:WJ[0]]),
                      reads=[f'Pb{hp}'], dma=True, slot='dbg')

            def st3b(i):
                j, h0, nh = tasks[i]
                w = WJ[j]
                pq = i % 3
                kts = kt_list(w)
                m = h0 // 2
                if nh == 2:
                    kv = h0 // 4
                    items = [
                        (lambda e, kv=kv, pq=pq, w=w: e.matmul(
                            pv(7 * 512, 128, 0, 64), V2v[0:w, 0, 64 * kv:64 * (kv + 1)],
                            PTs[pq][0:w, 0:128], start=True, stop=True)),
                        (lambda e, kv=kv, pq=pq, w=w: e.matmul(
                            pv(7 * 512 + 128, 128, 0, 64), V2v[0:w, 0, 64 * kv:64 * (kv + 1)],
                            PTs[pq][0:w, 128:256], start=True, stop=True)),
                    ]
                    MMG(items, ['V2', f'PTs{pq}'], ['bank7'])
                    E('scalar', lambda e, m=m, j=j: e.copy(
                        out=OTv[0:64, m, 128 * j:128 * (j + 1)], in_=pv(7 * 512, 128, 0, 64)),
                      reads=['bank7'], writes=['OT'])
                    E('scalar', lambda e, m=m, j=j: e.copy(
                        out=OTv[64:128, m, 128 * j:128 * (j + 1)],
                        in_=pv(7 * 512 + 128, 128, 0, 64)),
                      reads=['bank7'], writes=['OT'])
                else:
                    kv = h0 // 4
                    half = h0 % 2
                    items = [(lambda e, kt=kt, kv=kv, pq=pq, kts=kts: e.matmul(
                        pv(7 * 512, 128, 0, 64),
                        V2v[0:kts[kt], kt, 64 * kv:64 * (kv + 1)],
                        PTs[pq][0:kts[kt], 128 * kt:128 * (kt + 1)],
                        start=(kt == 0), stop=(kt == len(kts) - 1))) for kt in range(len(kts))]
                    MMG(items, ['V2', f'PTs{pq}'], ['bank7'])
                    E('scalar', lambda e, m=m, half=half, j=j: e.copy(
                        out=OTv[64 * half:64 * (half + 1), m, 128 * j:128 * (j + 1)],
                        in_=pv(7 * 512, 128, 0, 64)),
                      reads=['bank7'], writes=['OT'])

            wov = wob.rearrange("p (k c) -> p k c", k=8)
            xn2v = xn2.rearrange("p (t d) -> p t d", t=NQT)
            xn2lov = xn2lo.rearrange("p (t d) -> p t d", t=NQT)
            x1evs = []
            xlo_evs = {}

            def wo_block(j):
                tt = j
                sl = tt % 2
                E('sync', lambda e, tt=tt, sl=sl: e.dma_start(
                    out=xqt[sl], in_=IN['xq'][128 * tt:128 * (tt + 1)]),
                  dma=True, slot=f'xq{sl}', writes=[f'xqt{sl}'])
                for seg in range(4):
                    items = [(lambda e, k=k, seg=seg, tt=tt: e.matmul(
                        pv(6 * 512, 512), OTv[:, k, 128 * tt:128 * (tt + 1)],
                        wov[:, k, 512 * seg:512 * (seg + 1)],
                        start=(k == 0), stop=(k == 7))) for k in range(8)]
                    MMG(items, ['OT', 'wob'], ['bank6'])
                    E('vector', lambda e, seg=seg, sl=sl: e.scalar_tensor_tensor(
                        out=x1t[sl][:, 512 * seg:512 * (seg + 1)], in0=pv(6 * 512, 512),
                        scalar=1.0, in1=xqt[sl][:, 512 * seg:512 * (seg + 1)],
                        op0=ALU.mult, op1=ALU.add),
                      reads=['bank6', f'xqt{sl}'], writes=[f'x1t{sl}'])
                ev = E('sync', lambda e, tt=tt, sl=sl: e.dma_start(
                    out=x1sp[128 * tt:128 * (tt + 1)], in_=x1t[sl]),
                  reads=[f'x1t{sl}'], dma=True, slot=f'x1s{sl}')
                x1evs.append(ev)
                emit_ln(x1t[sl], xqt[sl], [f'x1t{sl}'], f'xqt{sl}', par=sl)
                E('scalar', lambda e, tt=tt, sl=sl: e.copy(out=xn2v[:, tt, :], in_=xqt[sl]),
                  reads=[f'xqt{sl}'], writes=['xn2'])
                E('vector', lambda e, tt=tt, sl=sl: e.tensor_tensor(
                    out=xlot[sl], in0=xqt[sl], in1=xn2v[:, tt, :], op=ALU.subtract),
                  reads=[f'xqt{sl}', 'xn2'], writes=[f'xlot{sl}'])
                xlo_evs[tt] = E('sync', lambda e, tt=tt, sl=sl: e.dma_start(
                    out=xlosp[tt], in_=xlot[sl]),
                  reads=[f'xlot{sl}'], dma=True, slot=f'xlo{sl}')

            last_of_j = {}
            for i, (jx, h0x, nhx) in enumerate(tasks):
                last_of_j[jx] = i
            wo_after = {v: jx for jx, v in last_of_j.items()}

            for step in range(NTASK + 3):
                if step < NTASK:
                    st1(step)
                if 0 <= step - 1 < NTASK:
                    st2(step - 1)
                if 0 <= step - 2 < NTASK:
                    st3a(step - 2)
                if 0 <= step - 3 < NTASK:
                    st3b(step - 3)
                    if step - 3 in wo_after:
                        wo_block(wo_after[step - 3])
            if dbg:
                E('gpsimd', lambda e: e.dma_start(out=DBG['d_ot'], in_=OT[:]), reads=['OT'], dma=True, slot='dbg')
                E('sync', lambda e: e.dma_start(out=DBG['d_x1'], in_=x1sp), extra=x1evs, dma=True, slot='dbg')
                for tt in range(NQT):
                    E('gpsimd', lambda e, tt=tt: e.dma_start(
                        out=DBG['d_xn2'][128 * tt:128 * (tt + 1)], in_=xn2v[:, tt, :]),
                      reads=['xn2'], dma=True, slot='dbg')

            pr.barrier()
            xn2Tv = xn2T.rearrange("p (k c) -> p k c", k=KD)
            xn2sTv = xn2sT.rearrange("p (k c) -> p k c", k=KD)
            wghv = wgh.rearrange("p (k c) -> p k c", k=KD)
            wglv = wgl.rearrange("p (k c) -> p k c", k=KD)
            for tt in range(NQT):
                sl = tt % 2
                E('sync', lambda e, tt=tt: e.dma_start(out=xn2lov[:, tt, :], in_=xlosp[tt]),
                  extra=[xlo_evs[tt]], dma=True, slot=f'xlo{tt % 2}', writes=['xn2lo'])
                emit_transposes(xn2v[:, tt, :], KD,
                                lambda k0, kn, tt=tt: xn2Tv[:, k0:k0 + kn, 128 * tt:128 * (tt + 1)],
                                'xn2T', 'xn2')
                xloTv = xloT[sl].rearrange("p (k c) -> p k c", k=KD)
                emit_transposes(xn2lov[:, tt, :], KD,
                                lambda k0, kn, sl=sl: xloT[sl].rearrange(
                                    "p (k c) -> p k c", k=KD)[:, k0:k0 + kn, :].squeeze(),
                                f'xloT{sl}', 'xn2lo')
                gbank = f'bank{6 + (tt * 128) // 512 % 2}'
                items = []
                for (wv_, xv_) in [(wghv, 'T'), (wglv, 'T'), (wghv, 'L')]:
                    for k in range(KD):
                        def mk(e, k=k, wv_=wv_, xv_=xv_, tt=tt, sl=sl):
                            rhs = (xn2Tv[:, k, 128 * tt:128 * (tt + 1)] if xv_ == 'T'
                                   else xloT[sl].rearrange("p (k c) -> p k c", k=KD)[:, k, :])
                            return e.matmul(pv(3072 + 128 * tt, 128, 0, 8), wv_[:, k, :], rhs,
                                            start=(k == 0 and xv_ == 'T' and wv_ is wghv),
                                            stop=(k == KD - 1 and xv_ == 'L'))
                        items.append(mk)
                MMG(items, ['wgh', 'wgl', 'xn2T', f'xloT{sl}'], [gbank])
            E('scalar', lambda e: e.copy(out=logitsT[0:8, :], in_=pv(3072, 1024, 0, 8)),
              reads=['bank6', 'bank7'], writes=['logitsT'])
            for j in range(NQT):
                E('tensor', lambda e, j=j: e.matmul(
                    pv(3072, 8), logitsT[0:8, 128 * j:128 * (j + 1)], identb[0:8, 0:8],
                    start=True, stop=True),
                  reads=['logitsT', 'identb'], writes=['bank6'])
                E('vector', lambda e: e.reduce_max(stat[:, 68:69], pv(3072, 8), axis=mybir.AxisListType.X),
                  reads=['bank6'], writes=['st_m1'])
                E('vector', lambda e: e.tensor_scalar(out=stat[:, 69:77], in0=pv(3072, 8),
                                                      scalar1=stat[:, 68:69], scalar2=0.0,
                                                      op0=ALU.is_equal, op1=ALU.add,
                                                      accum_out=stat[:, 77:78]),
                  reads=['bank6', 'st_m1'], writes=['st_eq', 'st_cnt'])
                E('vector', lambda e: e.scalar_tensor_tensor(out=stat[:, 69:77], in0=stat[:, 69:77],
                                                             scalar=-3e38, in1=pv(3072, 8),
                                                             op0=ALU.mult, op1=ALU.add),
                  reads=['st_eq', 'bank6'], writes=['st_eq'])
                E('vector', lambda e: e.reduce_max(stat[:, 78:79], stat[:, 69:77],
                                                   axis=mybir.AxisListType.X),
                  reads=['st_eq'], writes=['st_m2p'])
                E('vector', lambda e: e.tensor_scalar(out=stat[:, 79:80], in0=stat[:, 77:78],
                                                      scalar1=1.5, scalar2=None, op0=ALU.is_ge),
                  reads=['st_cnt'], writes=['st_im'])
                E('vector', lambda e: e.tensor_tensor(out=stat[:, 80:81], in0=stat[:, 68:69],
                                                      in1=stat[:, 78:79], op=ALU.subtract),
                  reads=['st_m1', 'st_m2p'], writes=['st_d'])
                E('vector', lambda e: e.tensor_tensor(out=stat[:, 80:81], in0=stat[:, 79:80],
                                                      in1=stat[:, 80:81], op=ALU.mult),
                  reads=['st_im', 'st_d'], writes=['st_d'])
                E('vector', lambda e: e.tensor_tensor(out=stat[:, 80:81], in0=stat[:, 78:79],
                                                      in1=stat[:, 80:81], op=ALU.add),
                  reads=['st_m2p', 'st_d'], writes=['st_d'])
                E('vector', lambda e, j=j: e.tensor_tensor(out=stat[:, 82 + j:83 + j],
                                                           in0=stat[:, 68:69],
                                                           in1=stat[:, 80:81], op=ALU.add),
                  reads=['st_m1', 'st_d'], writes=[f'st_sc{j}'])
                E('vector', lambda e, j=j: e.tensor_scalar(out=sdiag[j], in0=identb[:],
                                                           scalar1=stat[:, 82 + j:83 + j],
                                                           scalar2=None, op0=ALU.mult),
                  reads=['identb', f'st_sc{j}'], writes=[f'sdiag{j}'])
            if dbg:
                E('sync', lambda e: e.dma_start(out=DBG['d_scale'], in_=stat[:, 82:90],
                                                allow_slow_non_contiguous=True),
                  reads=[f'st_sc{j}' for j in range(8)], dma=True, slot='dbg')
            pr.barrier()
            for tt in range(NQT):
                emit_transposes(xn2v[:, tt, :], KD,
                                lambda k0, kn, tt=tt: xn2sTv[:, k0:k0 + kn, 128 * tt:128 * (tt + 1)],
                                'xn2sT', 'xn2', rhs=sdiag[tt], rhsbuf=f'sdiag{tt}')

            pr.barrier()
            if dbg:
                E('gpsimd', lambda e: e.dma_start(out=DBG['d_xs'], in_=xn2sT[:]),
                  reads=['xn2sT'], dma=True, slot='dbg')
                for j in range(NQT):
                    E('gpsimd', lambda e, j=j: e.dma_start(
                        out=DBG['d_sd'][:, 128 * j:128 * (j + 1)], in_=sdiag[j]),
                      reads=[f'sdiag{j}'], dma=True, slot='dbg')
            ms_evs = []
            for dm in range(KD):
                sl = dm % 2
                E('gpsimd', lambda e, dm=dm, sl=sl: e.dma_start(
                    out=web[sl].rearrange("p (k c) -> p k c", k=KD),
                    in_=IN['we'][dm],
                    max_dma_last_dim=8192),
                  dma=True, slot=f'we{sl}', writes=[f'web{sl}'])
                if dbg:
                    E('gpsimd', lambda e, dm=dm, sl=sl: e.dma_start(
                        out=DBG['d_we'][dm], in_=web[sl][:]),
                      reads=[f'web{sl}'], dma=True, slot='dbg')
                for seg in range(2):
                    pb = 2 * sl + seg
                    items = [(lambda e, k=k, seg=seg, sl=sl, pb=pb: e.matmul(
                        pv(512 * pb, 512), web[sl].rearrange("p (k c) -> p k c", k=KD)[:, k, :],
                        xn2sTv[:, k, 512 * seg:512 * (seg + 1)],
                        start=(k == 0), stop=(k == KD - 1))) for k in range(KD)]
                    moe_last_mm = MMG(items, [f'web{sl}', 'xn2sT'], [f'bank{pb}'])
                E('scalar', lambda e, sl=sl: e.copy(out=moer_m[sl], in_=pv(1024 * sl, 1024)),
                  reads=[f'bank{2 * sl}', f'bank{2 * sl + 1}'], writes=[f'moerm{sl}'])
                if dbg:
                    E('sync', lambda e, dm=dm, sl=sl: e.dma_start(out=DBG['d_mr'][dm], in_=moer_m[sl]),
                      reads=[f'moerm{sl}'], dma=True, slot='dbg')
                msev = E('sync', lambda e, dm=dm, sl=sl: e.dma_start(out=moesp[dm], in_=moer_m[sl]),
                  reads=[f'moerm{sl}'], dma=True, slot=f'ms{sl}')
                ms_evs.append(msev)

            hv = hbuf.rearrange("p (m c) -> p m c", m=NM_UP)
            for m in range(NM_UP):
                sl = m % 2
                E('gpsimd', lambda e, m=m, sl=sl: e.dma_start(
                    out=wub[sl].rearrange("p (k c) -> p k c", k=KD),
                    in_=IN['wu'][m], max_dma_last_dim=8192),
                  extra=[moe_last_mm] if m < 2 else (), dma=True, slot=f'wu{sl}',
                  writes=[f'wub{sl}'])
                for seg in range(2):
                    pb = 2 * sl + seg
                    items = [(lambda e, k=k, seg=seg, sl=sl, pb=pb: e.matmul(
                        pv(512 * pb, 512), wub[sl].rearrange("p (k c) -> p k c", k=KD)[:, k, :],
                        xn2Tv[:, k, 512 * seg:512 * (seg + 1)],
                        start=(k == 0), stop=(k == KD - 1))) for k in range(KD)]
                    up_last_mm = MMG(items, [f'wub{sl}', 'xn2T'], [f'bank{pb}'])
                E('scalar', lambda e, m=m, sl=sl: e.activation(
                    out=hv[:, m, :], in_=pv(1024 * sl, 1024), func=AF.Relu),
                  reads=[f'bank{2 * sl}', f'bank{2 * sl + 1}'], writes=['hbuf'],
                  extra=[ms_evs[-1]] if m == 0 else ())
                E('vector', lambda e, m=m: e.tensor_tensor(out=hv[:, m, :], in0=hv[:, m, :],
                                                           in1=hv[:, m, :], op=ALU.mult),
                  reads=['hbuf'], writes=['hbuf'])
            if dbg:
                E('gpsimd', lambda e: e.dma_start(out=DBG['d_h'], in_=hv[:, 0, :]),
                  reads=['hbuf'], dma=True, slot='dbg')

            ybv = ybuf.rearrange("p (m c) -> p m c", m=KD)
            for dm in range(KD):
                sl = dm % 2
                E('sync', lambda e, dm=dm: e.dma_start(out=moer[0], in_=moesp[dm]),
                  extra=[ms_evs[dm]] + (ms_evs[-2:] if dm == 0 else []),
                  dma=True, slot='mr0', writes=['moer0'])
                for hk in range(2):
                    wsl = 2 * (dm % 2) + hk
                    E('gpsimd', lambda e, dm=dm, hk=hk, wsl=wsl: e.dma_start(
                        out=wdb[wsl].rearrange("p (k c) -> p k c", k=32),
                        in_=IN['wd'][dm, :, 32 * hk:32 * (hk + 1), :],
                        max_dma_last_dim=8192),
                      extra=[up_last_mm] if dm < 2 else (), dma=True, slot=f'wd{wsl}',
                      writes=[f'wdb{wsl}'])
                for seg in range(2):
                    pb = 2 * sl + seg
                    items = []
                    for kp in range(NM_UP):
                        wsl = 2 * (dm % 2) + kp // 32
                        items.append(lambda e, kp=kp, wsl=wsl, seg=seg, pb=pb: e.matmul(
                            pv(512 * pb, 512),
                            wdb[wsl].rearrange("p (k c) -> p k c", k=32)[:, kp % 32, :],
                            hv[0:128, kp, 512 * seg:512 * (seg + 1)],
                            start=(kp == 0), stop=(kp == NM_UP - 1)))
                    dn_last_mm = MMG(items, [f'wdb{2 * (dm % 2)}', f'wdb{2 * (dm % 2) + 1}', 'hbuf'],
                                     [f'bank{pb}'])
                for seg in range(2):
                    pb = 2 * sl + seg
                    E('vector', lambda e, dm=dm, seg=seg, pb=pb: e.scalar_tensor_tensor(
                        out=ybv[:, dm, 512 * seg:512 * (seg + 1)], in0=pv(512 * pb, 512),
                        scalar=1.0, in1=moer[0][:, 512 * seg:512 * (seg + 1)],
                        op0=ALU.mult, op1=ALU.add),
                      reads=[f'bank{pb}', 'moer0'], writes=['ybuf'])

            if dbg:
                E('gpsimd', lambda e: e.dma_start(out=DBG['d_yb'], in_=ybuf[:]),
                  reads=['ybuf'], dma=True, slot='dbg')
                E('sync', lambda e: e.dma_start(out=DBG['d_moe'], in_=moesp), extra=ms_evs, dma=True, slot='dbg')
            ytv = ytok.rearrange("p (t d) -> p t d", t=NQT)
            dn_dep = [dn_last_mm]
            for dm in range(KD):
                for t0 in range(0, NQT, 4):
                    pbank = 4 + (t0 // 4) % 2
                    items = [(lambda e, tt=t0 + i, i=i, dm=dm, pbank=pbank: e.matmul(
                        pv(pbank * 512 + 128 * i, 128), ybv[:, dm, 128 * tt:128 * (tt + 1)],
                        identb[:], start=True, stop=True)) for i in range(4)]
                    MMG(items, ['ybuf', 'identb'], [f'bank{pbank}'])
                    E('scalar', lambda e, t0=t0, dm=dm, pbank=pbank: e.copy(
                        out=ytv[:, t0:t0 + 4, 128 * dm:128 * (dm + 1)], in_=pv(pbank * 512, 512)),
                      reads=[f'bank{pbank}'], writes=['ytok'], extra=dn_dep)
                    dn_dep = ()
            if dbg:
                for tt in range(NQT):
                    E('sync', lambda e, tt=tt: e.dma_start(
                        out=DBG['d_y'][128 * tt:128 * (tt + 1)], in_=ytv[:, tt, :]),
                      reads=['ytok'], dma=True, slot='dbg')

            E('sync', lambda e: e.dma_start(out=gamv, in_=IN['gammab']), dma=True, slot='gam',
              writes=['gamv'], extra=[dn_last_mm])
            E('sync', lambda e: e.dma_start(out=betv, in_=IN['betab']), dma=True, slot='bet',
              writes=['betv'])
            outevs = []

            def tA(i):
                tt = i
                par = i % 2
                cc = LNC[par]
                bn, mvc = cc['bn'], cc['mv']
                for ss in range(4):
                    E('vector', lambda e, ss=ss, tt=tt, bn=bn: e.bn_stats(
                        stat[:, bn + 6 * ss:bn + 6 * ss + 6], ytv[:, tt, 512 * ss:512 * (ss + 1)]),
                      reads=['ytok'], writes=[f'st_bn{par}'])
                E('vector', lambda e, bn=bn, mvc=mvc: e.bn_aggr(stat[:, mvc:mvc + 2],
                                                                stat[:, bn:bn + 24]),
                  reads=[f'st_bn{par}'], writes=[f'st_mv{par}'])

            def tB(i):
                tt = i
                sl = tt % 2
                par = i % 2
                cc = LNC[par]
                mvc, rs = cc['mv'], cc['rstd']
                E('sync', lambda e, tt=tt, sl=sl: e.dma_start(
                    out=x1r[sl], in_=x1sp[128 * tt:128 * (tt + 1)]),
                  extra=list(x1evs) + [dn_last_mm], dma=True, slot=f'x1r{sl}',
                  writes=[f'x1r{sl}'])
                E('scalar', lambda e, mvc=mvc, rs=rs: e.activation(
                    out=stat[:, rs:rs + 1], in_=stat[:, mvc + 1:mvc + 2],
                    func=AF.Ln, bias=stat[:, 0:1], scale=1.0),
                  reads=[f'st_mv{par}', 'st_eps'], writes=[f'st_r0{par}'])
                E('scalar', lambda e, rs=rs: e.activation(out=stat[:, rs:rs + 1],
                                                          in_=stat[:, rs:rs + 1],
                                                          func=AF.Exp, scale=-0.5),
                  reads=[f'st_r0{par}'], writes=[f'st_rstd{par}'])
                E('vector', lambda e, tt=tt, mvc=mvc, rs=rs: e.tensor_scalar(
                    out=ytv[:, tt, :], in0=ytv[:, tt, :],
                    scalar1=stat[:, mvc:mvc + 1], scalar2=stat[:, rs:rs + 1],
                    op0=ALU.subtract, op1=ALU.mult),
                  reads=['ytok', f'st_mv{par}', f'st_rstd{par}'], writes=['ytok'])

            def tC(i):
                tt = i
                sl = tt % 2
                E('gpsimd', lambda e, tt=tt: e.tensor_tensor(out=ytv[:, tt, :], in0=ytv[:, tt, :],
                                                             in1=gamv, op=ALU.mult),
                  reads=['ytok', 'gamv'], writes=['ytok'])
                E('vector', lambda e, tt=tt: e.tensor_tensor(out=ytv[:, tt, :], in0=ytv[:, tt, :],
                                                             in1=betv, op=ALU.add),
                  reads=['ytok', 'betv'], writes=['ytok'])
                E('vector', lambda e, tt=tt, sl=sl: e.tensor_tensor(
                    out=outt[sl], in0=ytv[:, tt, :], in1=x1r[sl], op=ALU.add),
                  reads=['ytok', f'x1r{sl}'], writes=[f'outt{sl}'])
                ev = E('sync', lambda e, tt=tt, sl=sl: e.dma_start(
                    out=out[128 * tt:128 * (tt + 1)], in_=outt[sl]),
                  reads=[f'outt{sl}'], dma=True, slot=f'out{tt % 2}')
                outevs.append(ev)

            for step in range(NQT + 2):
                if step < NQT:
                    tA(step)
                if 0 <= step - 1 < NQT:
                    tB(step - 1)
                if 0 <= step - 2 < NQT:
                    tC(step - 2)
            pr.final_wait('sync', outevs)
            pr.emit(block)
    return nc



_CACHED = {}


def _get_nc(dbg=False):
    if dbg not in _CACHED:
        _CACHED[dbg] = build_program(dbg=dbg)
    return _CACHED[dbg]


def kernel(**inputs):
    W = {}
    for k, v in inputs.items():
        if k == 'L0_attn_x':
            x_full = np.asarray(v, np.float32)
        elif k.startswith('L0_attn_'):
            W[k[len('L0_attn_'):]] = np.asarray(v)
        else:
            W[k[len('L0_ffn_'):]] = np.asarray(v)
    shared = host_shared_weights(W)
    in_maps = []
    qposs = []
    for core in range(8):
        d, qpos = host_core_inputs(core, x_full, shared)
        in_maps.append(d)
        qposs.append(qpos)
    nc = _get_nc(dbg=False)
    res = run_bass_kernel_spmd(nc, in_maps, core_ids=list(range(8)))
    outp = np.zeros((B, T, D), np.float32)
    for core in range(8):
        outp[core // 4, qposs[core]] = np.asarray(res.results[core]["out"])
    return outp


# version 23
# speedup vs baseline: 1.0387x; 1.0387x over previous
import numpy as np
import ml_dtypes
from contextlib import ExitStack
import concourse.bass as bass
import concourse.mybir as mybir
from concourse.bass_utils import run_bass_kernel_spmd
from concourse.alu_op_type import AluOpType as ALU

F32 = mybir.dt.float32
BF16 = mybir.dt.bfloat16
U8 = mybir.dt.uint8
AF = mybir.ActivationFunctionType

B, T, D = 2, 4096, 2048
H, HKV, DDS, DDG, DV = 16, 4, 32, 32, 64
MBK, LW = 64, 1024
REMOTE = T - LW
NPOOL = REMOTE // MBK
NDUM = 16
KEYS = NPOOL + NDUM + LW
KEYSP = 1152
NKT = 9
KTSZ = [128] * 8 + [64]
KTOF = [128 * i for i in range(NKT)]
TQ = 1024
NQT = 8
KD = 16
CH = 256
NCH = T // CH
NM_UP = 64
ROPE_BASE = 10000.0
INTERLEAVE = True
WJ = [1088, 64, 64, 64, 64, 64, 576, 1088] if INTERLEAVE else [KEYS] * 8


def kt_list(w):
    out = [128] * (w // 128)
    if w % 128:
        out.append(w % 128)
    return out
EPS = 1e-5
LN32H = -0.5 * float(np.log(32.0))

ENGINES = ("sync", "gpsimd", "tensor", "vector", "scalar")


class Prog:

    def __init__(self, nc, slots=()):
        self.nc = nc
        self.records = {e: [] for e in ENGINES}
        self.counts = {e: 0 for e in ENGINES}
        self.events = {}
        self.sems = {}
        self.slots = list(slots)
        self.slot_evs = {sl: [] for sl in slots}
        self.highwater = {}
        self.elide = False
        self._n = 0

    def make_sems(self, stack):
        for e in ENGINES:
            self.sems[e] = stack.enter_context(self.nc.semaphore(f"sb_{e}"))
        for sl in self.slots:
            self.sems[sl] = stack.enter_context(self.nc.semaphore(f"dq_{sl}"))
            self.counts[sl] = 0

    def op(self, eng, fn, wait=(), ev=None, dma=False, slot=None):
        recs = self.records[eng]
        if isinstance(wait, str):
            wait = [wait]
        need = {}
        for w in wait:
            if w is None:
                continue
            we, wv = self.events[w]
            if need.get(we, -1) < wv:
                need[we] = wv
        hw = self.highwater.setdefault(eng, {})
        for we, wv in need.items():
            if self.elide and hw.get(we, -1) >= wv:
                continue
            if hw.get(we, -1) < wv:
                hw[we] = wv
            recs.append(("wait", we, wv))
        if ev is None:
            ev = f"_e{self._n}"
            self._n += 1
        if dma:
            assert slot is not None, "dma ops need a slot"
            self.counts[slot] += 16
            self.events[ev] = (slot, self.counts[slot])
            self.slot_evs[slot].append(ev)
            recs.append(("op", fn, (slot, 16)))
        else:
            self.counts[eng] += 1
            self.events[ev] = (eng, self.counts[eng])
            recs.append(("op", fn, (eng, 1)))
        return ev

    def finalize_slot(self, slot):
        total = self.counts[slot]
        for ev in self.slot_evs[slot]:
            self.events[ev] = (slot, total)

    def barrier(self):
        heads = [(nm, c) for nm, c in self.counts.items() if c > 0]
        for e in ENGINES:
            hw = self.highwater.setdefault(e, {})
            for we, wv in heads:
                if we != e and hw.get(we, -1) < wv:
                    hw[we] = wv
                    self.records[e].append(("wait", we, wv))

    def final_wait(self, eng, evs):
        need = {}
        for w in evs:
            we, wv = self.events[w]
            if need.get(we, -1) < wv:
                need[we] = wv
        for we, wv in need.items():
            self.records[eng].append(("wait", we, wv))

    def emit(self, block):
        for eng in ENGINES:
            if not self.records[eng]:
                continue

            def make(eng):
                def body(e):
                    for rec in self.records[eng]:
                        if rec[0] == "wait":
                            e.wait_ge(self.sems[rec[1]], rec[2])
                        else:
                            semname, inc = rec[2]
                            rec[1](e).then_inc(self.sems[semname], inc)
                return body

            getattr(block, eng)(make(eng))


class Buf:

    def __init__(self):
        self.writes = []
        self.readers = []

    def w(self):
        return self.readers + self.writes

    def wrote(self, ev):
        self.writes.append(ev)
        self.readers = []

    def r(self):
        return list(self.writes)

    def read(self, ev):
        self.readers.append(ev)



def rope_tables(Tlen):
    inv = 1.0 / ROPE_BASE ** (np.arange(0, DDG, 2, dtype=np.float64) / DDG)
    ang = np.arange(Tlen, dtype=np.float64)[:, None] * inv[None, :]
    return np.cos(ang).astype(np.float32), np.sin(ang).astype(np.float32)


def n_visible(q):
    npool = min(NPOOL, (q - 63) // 64 + 1) if q >= 63 else 0
    return npool, max(0, q - (REMOTE - 1))


def host_shared_weights(W):
    def tileKM(Wm, nk, nm):
        return np.ascontiguousarray(Wm.reshape(nk, 128, nm, 128).transpose(2, 1, 0, 3))

    wq_cat = np.concatenate(
        [np.concatenate([W['Wq_sem'][:, h * DDS:(h + 1) * DDS],
                         W['Wq_geo'][:, h * DDG:(h + 1) * DDG]], 1) for h in range(H)], 1)
    wk_cat = np.concatenate(
        [np.concatenate([W['Wk_sem'][:, j * DDS:(j + 1) * DDS],
                         W['Wk_geo'][:, j * DDG:(j + 1) * DDG]], 1) for j in range(HKV)], 1)
    cos_k, sin_k = rope_tables(T)
    spm_k = np.concatenate([-sin_k, sin_k], 1)
    ktc = np.ascontiguousarray(cos_k.reshape(32, 128, 16).transpose(1, 0, 2).reshape(128, 512))
    kts = np.ascontiguousarray(spm_k.reshape(32, 128, 32).transpose(1, 0, 2).reshape(128, 1024))
    ls = np.asarray(W['logit_scale'], np.float32)
    lsrep = np.ascontiguousarray(
        np.stack([np.repeat(ls[2 * i:2 * i + 2], 64) for i in range(8)], axis=1))
    poolm = np.zeros((128, 2), np.float32)
    poolm[0:64, 0] = 1.0 / MBK
    poolm[64:128, 1] = 1.0 / MBK
    return {
        "wq": np.ascontiguousarray(wq_cat.reshape(KD, 128, 1024)),
        "wk": np.ascontiguousarray(wk_cat.reshape(KD, 128, 256)),
        "wv": np.ascontiguousarray(np.asarray(W['Wv'], np.float32).reshape(KD, 128, 256)),
        "wo": np.ascontiguousarray(np.asarray(W['Wo'], np.float32).reshape(8, 128, 2048)),
        "wg": np.ascontiguousarray(np.asarray(W['gate_w'], np.float32).reshape(KD, 128, 8)),
        "wu": tileKM(np.asarray(W['up_w'], np.float32), KD, NM_UP),
        "wd": tileKM(np.asarray(W['down_w'], np.float32), NM_UP, KD),
        "we": tileKM(np.asarray(W['expert_weight'], np.float32), KD, KD),
        "ident": np.eye(128, dtype=np.float32),
        "lsrep": lsrep,
        "poolm": poolm,
        "ktc": ktc, "kts": kts,
        "gammab": np.ascontiguousarray(
            np.broadcast_to(np.asarray(W['ln_gamma'], np.float32)[None, :], (128, D))),
        "betab": np.ascontiguousarray(
            np.broadcast_to(np.asarray(W['ln_beta'], np.float32)[None, :], (128, D))),
    }


def host_core_inputs(core, x_full, shared):
    b = core // 4
    cc = core % 4
    if INTERLEAVE:
        qpos = np.concatenate([np.arange((cc + 4 * j) * 128, (cc + 4 * j) * 128 + 128)
                               for j in range(NQT)])
    else:
        qpos = np.arange(cc * TQ, cc * TQ + TQ)
    cos_k, sin_k = rope_tables(T)
    spm_k = np.concatenate([-sin_k, sin_k], 1)
    qtc = np.ascontiguousarray(cos_k[qpos].reshape(8, 128, 16).transpose(1, 0, 2).reshape(128, 128))
    qts = np.ascontiguousarray(spm_k[qpos].reshape(8, 128, 32).transpose(1, 0, 2).reshape(128, 256))
    bm = np.full((TQ, KEYS), -1e9, np.float32)
    bm[:, NPOOL:NPOOL + NDUM] = -2e9
    for i, q in enumerate(qpos):
        npool, nloc = n_visible(q)
        bm[i, :npool] = 1e9
        bm[i, NPOOL + NDUM:NPOOL + NDUM + nloc] = 1e9
    d = dict(shared)
    d["xb"] = np.ascontiguousarray(x_full[b])
    d["xq"] = np.ascontiguousarray(x_full[b][qpos])
    d["qtc"] = qtc
    d["qts"] = qts
    d["bmask"] = np.ascontiguousarray(bm.reshape(NQT, 128, KEYS))
    return d, qpos



def build_program(dbg=False):
    nc = bass.Bass("TRN2", target_bir_lowering=False)
    IN = {}
    for name, shape in [
        ("xb", [T, D]), ("xq", [TQ, D]),
        ("wq", [KD, 128, 1024]), ("wk", [KD, 128, 256]), ("wv", [KD, 128, 256]),
        ("wo", [8, 128, 2048]), ("wg", [KD, 128, 8]),
        ("wu", [NM_UP, 128, KD, 128]), ("wd", [KD, 128, NM_UP, 128]),
        ("we", [KD, 128, KD, 128]),
        ("ident", [128, 128]), ("lsrep", [128, 8]), ("poolm", [128, 2]),
        ("ktc", [128, 512]), ("kts", [128, 1024]), ("qtc", [128, 128]), ("qts", [128, 256]),
        ("bmask", [NQT, 128, KEYS]), ("gammab", [128, D]), ("betab", [128, D]),
    ]:
        IN[name] = nc.dram_tensor(name, shape, F32, kind="ExternalInput").ap()
    out = nc.dram_tensor("out", [TQ, D], F32, kind="ExternalOutput").ap()
    x1sp = nc.dram_tensor("x1sp", [TQ, D], F32, kind="Internal").ap()
    moesp = nc.dram_tensor("moesp", [KD, 128, TQ], F32, kind="Internal").ap()
    DBG = {}
    if dbg:
        for name, shape in [("d_kt", [4, 128, KEYSP]), ("d_v2", [128, NKT * 256]),
                            ("d_qt", [128, NQT * 1024]), ("d_x1", [TQ, D]),
                            ("d_ot", [128, 8 * 1024]), ("d_xn2", [TQ, D]),
                            ("d_scale", [128, NQT]), ("d_y", [TQ, D]),
                            ("d_p", [128, KEYS]), ("d_h", [128, 1024]),
                            ("d_yb", [128, KD * 1024]), ("d_moe", [KD, 128, TQ]),
                            ("d_xs", [128, KD * 1024]), ("d_sd", [128, 8 * 128]),
                            ("d_we", [KD, 128, KD * 128]), ("d_mr", [KD, 128, TQ])]:
            DBG[name] = nc.dram_tensor(name, shape, F32, kind="ExternalOutput").ap()

    SLOTS = [*[f'ini{i}' for i in range(1, 12)], 'dbg', 'xch0', 'xch1', 'wob', 'bm0', 'bm1', 'xq0', 'xq1',
             'x1s0', 'x1s1', 'we0', 'we1', 'ms0', 'ms1', 'wu0', 'wu1',
             'mr0', 'mr1', 'wd0', 'wd1', 'wd2', 'wd3', 'gam', 'bet', 'x1r0', 'x1r1',
             'out0', 'out1', 'xch2']
    pr = Prog(nc, slots=SLOTS)
    stack = ExitStack()
    bufs = {}

    def gb(name):
        if name not in bufs:
            bufs[name] = Buf()
        return bufs[name]

    with stack:
        tabs = stack.enter_context(nc.sbuf_tensor("tabs", [128, 1920], F32))
        ktc_v, kts_v = tabs[:, 0:512], tabs[:, 512:1536]
        qtc_v, qts_v = tabs[:, 1536:1664], tabs[:, 1664:1920]
        identb = stack.enter_context(nc.sbuf_tensor("identb", [128, 128], BF16))
        poolb = stack.enter_context(nc.sbuf_tensor("poolb", [128, 2], BF16))
        wgf = stack.enter_context(nc.sbuf_tensor("wgf", [128, KD * 8], F32))
        wgh = stack.enter_context(nc.sbuf_tensor("wgh", [128, KD * 8], BF16))
        wgl = stack.enter_context(nc.sbuf_tensor("wgl", [128, KD * 8], BF16))
        svec = stack.enter_context(nc.sbuf_tensor("svec", [128, 8], F32))
        lsb = stack.enter_context(nc.sbuf_tensor("lsb", [128, 8], F32))
        stat = stack.enter_context(nc.sbuf_tensor("stat", [128, 96], F32))
        LNC = [dict(bn=8, mv=32, rstd=34), dict(bn=40, mv=64, rstd=66)]
        ARN = 203392
        arena = stack.enter_context(nc.sbuf_tensor("arena", [128, ARN], U8))

        def av(off, n, dt):
            sz = 4 if dt == F32 else 2
            assert off % 4 == 0 and off + n * sz <= ARN, (off, n, sz)
            return arena[:, off:off + n * sz].bitcast(dt)

        K = 1024
        KT = [av(114 * K + j * 2 * KEYSP, KEYSP, BF16) for j in range(4)]
        V2 = av(114 * K + 9216, NKT * 256, BF16)
        QT = av(114 * K + 9216 + 4608, NQT * 1024, BF16)
        xch = [av(0 + i * 16 * K, 2 * 2048, F32) for i in range(2)] + [av(166 * K, 2 * 2048, F32)]
        xnb = [av(32 * K + i * 8 * K, 2 * 2048, BF16) for i in range(2)] + [av(182 * K, 2 * 2048, BF16)]
        xnT = [av(48 * K + i * 8 * K, KD * CH, BF16) for i in range(2)] + [av(190 * K, KD * CH, BF16)]
        wqb = av(64 * K, KD * 1024, BF16)
        wkb = av(96 * K, KD * 256, BF16)
        wvb = av(104 * K, KD * 256, BF16)
        kbf = [av(144 * K + i * K, 2 * 256, BF16) for i in range(2)] + [av(112 * K, 2 * 256, BF16)]
        kswp = [av(146 * K + i * 512, 2 * 128, BF16) for i in range(2)] + [av(113 * K, 2 * 128, BF16)]
        xnpT = av(148 * K, KD * 48, BF16)
        qfb = [av(150 * K + i * 4 * K, 2 * 1024, BF16) for i in range(2)]
        qswp = [av(158 * K + i * 2 * K, 2 * 512, BF16) for i in range(2)]
        bmv = [av(0 + i * 5 * K, KEYS, F32) for i in range(2)]
        negS = [av(10 * K + i * 5 * K, KEYS, F32) for i in range(2)]
        Pb = [av(20 * K + i * 3 * K, KEYSP, BF16) for i in range(2)]
        PTs = [av(26 * K + i * 3 * K, KEYSP, BF16) for i in range(2)] + [av(98 * K, KEYSP, BF16)]
        diagb = [av(32 * K + i * 512, 128, BF16) for i in range(2)]
        OT = av(34 * K, 8 * 1024, BF16)
        wob = av(50 * K, 8 * 2048, BF16)
        xqt = [av(82 * K + i * 8 * K, 2048, F32) for i in range(2)]
        x1t = [av(98 * K + i * 8 * K, 2048, F32) for i in range(2)]
        xn2 = av(144 * K, NQT * 2048, BF16)
        xn2f = [av(116 * K + i * 8 * K, 2048, F32) for i in range(2)]
        xn2lo = av(0, NQT * 2048, BF16)
        xn2T = av(32 * K, KD * 1024, BF16)
        xn2sT = av(0, KD * 1024, BF16)
        xloT = [av(64 * K + i * 4 * K, KD * 128, BF16) for i in range(2)]
        wdb = [av(32 * K + i * 8 * K, 32 * 128, BF16) for i in range(4)]
        logitsT = av(82 * K, 1024, BF16)
        sdiag = [av(84 * K + j * 512, 128, BF16) for j in range(NQT)]
        hbuf = av(ARN - 131072, NM_UP * 1024, BF16)
        web = [av(90 * K + i * 4 * K, KD * 128, BF16) for i in range(2)]
        wub = [av(0 + i * 4 * K, KD * 128, BF16) for i in range(2)]
        moer_m = [av(64 * K + i * 4 * K, 1024, F32) for i in range(2)]
        moer = [av(64 * K, 1024, F32) for i in range(2)]
        assert 64 * K + 4096 <= ARN - 131072
        ybuf = av(0, KD * 1024, BF16)
        ytok = av(72 * K, NQT * 2048, F32)
        gamv = av(136 * K, 2048, F32)
        betv = av(144 * K, 2048, F32)
        x1r = [av(152 * K + i * 8 * K, 2048, F32) for i in range(2)]
        outt = [av(168 * K + i * 8 * K, 2048, F32) for i in range(2)]

        PSA = stack.enter_context(nc.psum_tensor("PSA", [128, 4096], F32))

        def pv(c0, n, p0=0, p1=128):
            return PSA[p0:p1, c0:c0 + n]

        def pbanks(c0, n):
            return [gb(f"bank{i}") for i in range(c0 // 512, (c0 + n + 511) // 512)]

        pr.make_sems(stack)

        def E(eng, fn, reads=(), writes=(), extra=(), dma=False, slot=None):
            wait = list(extra)
            for bnames in reads:
                wait += gb(bnames).r() if isinstance(bnames, str) else bnames.r()
            for bnames in writes:
                wait += gb(bnames).w() if isinstance(bnames, str) else bnames.w()
            ev = pr.op(eng, fn, wait=wait, dma=dma, slot=slot)
            for bnames in reads:
                (gb(bnames) if isinstance(bnames, str) else bnames).read(ev)
            for bnames in writes:
                (gb(bnames) if isinstance(bnames, str) else bnames).wrote(ev)
            return ev

        def MMG(items, reads, writes):
            wait = []
            for b in reads:
                wait += gb(b).r() if isinstance(b, str) else b.r()
            for b in writes:
                wait += gb(b).w() if isinstance(b, str) else b.w()
            for i, fn in enumerate(items[:-1]):
                pr.op('tensor', fn, wait=wait if i == 0 else ())
            ev = pr.op('tensor', items[-1], wait=wait if len(items) == 1 else ())
            for b in reads:
                (gb(b) if isinstance(b, str) else b).read(ev)
            for b in writes:
                (gb(b) if isinstance(b, str) else b).wrote(ev)
            return ev

        with nc.Block() as block:
            E('vector', lambda e: e.memset(stat[:, 0:1], EPS), writes=['st_eps'])
            E('vector', lambda e: e.memset(stat[:, 1:2], LN32H), writes=['st_ln32'])
            for j in range(4):
                E('vector', lambda e, j=j: e.memset(KT[j][:, :], 0.0), writes=[f'KT{j}'])
            E('vector', lambda e: e.memset(V2[:, :], 0.0), writes=['V2'])
            E('gpsimd', lambda e: e.dma_start(out=identb[:], in_=IN['ident']), dma=True, slot='ini1', writes=['identb'])
            E('gpsimd', lambda e: e.dma_start(out=poolb[:], in_=IN['poolm']), dma=True, slot='ini2', writes=['poolb'])
            E('sync', lambda e: e.dma_start(out=wgf[:], in_=IN['wg'].rearrange("k p c -> p k c")),
              dma=True, slot='ini3', writes=['wgf'])
            E('scalar', lambda e: e.copy(out=wgh[:], in_=wgf[:]), reads=['wgf'], writes=['wgh'])
            E('vector', lambda e: e.tensor_tensor(out=wgl[:], in0=wgf[:], in1=wgh[:],
                                                  op=ALU.subtract),
              reads=['wgf', 'wgh'], writes=['wgl'])
            E('gpsimd', lambda e: e.dma_start(out=wqb.rearrange("p (k c) -> p k c", k=KD),
                                              in_=IN['wq'].rearrange("k p c -> p k c")),
              dma=True, slot='ini4', writes=['wqb'])
            E('gpsimd', lambda e: e.dma_start(out=wkb.rearrange("p (k c) -> p k c", k=KD),
                                              in_=IN['wk'].rearrange("k p c -> p k c")),
              dma=True, slot='ini5', writes=['wkb'])
            E('gpsimd', lambda e: e.dma_start(out=wvb.rearrange("p (k c) -> p k c", k=KD),
                                              in_=IN['wv'].rearrange("k p c -> p k c")),
              dma=True, slot='ini6', writes=['wvb'])
            E('sync', lambda e: e.dma_start(out=ktc_v, in_=IN['ktc']), dma=True, slot='ini7', writes=['ktc'])
            E('sync', lambda e: e.dma_start(out=kts_v, in_=IN['kts']), dma=True, slot='ini8', writes=['kts'])
            E('sync', lambda e: e.dma_start(out=qtc_v, in_=IN['qtc']), dma=True, slot='ini9', writes=['qtc'])
            E('sync', lambda e: e.dma_start(out=qts_v, in_=IN['qts']), dma=True, slot='ini10', writes=['qts'])
            E('sync', lambda e: e.dma_start(out=lsb[:], in_=IN['lsrep']), dma=True, slot='ini11', writes=['lsb'])
            E('scalar', lambda e: e.activation(out=svec[:], in_=lsb[:], func=AF.Exp,
                                               bias=stat[:, 1:2], scale=1.0),
              reads=['lsb', 'st_ln32'], writes=['svec'])

            def emit_ln(xf32_tile, dst_bf, rdbufs, wrbuf, par=0):
                c = LNC[par]
                bn, mvc, rs = c['bn'], c['mv'], c['rstd']
                for s in range(4):
                    E('vector', lambda e, s=s: e.bn_stats(stat[:, bn + 6 * s:bn + 6 * s + 6],
                                                          xf32_tile[:, 512 * s:512 * (s + 1)]),
                      reads=rdbufs, writes=[f'st_bn{par}'])
                E('vector', lambda e: e.bn_aggr(stat[:, mvc:mvc + 2], stat[:, bn:bn + 24]),
                  reads=[f'st_bn{par}'], writes=[f'st_mv{par}'])
                E('scalar', lambda e: e.activation(out=stat[:, rs:rs + 1], in_=stat[:, mvc + 1:mvc + 2],
                                                   func=AF.Ln, bias=stat[:, 0:1], scale=1.0),
                  reads=[f'st_mv{par}', 'st_eps'], writes=[f'st_r0{par}'])
                E('scalar', lambda e: e.activation(out=stat[:, rs:rs + 1], in_=stat[:, rs:rs + 1],
                                                   func=AF.Exp, scale=-0.5),
                  reads=[f'st_r0{par}'], writes=[f'st_rstd{par}'])
                E('vector', lambda e: e.tensor_scalar(out=dst_bf, in0=xf32_tile,
                                                      scalar1=stat[:, mvc:mvc + 1],
                                                      scalar2=stat[:, rs:rs + 1],
                                                      op0=ALU.subtract, op1=ALU.mult),
                  reads=rdbufs + [f'st_mv{par}', f'st_rstd{par}'], writes=[wrbuf])

            def emit_transposes(src_bf16_tile, nk, dst_fn, wrbuf, rdbuf, scale_col=None,
                                rhs=None, rhsbuf='identb'):
                rhs_ap = identb[:] if rhs is None else rhs
                for k0 in range(0, nk, 4):
                    kn = min(4, nk - k0)
                    pbank = 4 + (k0 // 4) % 2
                    items = [
                        (lambda e, k=k0 + i, pbank=pbank, i=i: e.matmul(
                            pv(pbank * 512 + i * 128, 128),
                            src_bf16_tile[:, 128 * k:128 * (k + 1)], rhs_ap,
                            start=True, stop=True))
                        for i in range(kn)]
                    mev = MMG(items, [rdbuf, rhsbuf], [f'bank{pbank}'])
                    if scale_col is None:
                        E('scalar', lambda e, k0=k0, kn=kn, pbank=pbank: e.copy(
                            out=dst_fn(k0, kn), in_=pv(pbank * 512, kn * 128)),
                          reads=[f'bank{pbank}'], writes=[wrbuf])
                    else:
                        for i in range(kn):
                            E('scalar', lambda e, k=k0 + i, i=i, pbank=pbank: e.activation(
                                out=dst_fn(k, 1), in_=pv(pbank * 512 + i * 128, 128),
                                func=AF.Copy, scale=scale_col(k)),
                              reads=[f'bank{pbank}', 'svec'], writes=[wrbuf])

            def emit_rope(eng, tile_bf, nh, swp, cos_ap, spm_ap, rdbuf, swpbuf):
                geo = tile_bf.rearrange("p (h x) -> p h x", h=nh)[:, :, 32:64] \
                    .rearrange("p h (t d) -> p h t d", t=2)
                sw = swp.rearrange("p (h x) -> p h x", h=nh).rearrange("p h (t d) -> p h t d", t=2)
                cosb = cos_ap.unsqueeze(1).unsqueeze(2).broadcast_to([128, nh, 2, 16])
                spmb = spm_ap.rearrange("p (t d) -> p t d", t=2).unsqueeze(1) \
                    .broadcast_to([128, nh, 2, 16])
                E(eng, lambda e: e.tensor_copy(sw[:, :, 0, :], geo[:, :, 1, :]),
                  reads=[rdbuf], writes=[swpbuf])
                E(eng, lambda e: e.tensor_copy(sw[:, :, 1, :], geo[:, :, 0, :]),
                  reads=[rdbuf], writes=[swpbuf])
                E(eng, lambda e: e.tensor_tensor(out=geo, in0=geo, in1=cosb, op=ALU.mult),
                  reads=[rdbuf, 'ktc', 'qtc'], writes=[rdbuf])
                E(eng, lambda e: e.tensor_tensor(out=sw, in0=sw, in1=spmb, op=ALU.mult),
                  reads=[swpbuf, 'kts', 'qts'], writes=[swpbuf])
                E(eng, lambda e: e.tensor_tensor(out=geo, in0=geo, in1=sw, op=ALU.add),
                  reads=[rdbuf, swpbuf], writes=[rdbuf])

            def p1A(i):
                c, ti = i // 2, i % 2
                sl = c % 3
                if ti == 0:
                    E('sync', lambda e, c=c, sl=sl: e.dma_start(
                        out=xch[sl].rearrange("p (t d) -> p t d", t=2),
                        in_=IN['xb'][CH * c:CH * (c + 1)].rearrange("(t p) d -> p t d", p=128)),
                      dma=True, slot=f'xch{sl}', writes=[f'xch{sl}'])

            def p1B(i):
                c, ti = i // 2, i % 2
                sl = c % 3
                par = i % 2
                cc = LNC[par]
                bn, mvc, rs = cc['bn'], cc['mv'], cc['rstd']
                xt = xch[sl].rearrange("p (t d) -> p t d", t=2)[:, ti, :]
                for ss in range(4):
                    E('vector', lambda e, ss=ss, xt=xt, bn=bn: e.bn_stats(
                        stat[:, bn + 6 * ss:bn + 6 * ss + 6], xt[:, 512 * ss:512 * (ss + 1)]),
                      reads=[f'xch{sl}'], writes=[f'st_bn{par}'])
                E('vector', lambda e, bn=bn, mvc=mvc: e.bn_aggr(stat[:, mvc:mvc + 2],
                                                                stat[:, bn:bn + 24]),
                  reads=[f'st_bn{par}'], writes=[f'st_mv{par}'])
                E('scalar', lambda e, mvc=mvc, rs=rs: e.activation(
                    out=stat[:, rs:rs + 1], in_=stat[:, mvc + 1:mvc + 2],
                    func=AF.Ln, bias=stat[:, 0:1], scale=1.0),
                  reads=[f'st_mv{par}', 'st_eps'], writes=[f'st_r0{par}'])
                E('scalar', lambda e, rs=rs: e.activation(out=stat[:, rs:rs + 1],
                                                          in_=stat[:, rs:rs + 1],
                                                          func=AF.Exp, scale=-0.5),
                  reads=[f'st_r0{par}'], writes=[f'st_rstd{par}'])

            def p1C(i):
                c, ti = i // 2, i % 2
                sl = c % 3
                par = i % 2
                cc = LNC[par]
                mvc, rs = cc['mv'], cc['rstd']
                xt = xch[sl].rearrange("p (t d) -> p t d", t=2)[:, ti, :]
                xnv = xnb[sl].rearrange("p (t d) -> p t d", t=2)[:, ti, :]
                E('vector', lambda e, xt=xt, xnv=xnv, mvc=mvc, rs=rs: e.tensor_scalar(
                    out=xnv, in0=xt, scalar1=stat[:, mvc:mvc + 1], scalar2=stat[:, rs:rs + 1],
                    op0=ALU.subtract, op1=ALU.mult),
                  reads=[f'xch{sl}', f'st_mv{par}', f'st_rstd{par}'], writes=[f'xnb{sl}'])
                xnTv = xnT[sl].rearrange("p (k c) -> p k c", k=KD)
                emit_transposes(xnv, KD,
                                lambda k0, kn, ti=ti, xnTv=xnTv: xnTv[:, k0:k0 + kn,
                                                                      128 * ti:128 * (ti + 1)],
                                f'xnT{sl}', f'xnb{sl}')

            def p1D(i):
                c, ti = i // 2, i % 2
                sl = c % 3
                g = i
                xnv = xnb[sl].rearrange("p (t d) -> p t d", t=2)[:, ti, :]
                xnTv = xnT[sl].rearrange("p (k c) -> p k c", k=KD)
                items = [(lambda e, k=k, ti=ti, xnTv=xnTv: e.matmul(
                    pv(2 * 512, 256), xnTv[:, k, 128 * ti:128 * (ti + 1)],
                    wkb.rearrange("p (k c) -> p k c", k=KD)[:, k, :],
                    start=(k == 0), stop=(k == KD - 1))) for k in range(KD)]
                MMG(items, [f'xnT{sl}', 'wkb'], ['bank2'])
                kbv = kbf[sl].rearrange("p (t c) -> p t c", t=2)[:, ti, :]
                E('scalar', lambda e, kbv=kbv: e.copy(out=kbv, in_=pv(2 * 512, 256)),
                  reads=['bank2'], writes=[f'kbf{sl}'])
                emit_rope('gpsimd', kbv, HKV,
                          kswp[sl].rearrange("p (t c) -> p t c", t=2)[:, ti, :],
                          ktc_v[:, 16 * g:16 * (g + 1)], kts_v[:, 32 * g:32 * (g + 1)],
                          f'kbf{sl}', f'kswp{sl}')

            def p1E(i):
                c, ti = i // 2, i % 2
                sl = c % 3
                g = i
                xnv = xnb[sl].rearrange("p (t d) -> p t d", t=2)[:, ti, :]
                xnTv = xnT[sl].rearrange("p (k c) -> p k c", k=KD)
                kbv = kbf[sl].rearrange("p (t c) -> p t c", t=2)[:, ti, :]
                if c < 12:
                    for j2 in range(2):
                        E('tensor', lambda e, kbv=kbv, j2=j2, g=g: e.matmul(
                            pv(3072 + j2 * 128 + 2 * g, 2), kbv[:, 128 * j2:128 * (j2 + 1)],
                            poolb[:], start=True, stop=True),
                          reads=[f'kbf{sl}', 'poolb'], writes=['bank6'])
                    for dm in range(KD):
                        E('tensor', lambda e, xnv=xnv, dm=dm, ti=ti: e.matmul(
                            pv(3584 + dm * 4 + ti * 2, 2), xnv[:, 128 * dm:128 * (dm + 1)],
                            poolb[:], start=True, stop=True),
                          reads=[f'xnb{sl}', 'poolb'], writes=['bank7'])
                    if ti == 1:
                        xnpTv = xnpT.rearrange("p (k c) -> p k c", k=KD)
                        E('scalar', lambda e, c=c, xnpTv=xnpTv: e.copy(
                            out=xnpTv[:, :, 4 * c:4 * (c + 1)],
                            in_=pv(3584, 64).rearrange("p (k c) -> p k c", k=KD)),
                          reads=['bank7'], writes=['xnpT'])
                else:
                    m = 2 * (c - 12) + ti
                    items = [(lambda e, k=k, ti=ti, xnTv=xnTv: e.matmul(
                        pv(3 * 512, 256), xnTv[:, k, 128 * ti:128 * (ti + 1)],
                        wvb.rearrange("p (k c) -> p k c", k=KD)[:, k, :],
                        start=(k == 0), stop=(k == KD - 1))) for k in range(KD)]
                    MMG(items, [f'xnT{sl}', 'wvb'], ['bank3'])
                    V2v = V2.rearrange("p (t c) -> p t c", t=NKT)
                    kta = (64 + 128 * m) // 128
                    E('scalar', lambda e, kta=kta, V2v=V2v: e.copy(out=V2v[64:128, kta, :],
                                                                   in_=pv(3 * 512, 256, 0, 64)),
                      reads=['bank3'], writes=['V2'])
                    E('scalar', lambda e, kta=kta, V2v=V2v: e.copy(out=V2v[0:64, kta + 1, :],
                                                                   in_=pv(3 * 512, 256, 64, 128)),
                      reads=['bank3'], writes=['V2'])
                    for j2 in range(2):
                        E('tensor', lambda e, kbv=kbv, j2=j2: e.matmul(
                            pv(0, 128), kbv[:, 128 * j2:128 * (j2 + 1)], identb[:],
                            start=True, stop=True),
                          reads=[f'kbf{sl}', 'identb'], writes=['bank0'])
                        co = 64 + 128 * m
                        for jj, (p0, p1) in enumerate([(0, 64), (64, 128)]):
                            jk = 2 * j2 + jj
                            E('scalar', lambda e, jk=jk, co=co, p0=p0, p1=p1: e.copy(
                                out=KT[jk][0:64, co:co + 128], in_=pv(0, 128, p0, p1)),
                              reads=['bank0'], writes=[f'KT{jk}'])
                            E('scalar', lambda e, jk=jk, co=co, p0=p0, p1=p1: e.copy(
                                out=KT[jk][64:128, co:co + 128], in_=pv(0, 128, p0, p1)),
                              reads=['bank0'], writes=[f'KT{jk}'])

            NT1 = 2 * NCH
            NT1 = 2 * NCH
            for step in range(NT1 + 5):
                if step < NT1:
                    p1A(step)
                if 0 <= step - 1 < NT1:
                    p1B(step - 1)
                if 0 <= step - 2 < NT1:
                    p1C(step - 2)
                if 0 <= step - 3 < NT1:
                    p1D(step - 3)
                if 0 <= step - 5 < NT1:
                    p1E(step - 5)
            for j2 in range(2):
                for jj, (p0, p1) in enumerate([(0, 64), (64, 128)]):
                    jk = 2 * j2 + jj
                    E('scalar', lambda e, j2=j2, p0=p0, p1=p1, jk=jk: e.copy(
                        out=KT[jk][0:64, 0:48], in_=pv(3072 + j2 * 128, 48, p0, p1)),
                      reads=['bank6'], writes=[f'KT{jk}'])
                    E('scalar', lambda e, j2=j2, p0=p0, p1=p1, jk=jk: e.copy(
                        out=KT[jk][64:128, 0:48], in_=pv(3072 + j2 * 128, 48, p0, p1)),
                      reads=['bank6'], writes=[f'KT{jk}'])
            xnpTv = xnpT.rearrange("p (k c) -> p k c", k=KD)
            items = [(lambda e, k=k: e.matmul(
                pv(3 * 512, 256, 0, 48), xnpTv[:, k, :],
                wvb.rearrange("p (k c) -> p k c", k=KD)[:, k, :],
                start=(k == 0), stop=(k == KD - 1))) for k in range(KD)]
            MMG(items, ['xnpT', 'wvb'], ['bank3'])
            E('scalar', lambda e: e.copy(out=V2.rearrange("p (t c) -> p t c", t=NKT)[0:48, 0, :],
                                         in_=pv(3 * 512, 256, 0, 48)),
              reads=['bank3'], writes=['V2'])

            def q1A(i):
                qc, ti = i // 2, i % 2
                sl = qc % 2
                if ti == 0:
                    E('sync', lambda e, qc=qc, sl=sl: e.dma_start(
                        out=xch[sl].rearrange("p (t d) -> p t d", t=2),
                        in_=IN['xq'][CH * qc:CH * (qc + 1)].rearrange("(t p) d -> p t d", p=128)),
                      dma=True, slot=f'xch{sl}', writes=[f'xch{sl}'])

            def q1B(i):
                qc, ti = i // 2, i % 2
                sl = qc % 2
                par = i % 2
                cc = LNC[par]
                bn, mvc, rs = cc['bn'], cc['mv'], cc['rstd']
                xt = xch[sl].rearrange("p (t d) -> p t d", t=2)[:, ti, :]
                for ss in range(4):
                    E('vector', lambda e, ss=ss, xt=xt, bn=bn: e.bn_stats(
                        stat[:, bn + 6 * ss:bn + 6 * ss + 6], xt[:, 512 * ss:512 * (ss + 1)]),
                      reads=[f'xch{sl}'], writes=[f'st_bn{par}'])
                E('vector', lambda e, bn=bn, mvc=mvc: e.bn_aggr(stat[:, mvc:mvc + 2],
                                                                stat[:, bn:bn + 24]),
                  reads=[f'st_bn{par}'], writes=[f'st_mv{par}'])
                E('scalar', lambda e, mvc=mvc, rs=rs: e.activation(
                    out=stat[:, rs:rs + 1], in_=stat[:, mvc + 1:mvc + 2],
                    func=AF.Ln, bias=stat[:, 0:1], scale=1.0),
                  reads=[f'st_mv{par}', 'st_eps'], writes=[f'st_r0{par}'])
                E('scalar', lambda e, rs=rs: e.activation(out=stat[:, rs:rs + 1],
                                                          in_=stat[:, rs:rs + 1],
                                                          func=AF.Exp, scale=-0.5),
                  reads=[f'st_r0{par}'], writes=[f'st_rstd{par}'])

            def q1C(i):
                qc, ti = i // 2, i % 2
                sl = qc % 2
                par = i % 2
                cc = LNC[par]
                mvc, rs = cc['mv'], cc['rstd']
                xt = xch[sl].rearrange("p (t d) -> p t d", t=2)[:, ti, :]
                xnv = xnb[sl].rearrange("p (t d) -> p t d", t=2)[:, ti, :]
                E('vector', lambda e, xt=xt, xnv=xnv, mvc=mvc, rs=rs: e.tensor_scalar(
                    out=xnv, in0=xt, scalar1=stat[:, mvc:mvc + 1], scalar2=stat[:, rs:rs + 1],
                    op0=ALU.subtract, op1=ALU.mult),
                  reads=[f'xch{sl}', f'st_mv{par}', f'st_rstd{par}'], writes=[f'xnb{sl}'])
                xnTv = xnT[sl].rearrange("p (k c) -> p k c", k=KD)
                emit_transposes(xnv, KD,
                                lambda k0, kn, ti=ti, xnTv=xnTv: xnTv[:, k0:k0 + kn,
                                                                      128 * ti:128 * (ti + 1)],
                                f'xnT{sl}', f'xnb{sl}')

            def q1D(i):
                qc, ti = i // 2, i % 2
                sl = qc % 2
                jt = i
                xnTv = xnT[sl].rearrange("p (k c) -> p k c", k=KD)
                for seg in range(2):
                    items = [(lambda e, k=k, seg=seg, ti=ti, xnTv=xnTv: e.matmul(
                        pv(1024 + 512 * seg, 512), xnTv[:, k, 128 * ti:128 * (ti + 1)],
                        wqb.rearrange("p (k c) -> p k c", k=KD)[:, k, 512 * seg:512 * (seg + 1)],
                        start=(k == 0), stop=(k == KD - 1))) for k in range(KD)]
                    MMG(items, [f'xnT{sl}', 'wqb'], [f'bank{2 + seg}'])
                qv = qfb[sl].rearrange("p (t c) -> p t c", t=2)[:, ti, :]
                E('scalar', lambda e, qv=qv: e.copy(out=qv, in_=pv(1024, 1024)),
                  reads=['bank2', 'bank3'], writes=[f'qfb{sl}'])
                emit_rope('vector', qv, H,
                          qswp[sl].rearrange("p (t c) -> p t c", t=2)[:, ti, :],
                          qtc_v[:, 16 * jt:16 * (jt + 1)], qts_v[:, 32 * jt:32 * (jt + 1)],
                          f'qfb{sl}', f'qswp{sl}')

            def q1E(i):
                qc, ti = i // 2, i % 2
                sl = qc % 2
                jt = i
                qv = qfb[sl].rearrange("p (t c) -> p t c", t=2)[:, ti, :]
                QTv = QT.rearrange("p (m c) -> p m c", m=NQT)
                emit_transposes(qv, 8,
                                lambda k0, kn, jt=jt, QTv=QTv: QTv[:, k0, 128 * jt:128 * (jt + 1)],
                                'QT', f'qfb{sl}',
                                scale_col=lambda m: svec[:, m:m + 1])

            for step in range(8 + 5):
                if step < 8:
                    q1A(step)
                if 0 <= step - 1 < 8:
                    q1B(step - 1)
                if 0 <= step - 2 < 8:
                    q1C(step - 2)
                if 0 <= step - 3 < 8:
                    q1D(step - 3)
                if 0 <= step - 5 < 8:
                    q1E(step - 5)
            if dbg:
                for j in range(4):
                    E('gpsimd', lambda e, j=j: e.dma_start(out=DBG['d_kt'][j], in_=KT[j][:]),
                      reads=[f'KT{j}'], dma=True, slot='dbg')
                E('gpsimd', lambda e: e.dma_start(out=DBG['d_v2'], in_=V2[:]), reads=['V2'], dma=True, slot='dbg')
                E('gpsimd', lambda e: e.dma_start(out=DBG['d_qt'], in_=QT[:]), reads=['QT'], dma=True, slot='dbg')

            pr.barrier()
            QTv = QT.rearrange("p (m c) -> p m c", m=NQT)
            OTv = OT.rearrange("p (m c) -> p m c", m=NQT)
            V2v = V2.rearrange("p (t c) -> p t c", t=NKT)
            E('gpsimd', lambda e: e.dma_start(out=wob.rearrange("p (k c) -> p k c", k=8),
                                              in_=IN['wo'].rearrange("k p c -> p k c")),
              dma=True, slot='wob', writes=['wob'])
            flat = [(j, h) for j in range(NQT) for h in range(H)]
            NH = len(flat)

            def st1(i):
                j, h = flat[i]
                w = WJ[j]
                kv = h // 4
                m = h // 2
                half = h % 2
                sA = i % 2
                sbase = sA * 1536
                sbanks = [f'bank{b}' for b in range(sA * 3, sA * 3 + (w + 511) // 512)]
                if h == 0:
                    bs = j % 2
                    E('sync', lambda e, j=j, bs=bs, w=w: e.dma_start(
                        out=bmv[bs][:, 0:w], in_=IN['bmask'][j][:, 0:w]),
                      dma=True, slot=f'bm{bs}', writes=[f'bmv{bs}'])
                segs = [(o, min(512, w - o)) for o in range(0, w, 512)]
                items = [(lambda e, o=o, ww=ww, m=m, half=half, kv=kv, j=j, sbase=sbase: e.matmul(
                    pv(sbase + o, ww), QTv[64 * half:64 * (half + 1), m, 128 * j:128 * (j + 1)],
                    KT[kv][64 * half:64 * (half + 1), o:o + ww],
                    start=True, stop=True)) for (o, ww) in segs]
                MMG(items, ['QT', f'KT{kv}'], sbanks)

            def st2(i):
                j, h = flat[i]
                w = WJ[j]
                sA = i % 2
                sbase = sA * 1536
                hp = i % 2
                bs = j % 2
                sbanks = [f'bank{b}' for b in range(sA * 3, sA * 3 + (w + 511) // 512)]
                E('vector', lambda e, sbase=sbase, hp=hp, bs=bs, w=w: e.tensor_tensor(
                    out=negS[hp][:, 0:w], in0=pv(sbase, w), in1=bmv[bs][:, 0:w], op=ALU.min),
                  reads=sbanks + [f'bmv{bs}'], writes=[f'negS{hp}'])
                E('vector', lambda e, hp=hp, w=w: e.tensor_scalar(
                    out=negS[hp][:, 0:w], in0=negS[hp][:, 0:w], scalar1=-1.0, scalar2=3e38,
                    op0=ALU.mult, op1=ALU.min, accum_out=stat[:, 2 + hp:3 + hp]),
                  reads=[f'negS{hp}'], writes=[f'negS{hp}', f'st_negmax{hp}'])
                E('scalar', lambda e, hp=hp, w=w: e.activation(
                    out=Pb[hp][:, 0:w], in_=negS[hp][:, 0:w], func=AF.Exp,
                    bias=stat[:, 2 + hp:3 + hp], scale=-1.0, accum_out=stat[:, 4 + hp:5 + hp]),
                  reads=[f'negS{hp}', f'st_negmax{hp}'], writes=[f'Pb{hp}', f'st_rowsum{hp}'])
            def st3a(i):
                j, h = flat[i]
                w = WJ[j]
                hp = i % 2
                pq = i % 3
                kts = kt_list(w)
                E('vector', lambda e, hp=hp: e.reciprocal(stat[:, 6 + hp:7 + hp],
                                                          stat[:, 4 + hp:5 + hp]),
                  reads=[f'st_rowsum{hp}'], writes=[f'st_recip{hp}'])
                E('vector', lambda e, hp=hp: e.tensor_scalar(
                    out=diagb[hp], in0=identb[:], scalar1=stat[:, 6 + hp:7 + hp], scalar2=None,
                    op0=ALU.mult),
                  reads=['identb', f'st_recip{hp}'], writes=[f'diag{hp}'])
                for k0 in range(0, len(kts), 4):
                    kn = min(4, len(kts) - k0)
                    items = [(lambda e, kt=k0 + ii, ii=ii, hp=hp, kts=kts: e.matmul(
                        pv(6 * 512 + 128 * ii, 128, 0, kts[kt]),
                        Pb[hp][:, 128 * kt:128 * kt + kts[kt]], diagb[hp],
                        start=True, stop=True)) for ii in range(kn)]
                    MMG(items, [f'Pb{hp}', f'diag{hp}'], ['bank6'])
                    E('scalar', lambda e, k0=k0, kn=kn, pq=pq: e.copy(
                        out=PTs[pq][:, 128 * k0:128 * (k0 + kn)], in_=pv(6 * 512, 128 * kn)),
                      reads=['bank6'], writes=[f'PTs{pq}'])
                if dbg and j == 0 and h == 0:
                    E('gpsimd', lambda e, hp=hp: e.dma_start(out=DBG['d_p'][:, 0:WJ[0]],
                                                             in_=Pb[hp][:, 0:WJ[0]]),
                      reads=[f'Pb{hp}'], dma=True, slot='dbg')

            def st3b(i):
                j, h = flat[i]
                w = WJ[j]
                kv = h // 4
                m = h // 2
                half = h % 2
                pq = i % 3
                kts = kt_list(w)
                items = [(lambda e, kt=kt, kv=kv, pq=pq, kts=kts: e.matmul(
                    pv(7 * 512, 128, 0, 64),
                    V2v[0:kts[kt], kt, 64 * kv:64 * (kv + 1)],
                    PTs[pq][0:kts[kt], 128 * kt:128 * (kt + 1)],
                    start=(kt == 0), stop=(kt == len(kts) - 1))) for kt in range(len(kts))]
                MMG(items, ['V2', f'PTs{pq}'], ['bank7'])
                E('scalar', lambda e, m=m, half=half, j=j: e.copy(
                    out=OTv[64 * half:64 * (half + 1), m, 128 * j:128 * (j + 1)],
                    in_=pv(7 * 512, 128, 0, 64)),
                  reads=['bank7'], writes=['OT'])

            for step in range(NH + 3):
                if step < NH:
                    st1(step)
                if 0 <= step - 1 < NH:
                    st2(step - 1)
                if 0 <= step - 2 < NH:
                    st3a(step - 2)
                if 0 <= step - 3 < NH:
                    st3b(step - 3)
            if dbg:
                E('gpsimd', lambda e: e.dma_start(out=DBG['d_ot'], in_=OT[:]), reads=['OT'], dma=True, slot='dbg')

            pr.barrier()
            wov = wob.rearrange("p (k c) -> p k c", k=8)
            xn2v = xn2.rearrange("p (t d) -> p t d", t=NQT)
            xn2lov = xn2lo.rearrange("p (t d) -> p t d", t=NQT)
            x1evs = []
            for tt in range(NQT):
                sl = tt % 2
                E('sync', lambda e, tt=tt, sl=sl: e.dma_start(
                    out=xqt[sl], in_=IN['xq'][128 * tt:128 * (tt + 1)]),
                  dma=True, slot=f'xq{sl}', writes=[f'xqt{sl}'])
                for seg in range(4):
                    pb = seg % 2
                    items = [(lambda e, k=k, seg=seg, tt=tt, pb=pb: e.matmul(
                        pv(512 * pb, 512), OTv[:, k, 128 * tt:128 * (tt + 1)],
                        wov[:, k, 512 * seg:512 * (seg + 1)],
                        start=(k == 0), stop=(k == 7))) for k in range(8)]
                    MMG(items, ['OT', 'wob'], [f'bank{pb}'])
                    E('vector', lambda e, seg=seg, sl=sl, pb=pb: e.scalar_tensor_tensor(
                        out=x1t[sl][:, 512 * seg:512 * (seg + 1)], in0=pv(512 * pb, 512),
                        scalar=1.0, in1=xqt[sl][:, 512 * seg:512 * (seg + 1)],
                        op0=ALU.mult, op1=ALU.add),
                      reads=[f'bank{pb}', f'xqt{sl}'], writes=[f'x1t{sl}'])
                ev = E('sync', lambda e, tt=tt, sl=sl: e.dma_start(
                    out=x1sp[128 * tt:128 * (tt + 1)], in_=x1t[sl]),
                  reads=[f'x1t{sl}'], dma=True, slot=f'x1s{sl}')
                x1evs.append(ev)
                emit_ln(x1t[sl], xn2f[sl], [f'x1t{sl}'], f'xn2f{sl}', par=sl)
                E('scalar', lambda e, tt=tt, sl=sl: e.copy(out=xn2v[:, tt, :], in_=xn2f[sl]),
                  reads=[f'xn2f{sl}'], writes=['xn2'])
                E('vector', lambda e, tt=tt, sl=sl: e.tensor_tensor(
                    out=xn2lov[:, tt, :], in0=xn2f[sl], in1=xn2v[:, tt, :], op=ALU.subtract),
                  reads=[f'xn2f{sl}', 'xn2'], writes=['xn2lo'])
            if dbg:
                E('sync', lambda e: e.dma_start(out=DBG['d_x1'], in_=x1sp), extra=x1evs, dma=True, slot='dbg')
                for tt in range(NQT):
                    E('gpsimd', lambda e, tt=tt: e.dma_start(
                        out=DBG['d_xn2'][128 * tt:128 * (tt + 1)], in_=xn2v[:, tt, :]),
                      reads=['xn2'], dma=True, slot='dbg')

            pr.barrier()
            xn2Tv = xn2T.rearrange("p (k c) -> p k c", k=KD)
            xn2sTv = xn2sT.rearrange("p (k c) -> p k c", k=KD)
            wghv = wgh.rearrange("p (k c) -> p k c", k=KD)
            wglv = wgl.rearrange("p (k c) -> p k c", k=KD)
            for tt in range(NQT):
                sl = tt % 2
                emit_transposes(xn2v[:, tt, :], KD,
                                lambda k0, kn, tt=tt: xn2Tv[:, k0:k0 + kn, 128 * tt:128 * (tt + 1)],
                                'xn2T', 'xn2')
                xloTv = xloT[sl].rearrange("p (k c) -> p k c", k=KD)
                emit_transposes(xn2lov[:, tt, :], KD,
                                lambda k0, kn, sl=sl: xloT[sl].rearrange(
                                    "p (k c) -> p k c", k=KD)[:, k0:k0 + kn, :].squeeze(),
                                f'xloT{sl}', 'xn2lo')
                gbank = f'bank{6 + (tt * 128) // 512 % 2}'
                items = []
                for (wv_, xv_) in [(wghv, 'T'), (wglv, 'T'), (wghv, 'L')]:
                    for k in range(KD):
                        def mk(e, k=k, wv_=wv_, xv_=xv_, tt=tt, sl=sl):
                            rhs = (xn2Tv[:, k, 128 * tt:128 * (tt + 1)] if xv_ == 'T'
                                   else xloT[sl].rearrange("p (k c) -> p k c", k=KD)[:, k, :])
                            return e.matmul(pv(3072 + 128 * tt, 128, 0, 8), wv_[:, k, :], rhs,
                                            start=(k == 0 and xv_ == 'T' and wv_ is wghv),
                                            stop=(k == KD - 1 and xv_ == 'L'))
                        items.append(mk)
                MMG(items, ['wgh', 'wgl', 'xn2T', f'xloT{sl}'], [gbank])
            E('scalar', lambda e: e.copy(out=logitsT[0:8, :], in_=pv(3072, 1024, 0, 8)),
              reads=['bank6', 'bank7'], writes=['logitsT'])
            for j in range(NQT):
                E('tensor', lambda e, j=j: e.matmul(
                    pv(3072, 8), logitsT[0:8, 128 * j:128 * (j + 1)], identb[0:8, 0:8],
                    start=True, stop=True),
                  reads=['logitsT', 'identb'], writes=['bank6'])
                E('vector', lambda e: e.reduce_max(stat[:, 68:69], pv(3072, 8), axis=mybir.AxisListType.X),
                  reads=['bank6'], writes=['st_m1'])
                E('vector', lambda e: e.tensor_scalar(out=stat[:, 69:77], in0=pv(3072, 8),
                                                      scalar1=stat[:, 68:69], scalar2=0.0,
                                                      op0=ALU.is_equal, op1=ALU.add,
                                                      accum_out=stat[:, 77:78]),
                  reads=['bank6', 'st_m1'], writes=['st_eq', 'st_cnt'])
                E('vector', lambda e: e.scalar_tensor_tensor(out=stat[:, 69:77], in0=stat[:, 69:77],
                                                             scalar=-3e38, in1=pv(3072, 8),
                                                             op0=ALU.mult, op1=ALU.add),
                  reads=['st_eq', 'bank6'], writes=['st_eq'])
                E('vector', lambda e: e.reduce_max(stat[:, 78:79], stat[:, 69:77],
                                                   axis=mybir.AxisListType.X),
                  reads=['st_eq'], writes=['st_m2p'])
                E('vector', lambda e: e.tensor_scalar(out=stat[:, 79:80], in0=stat[:, 77:78],
                                                      scalar1=1.5, scalar2=None, op0=ALU.is_ge),
                  reads=['st_cnt'], writes=['st_im'])
                E('vector', lambda e: e.tensor_tensor(out=stat[:, 80:81], in0=stat[:, 68:69],
                                                      in1=stat[:, 78:79], op=ALU.subtract),
                  reads=['st_m1', 'st_m2p'], writes=['st_d'])
                E('vector', lambda e: e.tensor_tensor(out=stat[:, 80:81], in0=stat[:, 79:80],
                                                      in1=stat[:, 80:81], op=ALU.mult),
                  reads=['st_im', 'st_d'], writes=['st_d'])
                E('vector', lambda e: e.tensor_tensor(out=stat[:, 80:81], in0=stat[:, 78:79],
                                                      in1=stat[:, 80:81], op=ALU.add),
                  reads=['st_m2p', 'st_d'], writes=['st_d'])
                E('vector', lambda e, j=j: e.tensor_tensor(out=stat[:, 82 + j:83 + j],
                                                           in0=stat[:, 68:69],
                                                           in1=stat[:, 80:81], op=ALU.add),
                  reads=['st_m1', 'st_d'], writes=[f'st_sc{j}'])
                E('vector', lambda e, j=j: e.tensor_scalar(out=sdiag[j], in0=identb[:],
                                                           scalar1=stat[:, 82 + j:83 + j],
                                                           scalar2=None, op0=ALU.mult),
                  reads=['identb', f'st_sc{j}'], writes=[f'sdiag{j}'])
            if dbg:
                E('sync', lambda e: e.dma_start(out=DBG['d_scale'], in_=stat[:, 82:90],
                                                allow_slow_non_contiguous=True),
                  reads=[f'st_sc{j}' for j in range(8)], dma=True, slot='dbg')
            pr.barrier()
            for tt in range(NQT):
                emit_transposes(xn2v[:, tt, :], KD,
                                lambda k0, kn, tt=tt: xn2sTv[:, k0:k0 + kn, 128 * tt:128 * (tt + 1)],
                                'xn2sT', 'xn2', rhs=sdiag[tt], rhsbuf=f'sdiag{tt}')

            pr.barrier()
            if dbg:
                E('gpsimd', lambda e: e.dma_start(out=DBG['d_xs'], in_=xn2sT[:]),
                  reads=['xn2sT'], dma=True, slot='dbg')
                for j in range(NQT):
                    E('gpsimd', lambda e, j=j: e.dma_start(
                        out=DBG['d_sd'][:, 128 * j:128 * (j + 1)], in_=sdiag[j]),
                      reads=[f'sdiag{j}'], dma=True, slot='dbg')
            ms_evs = []
            for dm in range(KD):
                sl = dm % 2
                E('gpsimd', lambda e, dm=dm, sl=sl: e.dma_start(
                    out=web[sl].rearrange("p (k c) -> p k c", k=KD),
                    in_=IN['we'][dm],
                    max_dma_last_dim=8192),
                  dma=True, slot=f'we{sl}', writes=[f'web{sl}'])
                if dbg:
                    E('gpsimd', lambda e, dm=dm, sl=sl: e.dma_start(
                        out=DBG['d_we'][dm], in_=web[sl][:]),
                      reads=[f'web{sl}'], dma=True, slot='dbg')
                for seg in range(2):
                    pb = 2 * sl + seg
                    items = [(lambda e, k=k, seg=seg, sl=sl, pb=pb: e.matmul(
                        pv(512 * pb, 512), web[sl].rearrange("p (k c) -> p k c", k=KD)[:, k, :],
                        xn2sTv[:, k, 512 * seg:512 * (seg + 1)],
                        start=(k == 0), stop=(k == KD - 1))) for k in range(KD)]
                    moe_last_mm = MMG(items, [f'web{sl}', 'xn2sT'], [f'bank{pb}'])
                E('scalar', lambda e, sl=sl: e.copy(out=moer_m[sl], in_=pv(1024 * sl, 1024)),
                  reads=[f'bank{2 * sl}', f'bank{2 * sl + 1}'], writes=[f'moerm{sl}'])
                if dbg:
                    E('sync', lambda e, dm=dm, sl=sl: e.dma_start(out=DBG['d_mr'][dm], in_=moer_m[sl]),
                      reads=[f'moerm{sl}'], dma=True, slot='dbg')
                msev = E('sync', lambda e, dm=dm, sl=sl: e.dma_start(out=moesp[dm], in_=moer_m[sl]),
                  reads=[f'moerm{sl}'], dma=True, slot=f'ms{sl}')
                ms_evs.append(msev)

            hv = hbuf.rearrange("p (m c) -> p m c", m=NM_UP)
            for m in range(NM_UP):
                sl = m % 2
                E('gpsimd', lambda e, m=m, sl=sl: e.dma_start(
                    out=wub[sl].rearrange("p (k c) -> p k c", k=KD),
                    in_=IN['wu'][m], max_dma_last_dim=8192),
                  extra=[moe_last_mm] if m < 2 else (), dma=True, slot=f'wu{sl}',
                  writes=[f'wub{sl}'])
                for seg in range(2):
                    pb = 2 * sl + seg
                    items = [(lambda e, k=k, seg=seg, sl=sl, pb=pb: e.matmul(
                        pv(512 * pb, 512), wub[sl].rearrange("p (k c) -> p k c", k=KD)[:, k, :],
                        xn2Tv[:, k, 512 * seg:512 * (seg + 1)],
                        start=(k == 0), stop=(k == KD - 1))) for k in range(KD)]
                    up_last_mm = MMG(items, [f'wub{sl}', 'xn2T'], [f'bank{pb}'])
                E('scalar', lambda e, m=m, sl=sl: e.activation(
                    out=hv[:, m, :], in_=pv(1024 * sl, 1024), func=AF.Relu),
                  reads=[f'bank{2 * sl}', f'bank{2 * sl + 1}'], writes=['hbuf'],
                  extra=[ms_evs[-1]] if m == 0 else ())
                E('vector', lambda e, m=m: e.tensor_tensor(out=hv[:, m, :], in0=hv[:, m, :],
                                                           in1=hv[:, m, :], op=ALU.mult),
                  reads=['hbuf'], writes=['hbuf'])
            if dbg:
                E('gpsimd', lambda e: e.dma_start(out=DBG['d_h'], in_=hv[:, 0, :]),
                  reads=['hbuf'], dma=True, slot='dbg')

            ybv = ybuf.rearrange("p (m c) -> p m c", m=KD)
            for dm in range(KD):
                sl = dm % 2
                E('sync', lambda e, dm=dm: e.dma_start(out=moer[0], in_=moesp[dm]),
                  extra=[ms_evs[dm]] + (ms_evs[-2:] if dm == 0 else []),
                  dma=True, slot='mr0', writes=['moer0'])
                for hk in range(2):
                    wsl = 2 * (dm % 2) + hk
                    E('gpsimd', lambda e, dm=dm, hk=hk, wsl=wsl: e.dma_start(
                        out=wdb[wsl].rearrange("p (k c) -> p k c", k=32),
                        in_=IN['wd'][dm, :, 32 * hk:32 * (hk + 1), :],
                        max_dma_last_dim=8192),
                      extra=[up_last_mm] if dm < 2 else (), dma=True, slot=f'wd{wsl}',
                      writes=[f'wdb{wsl}'])
                for seg in range(2):
                    pb = 2 * sl + seg
                    items = []
                    for kp in range(NM_UP):
                        wsl = 2 * (dm % 2) + kp // 32
                        items.append(lambda e, kp=kp, wsl=wsl, seg=seg, pb=pb: e.matmul(
                            pv(512 * pb, 512),
                            wdb[wsl].rearrange("p (k c) -> p k c", k=32)[:, kp % 32, :],
                            hv[0:128, kp, 512 * seg:512 * (seg + 1)],
                            start=(kp == 0), stop=(kp == NM_UP - 1)))
                    dn_last_mm = MMG(items, [f'wdb{2 * (dm % 2)}', f'wdb{2 * (dm % 2) + 1}', 'hbuf'],
                                     [f'bank{pb}'])
                for seg in range(2):
                    pb = 2 * sl + seg
                    E('vector', lambda e, dm=dm, seg=seg, pb=pb: e.scalar_tensor_tensor(
                        out=ybv[:, dm, 512 * seg:512 * (seg + 1)], in0=pv(512 * pb, 512),
                        scalar=1.0, in1=moer[0][:, 512 * seg:512 * (seg + 1)],
                        op0=ALU.mult, op1=ALU.add),
                      reads=[f'bank{pb}', 'moer0'], writes=['ybuf'])

            if dbg:
                E('gpsimd', lambda e: e.dma_start(out=DBG['d_yb'], in_=ybuf[:]),
                  reads=['ybuf'], dma=True, slot='dbg')
                E('sync', lambda e: e.dma_start(out=DBG['d_moe'], in_=moesp), extra=ms_evs, dma=True, slot='dbg')
            ytv = ytok.rearrange("p (t d) -> p t d", t=NQT)
            dn_dep = [dn_last_mm]
            for dm in range(KD):
                for t0 in range(0, NQT, 4):
                    pbank = 4 + (t0 // 4) % 2
                    items = [(lambda e, tt=t0 + i, i=i, dm=dm, pbank=pbank: e.matmul(
                        pv(pbank * 512 + 128 * i, 128), ybv[:, dm, 128 * tt:128 * (tt + 1)],
                        identb[:], start=True, stop=True)) for i in range(4)]
                    MMG(items, ['ybuf', 'identb'], [f'bank{pbank}'])
                    E('scalar', lambda e, t0=t0, dm=dm, pbank=pbank: e.copy(
                        out=ytv[:, t0:t0 + 4, 128 * dm:128 * (dm + 1)], in_=pv(pbank * 512, 512)),
                      reads=[f'bank{pbank}'], writes=['ytok'], extra=dn_dep)
                    dn_dep = ()
            if dbg:
                for tt in range(NQT):
                    E('sync', lambda e, tt=tt: e.dma_start(
                        out=DBG['d_y'][128 * tt:128 * (tt + 1)], in_=ytv[:, tt, :]),
                      reads=['ytok'], dma=True, slot='dbg')

            E('sync', lambda e: e.dma_start(out=gamv, in_=IN['gammab']), dma=True, slot='gam',
              writes=['gamv'], extra=[dn_last_mm])
            E('sync', lambda e: e.dma_start(out=betv, in_=IN['betab']), dma=True, slot='bet',
              writes=['betv'])
            outevs = []
            for tt in range(NQT):
                sl = tt % 2
                E('sync', lambda e, tt=tt, sl=sl: e.dma_start(
                    out=x1r[sl], in_=x1sp[128 * tt:128 * (tt + 1)]),
                  extra=list(x1evs) + [dn_last_mm], dma=True, slot=f'x1r{sl}',
                  writes=[f'x1r{sl}'])
                emit_ln_apply_only = True
                par = tt % 2
                c = LNC[par]
                bn, mvc, rs = c['bn'], c['mv'], c['rstd']
                for s in range(4):
                    E('vector', lambda e, s=s, tt=tt, bn=bn: e.bn_stats(
                        stat[:, bn + 6 * s:bn + 6 * s + 6], ytv[:, tt, 512 * s:512 * (s + 1)]),
                      reads=['ytok'], writes=[f'st_bn{par}'])
                E('vector', lambda e, bn=bn, mvc=mvc: e.bn_aggr(stat[:, mvc:mvc + 2],
                                                                stat[:, bn:bn + 24]),
                  reads=[f'st_bn{par}'], writes=[f'st_mv{par}'])
                E('scalar', lambda e, mvc=mvc, rs=rs: e.activation(
                    out=stat[:, rs:rs + 1], in_=stat[:, mvc + 1:mvc + 2],
                    func=AF.Ln, bias=stat[:, 0:1], scale=1.0),
                  reads=[f'st_mv{par}', 'st_eps'], writes=[f'st_r0{par}'])
                E('scalar', lambda e, rs=rs: e.activation(out=stat[:, rs:rs + 1],
                                                          in_=stat[:, rs:rs + 1],
                                                          func=AF.Exp, scale=-0.5),
                  reads=[f'st_r0{par}'], writes=[f'st_rstd{par}'])
                E('vector', lambda e, tt=tt, mvc=mvc, rs=rs: e.tensor_scalar(
                    out=ytv[:, tt, :], in0=ytv[:, tt, :],
                    scalar1=stat[:, mvc:mvc + 1], scalar2=stat[:, rs:rs + 1],
                    op0=ALU.subtract, op1=ALU.mult),
                  reads=['ytok', f'st_mv{par}', f'st_rstd{par}'], writes=['ytok'])
                E('gpsimd', lambda e, tt=tt: e.tensor_tensor(out=ytv[:, tt, :], in0=ytv[:, tt, :],
                                                             in1=gamv, op=ALU.mult),
                  reads=['ytok', 'gamv'], writes=['ytok'])
                E('vector', lambda e, tt=tt: e.tensor_tensor(out=ytv[:, tt, :], in0=ytv[:, tt, :],
                                                             in1=betv, op=ALU.add),
                  reads=['ytok', 'betv'], writes=['ytok'])
                E('vector', lambda e, tt=tt, sl=sl: e.tensor_tensor(
                    out=outt[sl], in0=ytv[:, tt, :], in1=x1r[sl], op=ALU.add),
                  reads=['ytok', f'x1r{sl}'], writes=[f'outt{sl}'])
                ev = E('sync', lambda e, tt=tt, sl=sl: e.dma_start(
                    out=out[128 * tt:128 * (tt + 1)], in_=outt[sl]),
                  reads=[f'outt{sl}'], dma=True, slot=f'out{tt % 2}')
                outevs.append(ev)
            pr.final_wait('sync', outevs)
            pr.emit(block)
    return nc



_CACHED = {}


def _get_nc(dbg=False):
    if dbg not in _CACHED:
        _CACHED[dbg] = build_program(dbg=dbg)
    return _CACHED[dbg]


def kernel(**inputs):
    W = {}
    for k, v in inputs.items():
        if k == 'L0_attn_x':
            x_full = np.asarray(v, np.float32)
        elif k.startswith('L0_attn_'):
            W[k[len('L0_attn_'):]] = np.asarray(v)
        else:
            W[k[len('L0_ffn_'):]] = np.asarray(v)
    shared = host_shared_weights(W)
    in_maps = []
    qposs = []
    for core in range(8):
        d, qpos = host_core_inputs(core, x_full, shared)
        in_maps.append(d)
        qposs.append(qpos)
    nc = _get_nc(dbg=False)
    res = run_bass_kernel_spmd(nc, in_maps, core_ids=list(range(8)))
    outp = np.zeros((B, T, D), np.float32)
    for core in range(8):
        outp[core // 4, qposs[core]] = np.asarray(res.results[core]["out"])
    return outp


# version 24 (final)
# speedup vs baseline: 1.0460x; 1.0070x over previous
import numpy as np
import ml_dtypes
from contextlib import ExitStack
import concourse.bass as bass
import concourse.mybir as mybir
from concourse.bass_utils import run_bass_kernel_spmd
from concourse.alu_op_type import AluOpType as ALU

F32 = mybir.dt.float32
BF16 = mybir.dt.bfloat16
U8 = mybir.dt.uint8
AF = mybir.ActivationFunctionType

B, T, D = 2, 4096, 2048
H, HKV, DDS, DDG, DV = 16, 4, 32, 32, 64
MBK, LW = 64, 1024
REMOTE = T - LW
NPOOL = REMOTE // MBK
NDUM = 16
KEYS = NPOOL + NDUM + LW
KEYSP = 1152
NKT = 9
KTSZ = [128] * 8 + [64]
KTOF = [128 * i for i in range(NKT)]
TQ = 1024
NQT = 8
KD = 16
CH = 256
NCH = T // CH
NM_UP = 64
ROPE_BASE = 10000.0
INTERLEAVE = True
WJ = [1088, 64, 64, 64, 64, 64, 576, 1088] if INTERLEAVE else [KEYS] * 8


def kt_list(w):
    out = [128] * (w // 128)
    if w % 128:
        out.append(w % 128)
    return out
EPS = 1e-5
LN32H = -0.5 * float(np.log(32.0))

ENGINES = ("sync", "gpsimd", "tensor", "vector", "scalar")


class Prog:

    def __init__(self, nc, slots=()):
        self.nc = nc
        self.records = {e: [] for e in ENGINES}
        self.counts = {e: 0 for e in ENGINES}
        self.events = {}
        self.sems = {}
        self.slots = list(slots)
        self.slot_evs = {sl: [] for sl in slots}
        self.highwater = {}
        self.elide = False
        self._n = 0

    def make_sems(self, stack):
        for e in ENGINES:
            self.sems[e] = stack.enter_context(self.nc.semaphore(f"sb_{e}"))
        for sl in self.slots:
            self.sems[sl] = stack.enter_context(self.nc.semaphore(f"dq_{sl}"))
            self.counts[sl] = 0

    def op(self, eng, fn, wait=(), ev=None, dma=False, slot=None):
        recs = self.records[eng]
        if isinstance(wait, str):
            wait = [wait]
        need = {}
        for w in wait:
            if w is None:
                continue
            we, wv = self.events[w]
            if need.get(we, -1) < wv:
                need[we] = wv
        hw = self.highwater.setdefault(eng, {})
        for we, wv in need.items():
            if self.elide and hw.get(we, -1) >= wv:
                continue
            if hw.get(we, -1) < wv:
                hw[we] = wv
            recs.append(("wait", we, wv))
        if ev is None:
            ev = f"_e{self._n}"
            self._n += 1
        if dma:
            assert slot is not None, "dma ops need a slot"
            self.counts[slot] += 16
            self.events[ev] = (slot, self.counts[slot])
            self.slot_evs[slot].append(ev)
            recs.append(("op", fn, (slot, 16)))
        else:
            self.counts[eng] += 1
            self.events[ev] = (eng, self.counts[eng])
            recs.append(("op", fn, (eng, 1)))
        return ev

    def finalize_slot(self, slot):
        total = self.counts[slot]
        for ev in self.slot_evs[slot]:
            self.events[ev] = (slot, total)

    def barrier(self):
        heads = [(nm, c) for nm, c in self.counts.items() if c > 0]
        for e in ENGINES:
            hw = self.highwater.setdefault(e, {})
            for we, wv in heads:
                if we != e and hw.get(we, -1) < wv:
                    hw[we] = wv
                    self.records[e].append(("wait", we, wv))

    def final_wait(self, eng, evs):
        need = {}
        for w in evs:
            we, wv = self.events[w]
            if need.get(we, -1) < wv:
                need[we] = wv
        for we, wv in need.items():
            self.records[eng].append(("wait", we, wv))

    def emit(self, block):
        for eng in ENGINES:
            if not self.records[eng]:
                continue

            def make(eng):
                def body(e):
                    for rec in self.records[eng]:
                        if rec[0] == "wait":
                            e.wait_ge(self.sems[rec[1]], rec[2])
                        else:
                            semname, inc = rec[2]
                            rec[1](e).then_inc(self.sems[semname], inc)
                return body

            getattr(block, eng)(make(eng))


class Buf:

    def __init__(self):
        self.writes = []
        self.readers = []

    def w(self):
        return self.readers + self.writes

    def wrote(self, ev):
        self.writes.append(ev)
        self.readers = []

    def r(self):
        return list(self.writes)

    def read(self, ev):
        self.readers.append(ev)



def rope_tables(Tlen):
    inv = 1.0 / ROPE_BASE ** (np.arange(0, DDG, 2, dtype=np.float64) / DDG)
    ang = np.arange(Tlen, dtype=np.float64)[:, None] * inv[None, :]
    return np.cos(ang).astype(np.float32), np.sin(ang).astype(np.float32)


def n_visible(q):
    npool = min(NPOOL, (q - 63) // 64 + 1) if q >= 63 else 0
    return npool, max(0, q - (REMOTE - 1))


def host_shared_weights(W):
    def tileKM(Wm, nk, nm):
        return np.ascontiguousarray(Wm.reshape(nk, 128, nm, 128).transpose(2, 1, 0, 3))

    wq_cat = np.concatenate(
        [np.concatenate([W['Wq_sem'][:, h * DDS:(h + 1) * DDS],
                         W['Wq_geo'][:, h * DDG:(h + 1) * DDG]], 1) for h in range(H)], 1)
    wk_cat = np.concatenate(
        [np.concatenate([W['Wk_sem'][:, j * DDS:(j + 1) * DDS],
                         W['Wk_geo'][:, j * DDG:(j + 1) * DDG]], 1) for j in range(HKV)], 1)
    cos_k, sin_k = rope_tables(T)
    spm_k = np.concatenate([-sin_k, sin_k], 1)
    ktc = np.ascontiguousarray(cos_k.reshape(32, 128, 16).transpose(1, 0, 2).reshape(128, 512))
    kts = np.ascontiguousarray(spm_k.reshape(32, 128, 32).transpose(1, 0, 2).reshape(128, 1024))
    ls = np.asarray(W['logit_scale'], np.float32)
    lsrep = np.ascontiguousarray(
        np.stack([np.repeat(ls[2 * i:2 * i + 2], 64) for i in range(8)], axis=1))
    poolm = np.zeros((128, 2), np.float32)
    poolm[0:64, 0] = 1.0 / MBK
    poolm[64:128, 1] = 1.0 / MBK
    return {
        "wq": np.ascontiguousarray(wq_cat.reshape(KD, 128, 1024)),
        "wk": np.ascontiguousarray(wk_cat.reshape(KD, 128, 256)),
        "wv": np.ascontiguousarray(np.asarray(W['Wv'], np.float32).reshape(KD, 128, 256)),
        "wo": np.ascontiguousarray(np.asarray(W['Wo'], np.float32).reshape(8, 128, 2048)),
        "wg": np.ascontiguousarray(np.asarray(W['gate_w'], np.float32).reshape(KD, 128, 8)),
        "wu": tileKM(np.asarray(W['up_w'], np.float32), KD, NM_UP),
        "wd": tileKM(np.asarray(W['down_w'], np.float32), NM_UP, KD),
        "we": tileKM(np.asarray(W['expert_weight'], np.float32), KD, KD),
        "ident": np.eye(128, dtype=np.float32),
        "lsrep": lsrep,
        "poolm": poolm,
        "ktc": ktc, "kts": kts,
        "gammab": np.ascontiguousarray(
            np.broadcast_to(np.asarray(W['ln_gamma'], np.float32)[None, :], (128, D))),
        "betab": np.ascontiguousarray(
            np.broadcast_to(np.asarray(W['ln_beta'], np.float32)[None, :], (128, D))),
    }


def host_core_inputs(core, x_full, shared):
    b = core // 4
    cc = core % 4
    if INTERLEAVE:
        qpos = np.concatenate([np.arange((cc + 4 * j) * 128, (cc + 4 * j) * 128 + 128)
                               for j in range(NQT)])
    else:
        qpos = np.arange(cc * TQ, cc * TQ + TQ)
    cos_k, sin_k = rope_tables(T)
    spm_k = np.concatenate([-sin_k, sin_k], 1)
    qtc = np.ascontiguousarray(cos_k[qpos].reshape(8, 128, 16).transpose(1, 0, 2).reshape(128, 128))
    qts = np.ascontiguousarray(spm_k[qpos].reshape(8, 128, 32).transpose(1, 0, 2).reshape(128, 256))
    bm = np.full((TQ, KEYS), -1e9, np.float32)
    bm[:, NPOOL:NPOOL + NDUM] = -2e9
    for i, q in enumerate(qpos):
        npool, nloc = n_visible(q)
        bm[i, :npool] = 1e9
        bm[i, NPOOL + NDUM:NPOOL + NDUM + nloc] = 1e9
    d = dict(shared)
    d["xb"] = np.ascontiguousarray(x_full[b])
    d["xq"] = np.ascontiguousarray(x_full[b][qpos])
    d["qtc"] = qtc
    d["qts"] = qts
    d["bmask"] = np.ascontiguousarray(bm.reshape(NQT, 128, KEYS))
    return d, qpos



def build_program(dbg=False):
    nc = bass.Bass("TRN2", target_bir_lowering=False)
    IN = {}
    for name, shape in [
        ("xb", [T, D]), ("xq", [TQ, D]),
        ("wq", [KD, 128, 1024]), ("wk", [KD, 128, 256]), ("wv", [KD, 128, 256]),
        ("wo", [8, 128, 2048]), ("wg", [KD, 128, 8]),
        ("wu", [NM_UP, 128, KD, 128]), ("wd", [KD, 128, NM_UP, 128]),
        ("we", [KD, 128, KD, 128]),
        ("ident", [128, 128]), ("lsrep", [128, 8]), ("poolm", [128, 2]),
        ("ktc", [128, 512]), ("kts", [128, 1024]), ("qtc", [128, 128]), ("qts", [128, 256]),
        ("bmask", [NQT, 128, KEYS]), ("gammab", [128, D]), ("betab", [128, D]),
    ]:
        IN[name] = nc.dram_tensor(name, shape, F32, kind="ExternalInput").ap()
    out = nc.dram_tensor("out", [TQ, D], F32, kind="ExternalOutput").ap()
    x1sp = nc.dram_tensor("x1sp", [TQ, D], F32, kind="Internal").ap()
    moesp = nc.dram_tensor("moesp", [KD, 128, TQ], F32, kind="Internal").ap()
    DBG = {}
    if dbg:
        for name, shape in [("d_kt", [4, 128, KEYSP]), ("d_v2", [128, NKT * 256]),
                            ("d_qt", [128, NQT * 1024]), ("d_x1", [TQ, D]),
                            ("d_ot", [128, 8 * 1024]), ("d_xn2", [TQ, D]),
                            ("d_scale", [128, NQT]), ("d_y", [TQ, D]),
                            ("d_p", [128, KEYS]), ("d_h", [128, 1024]),
                            ("d_yb", [128, KD * 1024]), ("d_moe", [KD, 128, TQ]),
                            ("d_xs", [128, KD * 1024]), ("d_sd", [128, 8 * 128]),
                            ("d_we", [KD, 128, KD * 128]), ("d_mr", [KD, 128, TQ])]:
            DBG[name] = nc.dram_tensor(name, shape, F32, kind="ExternalOutput").ap()

    SLOTS = [*[f'ini{i}' for i in range(1, 12)], 'dbg', 'xch0', 'xch1', 'wob', 'bm0', 'bm1', 'xq0', 'xq1',
             'x1s0', 'x1s1', 'we0', 'we1', 'ms0', 'ms1', 'wu0', 'wu1',
             'mr0', 'mr1', 'wd0', 'wd1', 'wd2', 'wd3', 'gam', 'bet', 'x1r0', 'x1r1',
             'out0', 'out1', 'xch2']
    pr = Prog(nc, slots=SLOTS)
    stack = ExitStack()
    bufs = {}

    def gb(name):
        if name not in bufs:
            bufs[name] = Buf()
        return bufs[name]

    with stack:
        tabs = stack.enter_context(nc.sbuf_tensor("tabs", [128, 1920], F32))
        ktc_v, kts_v = tabs[:, 0:512], tabs[:, 512:1536]
        qtc_v, qts_v = tabs[:, 1536:1664], tabs[:, 1664:1920]
        identb = stack.enter_context(nc.sbuf_tensor("identb", [128, 128], BF16))
        poolb = stack.enter_context(nc.sbuf_tensor("poolb", [128, 2], BF16))
        wgf = stack.enter_context(nc.sbuf_tensor("wgf", [128, KD * 8], F32))
        wgh = stack.enter_context(nc.sbuf_tensor("wgh", [128, KD * 8], BF16))
        wgl = stack.enter_context(nc.sbuf_tensor("wgl", [128, KD * 8], BF16))
        svec = stack.enter_context(nc.sbuf_tensor("svec", [128, 8], F32))
        lsb = stack.enter_context(nc.sbuf_tensor("lsb", [128, 8], F32))
        stat = stack.enter_context(nc.sbuf_tensor("stat", [128, 96], F32))
        LNC = [dict(bn=8, mv=32, rstd=34), dict(bn=40, mv=64, rstd=66)]
        ARN = 203392
        arena = stack.enter_context(nc.sbuf_tensor("arena", [128, ARN], U8))

        def av(off, n, dt):
            sz = 4 if dt == F32 else 2
            assert off % 4 == 0 and off + n * sz <= ARN, (off, n, sz)
            return arena[:, off:off + n * sz].bitcast(dt)

        K = 1024
        KT = [av(114 * K + j * 2 * KEYSP, KEYSP, BF16) for j in range(4)]
        V2 = av(114 * K + 9216, NKT * 256, BF16)
        QT = av(114 * K + 9216 + 4608, NQT * 1024, BF16)
        xch = [av(0 + i * 16 * K, 2 * 2048, F32) for i in range(2)] + [av(166 * K, 2 * 2048, F32)]
        xnb = [av(32 * K + i * 8 * K, 2 * 2048, BF16) for i in range(2)] + [av(182 * K, 2 * 2048, BF16)]
        xnT = [av(48 * K + i * 8 * K, KD * CH, BF16) for i in range(2)] + [av(190 * K, KD * CH, BF16)]
        wqb = av(64 * K, KD * 1024, BF16)
        wkb = av(96 * K, KD * 256, BF16)
        wvb = av(104 * K, KD * 256, BF16)
        kbf = [av(144 * K + i * K, 2 * 256, BF16) for i in range(2)] + [av(112 * K, 2 * 256, BF16)]
        kswp = [av(146 * K + i * 512, 2 * 128, BF16) for i in range(2)] + [av(113 * K, 2 * 128, BF16)]
        xnpT = av(148 * K, KD * 48, BF16)
        qfb = [av(150 * K + i * 4 * K, 2 * 1024, BF16) for i in range(2)]
        qswp = [av(158 * K + i * 2 * K, 2 * 512, BF16) for i in range(2)]
        bmv = [av(0 + i * 5 * K, KEYS, F32) for i in range(2)]
        negS = [av(10 * K + i * 5 * K, KEYS, F32) for i in range(2)]
        Pb = [av(20 * K + i * 3 * K, KEYSP, BF16) for i in range(2)]
        PTs = [av(26 * K + i * 3 * K, KEYSP, BF16) for i in range(2)] + [av(98 * K, KEYSP, BF16)]
        diagb = [av(32 * K + i * 512, 128, BF16) for i in range(2)]
        OT = av(34 * K, 8 * 1024, BF16)
        wob = av(50 * K, 8 * 2048, BF16)
        xqt = [av(82 * K + i * 8 * K, 2048, F32) for i in range(2)]
        x1t = [av(98 * K + i * 8 * K, 2048, F32) for i in range(2)]
        xn2 = av(144 * K, NQT * 2048, BF16)
        xn2f = [av(116 * K + i * 8 * K, 2048, F32) for i in range(2)]
        xn2lo = av(0, NQT * 2048, BF16)
        xn2T = av(32 * K, KD * 1024, BF16)
        xn2sT = av(0, KD * 1024, BF16)
        xloT = [av(64 * K + i * 4 * K, KD * 128, BF16) for i in range(2)]
        wdb = [av(32 * K + i * 8 * K, 32 * 128, BF16) for i in range(4)]
        logitsT = av(82 * K, 1024, BF16)
        sdiag = [av(84 * K + j * 512, 128, BF16) for j in range(NQT)]
        hbuf = av(ARN - 131072, NM_UP * 1024, BF16)
        web = [av(90 * K + i * 4 * K, KD * 128, BF16) for i in range(2)]
        wub = [av(0 + i * 4 * K, KD * 128, BF16) for i in range(2)]
        moer_m = [av(64 * K + i * 4 * K, 1024, F32) for i in range(2)]
        moer = [av(64 * K, 1024, F32) for i in range(2)]
        assert 64 * K + 4096 <= ARN - 131072
        ybuf = av(0, KD * 1024, BF16)
        ytok = av(72 * K, NQT * 2048, F32)
        gamv = av(136 * K, 2048, F32)
        betv = av(144 * K, 2048, F32)
        x1r = [av(152 * K + i * 8 * K, 2048, F32) for i in range(2)]
        outt = [av(168 * K + i * 8 * K, 2048, F32) for i in range(2)]

        PSA = stack.enter_context(nc.psum_tensor("PSA", [128, 4096], F32))

        def pv(c0, n, p0=0, p1=128):
            return PSA[p0:p1, c0:c0 + n]

        def pbanks(c0, n):
            return [gb(f"bank{i}") for i in range(c0 // 512, (c0 + n + 511) // 512)]

        pr.make_sems(stack)

        def E(eng, fn, reads=(), writes=(), extra=(), dma=False, slot=None):
            wait = list(extra)
            for bnames in reads:
                wait += gb(bnames).r() if isinstance(bnames, str) else bnames.r()
            for bnames in writes:
                wait += gb(bnames).w() if isinstance(bnames, str) else bnames.w()
            ev = pr.op(eng, fn, wait=wait, dma=dma, slot=slot)
            for bnames in reads:
                (gb(bnames) if isinstance(bnames, str) else bnames).read(ev)
            for bnames in writes:
                (gb(bnames) if isinstance(bnames, str) else bnames).wrote(ev)
            return ev

        def MMG(items, reads, writes):
            wait = []
            for b in reads:
                wait += gb(b).r() if isinstance(b, str) else b.r()
            for b in writes:
                wait += gb(b).w() if isinstance(b, str) else b.w()
            for i, fn in enumerate(items[:-1]):
                pr.op('tensor', fn, wait=wait if i == 0 else ())
            ev = pr.op('tensor', items[-1], wait=wait if len(items) == 1 else ())
            for b in reads:
                (gb(b) if isinstance(b, str) else b).read(ev)
            for b in writes:
                (gb(b) if isinstance(b, str) else b).wrote(ev)
            return ev

        with nc.Block() as block:
            E('vector', lambda e: e.memset(stat[:, 0:1], EPS), writes=['st_eps'])
            E('vector', lambda e: e.memset(stat[:, 1:2], LN32H), writes=['st_ln32'])
            for j in range(4):
                E('vector', lambda e, j=j: e.memset(KT[j][:, :], 0.0), writes=[f'KT{j}'])
            E('vector', lambda e: e.memset(V2[:, :], 0.0), writes=['V2'])
            E('gpsimd', lambda e: e.dma_start(out=identb[:], in_=IN['ident']), dma=True, slot='ini1', writes=['identb'])
            E('gpsimd', lambda e: e.dma_start(out=poolb[:], in_=IN['poolm']), dma=True, slot='ini2', writes=['poolb'])
            E('sync', lambda e: e.dma_start(out=wgf[:], in_=IN['wg'].rearrange("k p c -> p k c")),
              dma=True, slot='ini3', writes=['wgf'])
            E('scalar', lambda e: e.copy(out=wgh[:], in_=wgf[:]), reads=['wgf'], writes=['wgh'])
            E('vector', lambda e: e.tensor_tensor(out=wgl[:], in0=wgf[:], in1=wgh[:],
                                                  op=ALU.subtract),
              reads=['wgf', 'wgh'], writes=['wgl'])
            E('gpsimd', lambda e: e.dma_start(out=wqb.rearrange("p (k c) -> p k c", k=KD),
                                              in_=IN['wq'].rearrange("k p c -> p k c")),
              dma=True, slot='ini4', writes=['wqb'])
            E('gpsimd', lambda e: e.dma_start(out=wkb.rearrange("p (k c) -> p k c", k=KD),
                                              in_=IN['wk'].rearrange("k p c -> p k c")),
              dma=True, slot='ini5', writes=['wkb'])
            E('gpsimd', lambda e: e.dma_start(out=wvb.rearrange("p (k c) -> p k c", k=KD),
                                              in_=IN['wv'].rearrange("k p c -> p k c")),
              dma=True, slot='ini6', writes=['wvb'])
            E('sync', lambda e: e.dma_start(out=ktc_v, in_=IN['ktc']), dma=True, slot='ini7', writes=['ktc'])
            E('sync', lambda e: e.dma_start(out=kts_v, in_=IN['kts']), dma=True, slot='ini8', writes=['kts'])
            E('sync', lambda e: e.dma_start(out=qtc_v, in_=IN['qtc']), dma=True, slot='ini9', writes=['qtc'])
            E('sync', lambda e: e.dma_start(out=qts_v, in_=IN['qts']), dma=True, slot='ini10', writes=['qts'])
            E('sync', lambda e: e.dma_start(out=lsb[:], in_=IN['lsrep']), dma=True, slot='ini11', writes=['lsb'])
            E('scalar', lambda e: e.activation(out=svec[:], in_=lsb[:], func=AF.Exp,
                                               bias=stat[:, 1:2], scale=1.0),
              reads=['lsb', 'st_ln32'], writes=['svec'])

            def emit_ln(xf32_tile, dst_bf, rdbufs, wrbuf, par=0):
                c = LNC[par]
                bn, mvc, rs = c['bn'], c['mv'], c['rstd']
                for s in range(4):
                    E('vector', lambda e, s=s: e.bn_stats(stat[:, bn + 6 * s:bn + 6 * s + 6],
                                                          xf32_tile[:, 512 * s:512 * (s + 1)]),
                      reads=rdbufs, writes=[f'st_bn{par}'])
                E('vector', lambda e: e.bn_aggr(stat[:, mvc:mvc + 2], stat[:, bn:bn + 24]),
                  reads=[f'st_bn{par}'], writes=[f'st_mv{par}'])
                E('scalar', lambda e: e.activation(out=stat[:, rs:rs + 1], in_=stat[:, mvc + 1:mvc + 2],
                                                   func=AF.Ln, bias=stat[:, 0:1], scale=1.0),
                  reads=[f'st_mv{par}', 'st_eps'], writes=[f'st_r0{par}'])
                E('scalar', lambda e: e.activation(out=stat[:, rs:rs + 1], in_=stat[:, rs:rs + 1],
                                                   func=AF.Exp, scale=-0.5),
                  reads=[f'st_r0{par}'], writes=[f'st_rstd{par}'])
                E('vector', lambda e: e.tensor_scalar(out=dst_bf, in0=xf32_tile,
                                                      scalar1=stat[:, mvc:mvc + 1],
                                                      scalar2=stat[:, rs:rs + 1],
                                                      op0=ALU.subtract, op1=ALU.mult),
                  reads=rdbufs + [f'st_mv{par}', f'st_rstd{par}'], writes=[wrbuf])

            def emit_transposes(src_bf16_tile, nk, dst_fn, wrbuf, rdbuf, scale_col=None,
                                rhs=None, rhsbuf='identb'):
                rhs_ap = identb[:] if rhs is None else rhs
                for k0 in range(0, nk, 4):
                    kn = min(4, nk - k0)
                    pbank = 4 + (k0 // 4) % 2
                    items = [
                        (lambda e, k=k0 + i, pbank=pbank, i=i: e.matmul(
                            pv(pbank * 512 + i * 128, 128),
                            src_bf16_tile[:, 128 * k:128 * (k + 1)], rhs_ap,
                            start=True, stop=True))
                        for i in range(kn)]
                    mev = MMG(items, [rdbuf, rhsbuf], [f'bank{pbank}'])
                    if scale_col is None:
                        E('scalar', lambda e, k0=k0, kn=kn, pbank=pbank: e.copy(
                            out=dst_fn(k0, kn), in_=pv(pbank * 512, kn * 128)),
                          reads=[f'bank{pbank}'], writes=[wrbuf])
                    else:
                        for i in range(kn):
                            E('scalar', lambda e, k=k0 + i, i=i, pbank=pbank: e.activation(
                                out=dst_fn(k, 1), in_=pv(pbank * 512 + i * 128, 128),
                                func=AF.Copy, scale=scale_col(k)),
                              reads=[f'bank{pbank}', 'svec'], writes=[wrbuf])

            def emit_rope(eng, tile_bf, nh, swp, cos_ap, spm_ap, rdbuf, swpbuf):
                geo = tile_bf.rearrange("p (h x) -> p h x", h=nh)[:, :, 32:64] \
                    .rearrange("p h (t d) -> p h t d", t=2)
                sw = swp.rearrange("p (h x) -> p h x", h=nh).rearrange("p h (t d) -> p h t d", t=2)
                cosb = cos_ap.unsqueeze(1).unsqueeze(2).broadcast_to([128, nh, 2, 16])
                spmb = spm_ap.rearrange("p (t d) -> p t d", t=2).unsqueeze(1) \
                    .broadcast_to([128, nh, 2, 16])
                E(eng, lambda e: e.tensor_copy(sw[:, :, 0, :], geo[:, :, 1, :]),
                  reads=[rdbuf], writes=[swpbuf])
                E(eng, lambda e: e.tensor_copy(sw[:, :, 1, :], geo[:, :, 0, :]),
                  reads=[rdbuf], writes=[swpbuf])
                E(eng, lambda e: e.tensor_tensor(out=geo, in0=geo, in1=cosb, op=ALU.mult),
                  reads=[rdbuf, 'ktc', 'qtc'], writes=[rdbuf])
                E(eng, lambda e: e.tensor_tensor(out=sw, in0=sw, in1=spmb, op=ALU.mult),
                  reads=[swpbuf, 'kts', 'qts'], writes=[swpbuf])
                E(eng, lambda e: e.tensor_tensor(out=geo, in0=geo, in1=sw, op=ALU.add),
                  reads=[rdbuf, swpbuf], writes=[rdbuf])

            def p1A(i):
                c, ti = i // 2, i % 2
                sl = c % 3
                if ti == 0:
                    E('sync', lambda e, c=c, sl=sl: e.dma_start(
                        out=xch[sl].rearrange("p (t d) -> p t d", t=2),
                        in_=IN['xb'][CH * c:CH * (c + 1)].rearrange("(t p) d -> p t d", p=128)),
                      dma=True, slot=f'xch{sl}', writes=[f'xch{sl}'])

            def p1B(i):
                c, ti = i // 2, i % 2
                sl = c % 3
                par = i % 2
                cc = LNC[par]
                bn, mvc, rs = cc['bn'], cc['mv'], cc['rstd']
                xt = xch[sl].rearrange("p (t d) -> p t d", t=2)[:, ti, :]
                for ss in range(4):
                    E('vector', lambda e, ss=ss, xt=xt, bn=bn: e.bn_stats(
                        stat[:, bn + 6 * ss:bn + 6 * ss + 6], xt[:, 512 * ss:512 * (ss + 1)]),
                      reads=[f'xch{sl}'], writes=[f'st_bn{par}'])
                E('vector', lambda e, bn=bn, mvc=mvc: e.bn_aggr(stat[:, mvc:mvc + 2],
                                                                stat[:, bn:bn + 24]),
                  reads=[f'st_bn{par}'], writes=[f'st_mv{par}'])
                E('scalar', lambda e, mvc=mvc, rs=rs: e.activation(
                    out=stat[:, rs:rs + 1], in_=stat[:, mvc + 1:mvc + 2],
                    func=AF.Ln, bias=stat[:, 0:1], scale=1.0),
                  reads=[f'st_mv{par}', 'st_eps'], writes=[f'st_r0{par}'])
                E('scalar', lambda e, rs=rs: e.activation(out=stat[:, rs:rs + 1],
                                                          in_=stat[:, rs:rs + 1],
                                                          func=AF.Exp, scale=-0.5),
                  reads=[f'st_r0{par}'], writes=[f'st_rstd{par}'])

            def p1C(i):
                c, ti = i // 2, i % 2
                sl = c % 3
                par = i % 2
                cc = LNC[par]
                mvc, rs = cc['mv'], cc['rstd']
                xt = xch[sl].rearrange("p (t d) -> p t d", t=2)[:, ti, :]
                xnv = xnb[sl].rearrange("p (t d) -> p t d", t=2)[:, ti, :]
                E('vector', lambda e, xt=xt, xnv=xnv, mvc=mvc, rs=rs: e.tensor_scalar(
                    out=xnv, in0=xt, scalar1=stat[:, mvc:mvc + 1], scalar2=stat[:, rs:rs + 1],
                    op0=ALU.subtract, op1=ALU.mult),
                  reads=[f'xch{sl}', f'st_mv{par}', f'st_rstd{par}'], writes=[f'xnb{sl}'])
                xnTv = xnT[sl].rearrange("p (k c) -> p k c", k=KD)
                emit_transposes(xnv, KD,
                                lambda k0, kn, ti=ti, xnTv=xnTv: xnTv[:, k0:k0 + kn,
                                                                      128 * ti:128 * (ti + 1)],
                                f'xnT{sl}', f'xnb{sl}')

            def p1D(i):
                c, ti = i // 2, i % 2
                sl = c % 3
                g = i
                xnv = xnb[sl].rearrange("p (t d) -> p t d", t=2)[:, ti, :]
                xnTv = xnT[sl].rearrange("p (k c) -> p k c", k=KD)
                items = [(lambda e, k=k, ti=ti, xnTv=xnTv: e.matmul(
                    pv(2 * 512, 256), xnTv[:, k, 128 * ti:128 * (ti + 1)],
                    wkb.rearrange("p (k c) -> p k c", k=KD)[:, k, :],
                    start=(k == 0), stop=(k == KD - 1))) for k in range(KD)]
                MMG(items, [f'xnT{sl}', 'wkb'], ['bank2'])
                kbv = kbf[sl].rearrange("p (t c) -> p t c", t=2)[:, ti, :]
                E('scalar', lambda e, kbv=kbv: e.copy(out=kbv, in_=pv(2 * 512, 256)),
                  reads=['bank2'], writes=[f'kbf{sl}'])
                emit_rope('gpsimd', kbv, HKV,
                          kswp[sl].rearrange("p (t c) -> p t c", t=2)[:, ti, :],
                          ktc_v[:, 16 * g:16 * (g + 1)], kts_v[:, 32 * g:32 * (g + 1)],
                          f'kbf{sl}', f'kswp{sl}')

            def p1E(i):
                c, ti = i // 2, i % 2
                sl = c % 3
                g = i
                xnv = xnb[sl].rearrange("p (t d) -> p t d", t=2)[:, ti, :]
                xnTv = xnT[sl].rearrange("p (k c) -> p k c", k=KD)
                kbv = kbf[sl].rearrange("p (t c) -> p t c", t=2)[:, ti, :]
                if c < 12:
                    for j2 in range(2):
                        E('tensor', lambda e, kbv=kbv, j2=j2, g=g: e.matmul(
                            pv(3072 + j2 * 128 + 2 * g, 2), kbv[:, 128 * j2:128 * (j2 + 1)],
                            poolb[:], start=True, stop=True),
                          reads=[f'kbf{sl}', 'poolb'], writes=['bank6'])
                    for dm in range(KD):
                        E('tensor', lambda e, xnv=xnv, dm=dm, ti=ti: e.matmul(
                            pv(3584 + dm * 4 + ti * 2, 2), xnv[:, 128 * dm:128 * (dm + 1)],
                            poolb[:], start=True, stop=True),
                          reads=[f'xnb{sl}', 'poolb'], writes=['bank7'])
                    if ti == 1:
                        xnpTv = xnpT.rearrange("p (k c) -> p k c", k=KD)
                        E('scalar', lambda e, c=c, xnpTv=xnpTv: e.copy(
                            out=xnpTv[:, :, 4 * c:4 * (c + 1)],
                            in_=pv(3584, 64).rearrange("p (k c) -> p k c", k=KD)),
                          reads=['bank7'], writes=['xnpT'])
                else:
                    m = 2 * (c - 12) + ti
                    items = [(lambda e, k=k, ti=ti, xnTv=xnTv: e.matmul(
                        pv(3 * 512, 256), xnTv[:, k, 128 * ti:128 * (ti + 1)],
                        wvb.rearrange("p (k c) -> p k c", k=KD)[:, k, :],
                        start=(k == 0), stop=(k == KD - 1))) for k in range(KD)]
                    MMG(items, [f'xnT{sl}', 'wvb'], ['bank3'])
                    V2v = V2.rearrange("p (t c) -> p t c", t=NKT)
                    kta = (64 + 128 * m) // 128
                    E('scalar', lambda e, kta=kta, V2v=V2v: e.copy(out=V2v[64:128, kta, :],
                                                                   in_=pv(3 * 512, 256, 0, 64)),
                      reads=['bank3'], writes=['V2'])
                    E('scalar', lambda e, kta=kta, V2v=V2v: e.copy(out=V2v[0:64, kta + 1, :],
                                                                   in_=pv(3 * 512, 256, 64, 128)),
                      reads=['bank3'], writes=['V2'])
                    for j2 in range(2):
                        E('tensor', lambda e, kbv=kbv, j2=j2: e.matmul(
                            pv(0, 128), kbv[:, 128 * j2:128 * (j2 + 1)], identb[:],
                            start=True, stop=True),
                          reads=[f'kbf{sl}', 'identb'], writes=['bank0'])
                        co = 64 + 128 * m
                        for jj, (p0, p1) in enumerate([(0, 64), (64, 128)]):
                            jk = 2 * j2 + jj
                            E('scalar', lambda e, jk=jk, co=co, p0=p0, p1=p1: e.copy(
                                out=KT[jk][0:64, co:co + 128], in_=pv(0, 128, p0, p1)),
                              reads=['bank0'], writes=[f'KT{jk}'])
                            E('scalar', lambda e, jk=jk, co=co, p0=p0, p1=p1: e.copy(
                                out=KT[jk][64:128, co:co + 128], in_=pv(0, 128, p0, p1)),
                              reads=['bank0'], writes=[f'KT{jk}'])

            NT1 = 2 * NCH
            NT1 = 2 * NCH
            for step in range(NT1 + 5):
                if step < NT1:
                    p1A(step)
                if 0 <= step - 1 < NT1:
                    p1B(step - 1)
                if 0 <= step - 2 < NT1:
                    p1C(step - 2)
                if 0 <= step - 3 < NT1:
                    p1D(step - 3)
                if 0 <= step - 5 < NT1:
                    p1E(step - 5)
            for j2 in range(2):
                for jj, (p0, p1) in enumerate([(0, 64), (64, 128)]):
                    jk = 2 * j2 + jj
                    E('scalar', lambda e, j2=j2, p0=p0, p1=p1, jk=jk: e.copy(
                        out=KT[jk][0:64, 0:48], in_=pv(3072 + j2 * 128, 48, p0, p1)),
                      reads=['bank6'], writes=[f'KT{jk}'])
                    E('scalar', lambda e, j2=j2, p0=p0, p1=p1, jk=jk: e.copy(
                        out=KT[jk][64:128, 0:48], in_=pv(3072 + j2 * 128, 48, p0, p1)),
                      reads=['bank6'], writes=[f'KT{jk}'])
            xnpTv = xnpT.rearrange("p (k c) -> p k c", k=KD)
            items = [(lambda e, k=k: e.matmul(
                pv(3 * 512, 256, 0, 48), xnpTv[:, k, :],
                wvb.rearrange("p (k c) -> p k c", k=KD)[:, k, :],
                start=(k == 0), stop=(k == KD - 1))) for k in range(KD)]
            MMG(items, ['xnpT', 'wvb'], ['bank3'])
            E('scalar', lambda e: e.copy(out=V2.rearrange("p (t c) -> p t c", t=NKT)[0:48, 0, :],
                                         in_=pv(3 * 512, 256, 0, 48)),
              reads=['bank3'], writes=['V2'])

            def q1A(i):
                qc, ti = i // 2, i % 2
                sl = qc % 2
                if ti == 0:
                    E('sync', lambda e, qc=qc, sl=sl: e.dma_start(
                        out=xch[sl].rearrange("p (t d) -> p t d", t=2),
                        in_=IN['xq'][CH * qc:CH * (qc + 1)].rearrange("(t p) d -> p t d", p=128)),
                      dma=True, slot=f'xch{sl}', writes=[f'xch{sl}'])

            def q1B(i):
                qc, ti = i // 2, i % 2
                sl = qc % 2
                par = i % 2
                cc = LNC[par]
                bn, mvc, rs = cc['bn'], cc['mv'], cc['rstd']
                xt = xch[sl].rearrange("p (t d) -> p t d", t=2)[:, ti, :]
                for ss in range(4):
                    E('vector', lambda e, ss=ss, xt=xt, bn=bn: e.bn_stats(
                        stat[:, bn + 6 * ss:bn + 6 * ss + 6], xt[:, 512 * ss:512 * (ss + 1)]),
                      reads=[f'xch{sl}'], writes=[f'st_bn{par}'])
                E('vector', lambda e, bn=bn, mvc=mvc: e.bn_aggr(stat[:, mvc:mvc + 2],
                                                                stat[:, bn:bn + 24]),
                  reads=[f'st_bn{par}'], writes=[f'st_mv{par}'])
                E('scalar', lambda e, mvc=mvc, rs=rs: e.activation(
                    out=stat[:, rs:rs + 1], in_=stat[:, mvc + 1:mvc + 2],
                    func=AF.Ln, bias=stat[:, 0:1], scale=1.0),
                  reads=[f'st_mv{par}', 'st_eps'], writes=[f'st_r0{par}'])
                E('scalar', lambda e, rs=rs: e.activation(out=stat[:, rs:rs + 1],
                                                          in_=stat[:, rs:rs + 1],
                                                          func=AF.Exp, scale=-0.5),
                  reads=[f'st_r0{par}'], writes=[f'st_rstd{par}'])

            def q1C(i):
                qc, ti = i // 2, i % 2
                sl = qc % 2
                par = i % 2
                cc = LNC[par]
                mvc, rs = cc['mv'], cc['rstd']
                xt = xch[sl].rearrange("p (t d) -> p t d", t=2)[:, ti, :]
                xnv = xnb[sl].rearrange("p (t d) -> p t d", t=2)[:, ti, :]
                E('vector', lambda e, xt=xt, xnv=xnv, mvc=mvc, rs=rs: e.tensor_scalar(
                    out=xnv, in0=xt, scalar1=stat[:, mvc:mvc + 1], scalar2=stat[:, rs:rs + 1],
                    op0=ALU.subtract, op1=ALU.mult),
                  reads=[f'xch{sl}', f'st_mv{par}', f'st_rstd{par}'], writes=[f'xnb{sl}'])
                xnTv = xnT[sl].rearrange("p (k c) -> p k c", k=KD)
                emit_transposes(xnv, KD,
                                lambda k0, kn, ti=ti, xnTv=xnTv: xnTv[:, k0:k0 + kn,
                                                                      128 * ti:128 * (ti + 1)],
                                f'xnT{sl}', f'xnb{sl}')

            def q1D(i):
                qc, ti = i // 2, i % 2
                sl = qc % 2
                jt = i
                xnTv = xnT[sl].rearrange("p (k c) -> p k c", k=KD)
                for seg in range(2):
                    items = [(lambda e, k=k, seg=seg, ti=ti, xnTv=xnTv: e.matmul(
                        pv(1024 + 512 * seg, 512), xnTv[:, k, 128 * ti:128 * (ti + 1)],
                        wqb.rearrange("p (k c) -> p k c", k=KD)[:, k, 512 * seg:512 * (seg + 1)],
                        start=(k == 0), stop=(k == KD - 1))) for k in range(KD)]
                    MMG(items, [f'xnT{sl}', 'wqb'], [f'bank{2 + seg}'])
                qv = qfb[sl].rearrange("p (t c) -> p t c", t=2)[:, ti, :]
                E('scalar', lambda e, qv=qv: e.copy(out=qv, in_=pv(1024, 1024)),
                  reads=['bank2', 'bank3'], writes=[f'qfb{sl}'])
                emit_rope('vector', qv, H,
                          qswp[sl].rearrange("p (t c) -> p t c", t=2)[:, ti, :],
                          qtc_v[:, 16 * jt:16 * (jt + 1)], qts_v[:, 32 * jt:32 * (jt + 1)],
                          f'qfb{sl}', f'qswp{sl}')

            def q1E(i):
                qc, ti = i // 2, i % 2
                sl = qc % 2
                jt = i
                qv = qfb[sl].rearrange("p (t c) -> p t c", t=2)[:, ti, :]
                QTv = QT.rearrange("p (m c) -> p m c", m=NQT)
                emit_transposes(qv, 8,
                                lambda k0, kn, jt=jt, QTv=QTv: QTv[:, k0, 128 * jt:128 * (jt + 1)],
                                'QT', f'qfb{sl}',
                                scale_col=lambda m: svec[:, m:m + 1])

            for step in range(8 + 5):
                if step < 8:
                    q1A(step)
                if 0 <= step - 1 < 8:
                    q1B(step - 1)
                if 0 <= step - 2 < 8:
                    q1C(step - 2)
                if 0 <= step - 3 < 8:
                    q1D(step - 3)
                if 0 <= step - 5 < 8:
                    q1E(step - 5)
            if dbg:
                for j in range(4):
                    E('gpsimd', lambda e, j=j: e.dma_start(out=DBG['d_kt'][j], in_=KT[j][:]),
                      reads=[f'KT{j}'], dma=True, slot='dbg')
                E('gpsimd', lambda e: e.dma_start(out=DBG['d_v2'], in_=V2[:]), reads=['V2'], dma=True, slot='dbg')
                E('gpsimd', lambda e: e.dma_start(out=DBG['d_qt'], in_=QT[:]), reads=['QT'], dma=True, slot='dbg')

            pr.barrier()
            QTv = QT.rearrange("p (m c) -> p m c", m=NQT)
            OTv = OT.rearrange("p (m c) -> p m c", m=NQT)
            V2v = V2.rearrange("p (t c) -> p t c", t=NKT)
            E('gpsimd', lambda e: e.dma_start(out=wob.rearrange("p (k c) -> p k c", k=8),
                                              in_=IN['wo'].rearrange("k p c -> p k c")),
              dma=True, slot='wob', writes=['wob'])
            flat = [(j, h) for j in range(NQT) for h in range(H)]
            NH = len(flat)

            def st1(i):
                j, h = flat[i]
                w = WJ[j]
                kv = h // 4
                m = h // 2
                half = h % 2
                sA = i % 2
                sbase = sA * 1536
                sbanks = [f'bank{b}' for b in range(sA * 3, sA * 3 + (w + 511) // 512)]
                if h == 0:
                    bs = j % 2
                    E('sync', lambda e, j=j, bs=bs, w=w: e.dma_start(
                        out=bmv[bs][:, 0:w], in_=IN['bmask'][j][:, 0:w]),
                      dma=True, slot=f'bm{bs}', writes=[f'bmv{bs}'])
                segs = [(o, min(512, w - o)) for o in range(0, w, 512)]
                items = [(lambda e, o=o, ww=ww, m=m, half=half, kv=kv, j=j, sbase=sbase: e.matmul(
                    pv(sbase + o, ww), QTv[64 * half:64 * (half + 1), m, 128 * j:128 * (j + 1)],
                    KT[kv][64 * half:64 * (half + 1), o:o + ww],
                    start=True, stop=True)) for (o, ww) in segs]
                MMG(items, ['QT', f'KT{kv}'], sbanks)

            def st2(i):
                j, h = flat[i]
                w = WJ[j]
                sA = i % 2
                sbase = sA * 1536
                hp = i % 2
                bs = j % 2
                sbanks = [f'bank{b}' for b in range(sA * 3, sA * 3 + (w + 511) // 512)]
                E('vector', lambda e, sbase=sbase, hp=hp, bs=bs, w=w: e.tensor_tensor(
                    out=negS[hp][:, 0:w], in0=pv(sbase, w), in1=bmv[bs][:, 0:w], op=ALU.min),
                  reads=sbanks + [f'bmv{bs}'], writes=[f'negS{hp}'])
                E('vector', lambda e, hp=hp, w=w: e.tensor_scalar(
                    out=negS[hp][:, 0:w], in0=negS[hp][:, 0:w], scalar1=-1.0, scalar2=3e38,
                    op0=ALU.mult, op1=ALU.min, accum_out=stat[:, 2 + hp:3 + hp]),
                  reads=[f'negS{hp}'], writes=[f'negS{hp}', f'st_negmax{hp}'])
                E('scalar', lambda e, hp=hp, w=w: e.activation(
                    out=Pb[hp][:, 0:w], in_=negS[hp][:, 0:w], func=AF.Exp,
                    bias=stat[:, 2 + hp:3 + hp], scale=-1.0, accum_out=stat[:, 4 + hp:5 + hp]),
                  reads=[f'negS{hp}', f'st_negmax{hp}'], writes=[f'Pb{hp}', f'st_rowsum{hp}'])
            def st3a(i):
                j, h = flat[i]
                w = WJ[j]
                hp = i % 2
                pq = i % 3
                kts = kt_list(w)
                E('vector', lambda e, hp=hp: e.reciprocal(stat[:, 6 + hp:7 + hp],
                                                          stat[:, 4 + hp:5 + hp]),
                  reads=[f'st_rowsum{hp}'], writes=[f'st_recip{hp}'])
                E('vector', lambda e, hp=hp: e.tensor_scalar(
                    out=diagb[hp], in0=identb[:], scalar1=stat[:, 6 + hp:7 + hp], scalar2=None,
                    op0=ALU.mult),
                  reads=['identb', f'st_recip{hp}'], writes=[f'diag{hp}'])
                for k0 in range(0, len(kts), 4):
                    kn = min(4, len(kts) - k0)
                    items = [(lambda e, kt=k0 + ii, ii=ii, hp=hp, kts=kts: e.matmul(
                        pv(6 * 512 + 128 * ii, 128, 0, kts[kt]),
                        Pb[hp][:, 128 * kt:128 * kt + kts[kt]], diagb[hp],
                        start=True, stop=True)) for ii in range(kn)]
                    MMG(items, [f'Pb{hp}', f'diag{hp}'], ['bank6'])
                    E('scalar', lambda e, k0=k0, kn=kn, pq=pq: e.copy(
                        out=PTs[pq][:, 128 * k0:128 * (k0 + kn)], in_=pv(6 * 512, 128 * kn)),
                      reads=['bank6'], writes=[f'PTs{pq}'])
                if dbg and j == 0 and h == 0:
                    E('gpsimd', lambda e, hp=hp: e.dma_start(out=DBG['d_p'][:, 0:WJ[0]],
                                                             in_=Pb[hp][:, 0:WJ[0]]),
                      reads=[f'Pb{hp}'], dma=True, slot='dbg')

            def st3b(i):
                j, h = flat[i]
                w = WJ[j]
                kv = h // 4
                m = h // 2
                half = h % 2
                pq = i % 3
                kts = kt_list(w)
                items = [(lambda e, kt=kt, kv=kv, pq=pq, kts=kts: e.matmul(
                    pv(7 * 512, 128, 0, 64),
                    V2v[0:kts[kt], kt, 64 * kv:64 * (kv + 1)],
                    PTs[pq][0:kts[kt], 128 * kt:128 * (kt + 1)],
                    start=(kt == 0), stop=(kt == len(kts) - 1))) for kt in range(len(kts))]
                MMG(items, ['V2', f'PTs{pq}'], ['bank7'])
                E('scalar', lambda e, m=m, half=half, j=j: e.copy(
                    out=OTv[64 * half:64 * (half + 1), m, 128 * j:128 * (j + 1)],
                    in_=pv(7 * 512, 128, 0, 64)),
                  reads=['bank7'], writes=['OT'])

            for step in range(NH + 3):
                if step < NH:
                    st1(step)
                if 0 <= step - 1 < NH:
                    st2(step - 1)
                if 0 <= step - 2 < NH:
                    st3a(step - 2)
                if 0 <= step - 3 < NH:
                    st3b(step - 3)
            if dbg:
                E('gpsimd', lambda e: e.dma_start(out=DBG['d_ot'], in_=OT[:]), reads=['OT'], dma=True, slot='dbg')

            pr.barrier()
            wov = wob.rearrange("p (k c) -> p k c", k=8)
            xn2v = xn2.rearrange("p (t d) -> p t d", t=NQT)
            xn2lov = xn2lo.rearrange("p (t d) -> p t d", t=NQT)
            x1evs = []
            for tt in range(NQT):
                sl = tt % 2
                E('sync', lambda e, tt=tt, sl=sl: e.dma_start(
                    out=xqt[sl], in_=IN['xq'][128 * tt:128 * (tt + 1)]),
                  dma=True, slot=f'xq{sl}', writes=[f'xqt{sl}'])
                for seg in range(4):
                    pb = seg % 2
                    items = [(lambda e, k=k, seg=seg, tt=tt, pb=pb: e.matmul(
                        pv(512 * pb, 512), OTv[:, k, 128 * tt:128 * (tt + 1)],
                        wov[:, k, 512 * seg:512 * (seg + 1)],
                        start=(k == 0), stop=(k == 7))) for k in range(8)]
                    MMG(items, ['OT', 'wob'], [f'bank{pb}'])
                    E('vector', lambda e, seg=seg, sl=sl, pb=pb: e.scalar_tensor_tensor(
                        out=x1t[sl][:, 512 * seg:512 * (seg + 1)], in0=pv(512 * pb, 512),
                        scalar=1.0, in1=xqt[sl][:, 512 * seg:512 * (seg + 1)],
                        op0=ALU.mult, op1=ALU.add),
                      reads=[f'bank{pb}', f'xqt{sl}'], writes=[f'x1t{sl}'])
                ev = E('sync', lambda e, tt=tt, sl=sl: e.dma_start(
                    out=x1sp[128 * tt:128 * (tt + 1)], in_=x1t[sl]),
                  reads=[f'x1t{sl}'], dma=True, slot=f'x1s{sl}')
                x1evs.append(ev)
                emit_ln(x1t[sl], xn2f[sl], [f'x1t{sl}'], f'xn2f{sl}', par=sl)
                E('scalar', lambda e, tt=tt, sl=sl: e.copy(out=xn2v[:, tt, :], in_=xn2f[sl]),
                  reads=[f'xn2f{sl}'], writes=['xn2'])
                E('vector', lambda e, tt=tt, sl=sl: e.tensor_tensor(
                    out=xn2lov[:, tt, :], in0=xn2f[sl], in1=xn2v[:, tt, :], op=ALU.subtract),
                  reads=[f'xn2f{sl}', 'xn2'], writes=['xn2lo'])
            if dbg:
                E('sync', lambda e: e.dma_start(out=DBG['d_x1'], in_=x1sp), extra=x1evs, dma=True, slot='dbg')
                for tt in range(NQT):
                    E('gpsimd', lambda e, tt=tt: e.dma_start(
                        out=DBG['d_xn2'][128 * tt:128 * (tt + 1)], in_=xn2v[:, tt, :]),
                      reads=['xn2'], dma=True, slot='dbg')

            pr.barrier()
            xn2Tv = xn2T.rearrange("p (k c) -> p k c", k=KD)
            xn2sTv = xn2sT.rearrange("p (k c) -> p k c", k=KD)
            wghv = wgh.rearrange("p (k c) -> p k c", k=KD)
            wglv = wgl.rearrange("p (k c) -> p k c", k=KD)
            for tt in range(NQT):
                sl = tt % 2
                emit_transposes(xn2v[:, tt, :], KD,
                                lambda k0, kn, tt=tt: xn2Tv[:, k0:k0 + kn, 128 * tt:128 * (tt + 1)],
                                'xn2T', 'xn2')
                xloTv = xloT[sl].rearrange("p (k c) -> p k c", k=KD)
                emit_transposes(xn2lov[:, tt, :], KD,
                                lambda k0, kn, sl=sl: xloT[sl].rearrange(
                                    "p (k c) -> p k c", k=KD)[:, k0:k0 + kn, :].squeeze(),
                                f'xloT{sl}', 'xn2lo')
                gbank = f'bank{6 + (tt * 128) // 512 % 2}'
                items = []
                for (wv_, xv_) in [(wghv, 'T'), (wglv, 'T'), (wghv, 'L')]:
                    for k in range(KD):
                        def mk(e, k=k, wv_=wv_, xv_=xv_, tt=tt, sl=sl):
                            rhs = (xn2Tv[:, k, 128 * tt:128 * (tt + 1)] if xv_ == 'T'
                                   else xloT[sl].rearrange("p (k c) -> p k c", k=KD)[:, k, :])
                            return e.matmul(pv(3072 + 128 * tt, 128, 0, 8), wv_[:, k, :], rhs,
                                            start=(k == 0 and xv_ == 'T' and wv_ is wghv),
                                            stop=(k == KD - 1 and xv_ == 'L'))
                        items.append(mk)
                MMG(items, ['wgh', 'wgl', 'xn2T', f'xloT{sl}'], [gbank])
            E('scalar', lambda e: e.copy(out=logitsT[0:8, :], in_=pv(3072, 1024, 0, 8)),
              reads=['bank6', 'bank7'], writes=['logitsT'])
            for j in range(NQT):
                E('tensor', lambda e, j=j: e.matmul(
                    pv(3072, 8), logitsT[0:8, 128 * j:128 * (j + 1)], identb[0:8, 0:8],
                    start=True, stop=True),
                  reads=['logitsT', 'identb'], writes=['bank6'])
                E('vector', lambda e: e.reduce_max(stat[:, 68:69], pv(3072, 8), axis=mybir.AxisListType.X),
                  reads=['bank6'], writes=['st_m1'])
                E('vector', lambda e: e.tensor_scalar(out=stat[:, 69:77], in0=pv(3072, 8),
                                                      scalar1=stat[:, 68:69], scalar2=0.0,
                                                      op0=ALU.is_equal, op1=ALU.add,
                                                      accum_out=stat[:, 77:78]),
                  reads=['bank6', 'st_m1'], writes=['st_eq', 'st_cnt'])
                E('vector', lambda e: e.scalar_tensor_tensor(out=stat[:, 69:77], in0=stat[:, 69:77],
                                                             scalar=-3e38, in1=pv(3072, 8),
                                                             op0=ALU.mult, op1=ALU.add),
                  reads=['st_eq', 'bank6'], writes=['st_eq'])
                E('vector', lambda e: e.reduce_max(stat[:, 78:79], stat[:, 69:77],
                                                   axis=mybir.AxisListType.X),
                  reads=['st_eq'], writes=['st_m2p'])
                E('vector', lambda e: e.tensor_scalar(out=stat[:, 79:80], in0=stat[:, 77:78],
                                                      scalar1=1.5, scalar2=None, op0=ALU.is_ge),
                  reads=['st_cnt'], writes=['st_im'])
                E('vector', lambda e: e.tensor_tensor(out=stat[:, 80:81], in0=stat[:, 68:69],
                                                      in1=stat[:, 78:79], op=ALU.subtract),
                  reads=['st_m1', 'st_m2p'], writes=['st_d'])
                E('vector', lambda e: e.tensor_tensor(out=stat[:, 80:81], in0=stat[:, 79:80],
                                                      in1=stat[:, 80:81], op=ALU.mult),
                  reads=['st_im', 'st_d'], writes=['st_d'])
                E('vector', lambda e: e.tensor_tensor(out=stat[:, 80:81], in0=stat[:, 78:79],
                                                      in1=stat[:, 80:81], op=ALU.add),
                  reads=['st_m2p', 'st_d'], writes=['st_d'])
                E('vector', lambda e, j=j: e.tensor_tensor(out=stat[:, 82 + j:83 + j],
                                                           in0=stat[:, 68:69],
                                                           in1=stat[:, 80:81], op=ALU.add),
                  reads=['st_m1', 'st_d'], writes=[f'st_sc{j}'])
                E('vector', lambda e, j=j: e.tensor_scalar(out=sdiag[j], in0=identb[:],
                                                           scalar1=stat[:, 82 + j:83 + j],
                                                           scalar2=None, op0=ALU.mult),
                  reads=['identb', f'st_sc{j}'], writes=[f'sdiag{j}'])
            if dbg:
                E('sync', lambda e: e.dma_start(out=DBG['d_scale'], in_=stat[:, 82:90],
                                                allow_slow_non_contiguous=True),
                  reads=[f'st_sc{j}' for j in range(8)], dma=True, slot='dbg')
            pr.barrier()
            for tt in range(NQT):
                emit_transposes(xn2v[:, tt, :], KD,
                                lambda k0, kn, tt=tt: xn2sTv[:, k0:k0 + kn, 128 * tt:128 * (tt + 1)],
                                'xn2sT', 'xn2', rhs=sdiag[tt], rhsbuf=f'sdiag{tt}')

            pr.barrier()
            if dbg:
                E('gpsimd', lambda e: e.dma_start(out=DBG['d_xs'], in_=xn2sT[:]),
                  reads=['xn2sT'], dma=True, slot='dbg')
                for j in range(NQT):
                    E('gpsimd', lambda e, j=j: e.dma_start(
                        out=DBG['d_sd'][:, 128 * j:128 * (j + 1)], in_=sdiag[j]),
                      reads=[f'sdiag{j}'], dma=True, slot='dbg')
            ms_evs = []
            for dm in range(KD):
                sl = dm % 2
                E('gpsimd', lambda e, dm=dm, sl=sl: e.dma_start(
                    out=web[sl].rearrange("p (k c) -> p k c", k=KD),
                    in_=IN['we'][dm],
                    max_dma_last_dim=8192),
                  dma=True, slot=f'we{sl}', writes=[f'web{sl}'])
                if dbg:
                    E('gpsimd', lambda e, dm=dm, sl=sl: e.dma_start(
                        out=DBG['d_we'][dm], in_=web[sl][:]),
                      reads=[f'web{sl}'], dma=True, slot='dbg')
                for seg in range(2):
                    pb = 2 * sl + seg
                    items = [(lambda e, k=k, seg=seg, sl=sl, pb=pb: e.matmul(
                        pv(512 * pb, 512), web[sl].rearrange("p (k c) -> p k c", k=KD)[:, k, :],
                        xn2sTv[:, k, 512 * seg:512 * (seg + 1)],
                        start=(k == 0), stop=(k == KD - 1))) for k in range(KD)]
                    moe_last_mm = MMG(items, [f'web{sl}', 'xn2sT'], [f'bank{pb}'])
                E('scalar', lambda e, sl=sl: e.copy(out=moer_m[sl], in_=pv(1024 * sl, 1024)),
                  reads=[f'bank{2 * sl}', f'bank{2 * sl + 1}'], writes=[f'moerm{sl}'])
                if dbg:
                    E('sync', lambda e, dm=dm, sl=sl: e.dma_start(out=DBG['d_mr'][dm], in_=moer_m[sl]),
                      reads=[f'moerm{sl}'], dma=True, slot='dbg')
                msev = E('sync', lambda e, dm=dm, sl=sl: e.dma_start(out=moesp[dm], in_=moer_m[sl]),
                  reads=[f'moerm{sl}'], dma=True, slot=f'ms{sl}')
                ms_evs.append(msev)

            hv = hbuf.rearrange("p (m c) -> p m c", m=NM_UP)
            for m in range(NM_UP):
                sl = m % 2
                E('gpsimd', lambda e, m=m, sl=sl: e.dma_start(
                    out=wub[sl].rearrange("p (k c) -> p k c", k=KD),
                    in_=IN['wu'][m], max_dma_last_dim=8192),
                  extra=[moe_last_mm] if m < 2 else (), dma=True, slot=f'wu{sl}',
                  writes=[f'wub{sl}'])
                for seg in range(2):
                    pb = 2 * sl + seg
                    items = [(lambda e, k=k, seg=seg, sl=sl, pb=pb: e.matmul(
                        pv(512 * pb, 512), wub[sl].rearrange("p (k c) -> p k c", k=KD)[:, k, :],
                        xn2Tv[:, k, 512 * seg:512 * (seg + 1)],
                        start=(k == 0), stop=(k == KD - 1))) for k in range(KD)]
                    up_last_mm = MMG(items, [f'wub{sl}', 'xn2T'], [f'bank{pb}'])
                E('scalar', lambda e, m=m, sl=sl: e.activation(
                    out=hv[:, m, :], in_=pv(1024 * sl, 1024), func=AF.Relu),
                  reads=[f'bank{2 * sl}', f'bank{2 * sl + 1}'], writes=['hbuf'],
                  extra=[ms_evs[-1]] if m == 0 else ())
                E('vector', lambda e, m=m: e.tensor_tensor(out=hv[:, m, :], in0=hv[:, m, :],
                                                           in1=hv[:, m, :], op=ALU.mult),
                  reads=['hbuf'], writes=['hbuf'])
            if dbg:
                E('gpsimd', lambda e: e.dma_start(out=DBG['d_h'], in_=hv[:, 0, :]),
                  reads=['hbuf'], dma=True, slot='dbg')

            ybv = ybuf.rearrange("p (m c) -> p m c", m=KD)
            for dm in range(KD):
                sl = dm % 2
                E('sync', lambda e, dm=dm: e.dma_start(out=moer[0], in_=moesp[dm]),
                  extra=[ms_evs[dm]] + (ms_evs[-2:] if dm == 0 else []),
                  dma=True, slot='mr0', writes=['moer0'])
                for hk in range(2):
                    wsl = 2 * (dm % 2) + hk
                    E('gpsimd', lambda e, dm=dm, hk=hk, wsl=wsl: e.dma_start(
                        out=wdb[wsl].rearrange("p (k c) -> p k c", k=32),
                        in_=IN['wd'][dm, :, 32 * hk:32 * (hk + 1), :],
                        max_dma_last_dim=8192),
                      extra=[up_last_mm] if dm < 2 else (), dma=True, slot=f'wd{wsl}',
                      writes=[f'wdb{wsl}'])
                for seg in range(2):
                    pb = 2 * sl + seg
                    items = []
                    for kp in range(NM_UP):
                        wsl = 2 * (dm % 2) + kp // 32
                        items.append(lambda e, kp=kp, wsl=wsl, seg=seg, pb=pb: e.matmul(
                            pv(512 * pb, 512),
                            wdb[wsl].rearrange("p (k c) -> p k c", k=32)[:, kp % 32, :],
                            hv[0:128, kp, 512 * seg:512 * (seg + 1)],
                            start=(kp == 0), stop=(kp == NM_UP - 1)))
                    dn_last_mm = MMG(items, [f'wdb{2 * (dm % 2)}', f'wdb{2 * (dm % 2) + 1}', 'hbuf'],
                                     [f'bank{pb}'])
                for seg in range(2):
                    pb = 2 * sl + seg
                    E('vector', lambda e, dm=dm, seg=seg, pb=pb: e.scalar_tensor_tensor(
                        out=ybv[:, dm, 512 * seg:512 * (seg + 1)], in0=pv(512 * pb, 512),
                        scalar=1.0, in1=moer[0][:, 512 * seg:512 * (seg + 1)],
                        op0=ALU.mult, op1=ALU.add),
                      reads=[f'bank{pb}', 'moer0'], writes=['ybuf'])

            if dbg:
                E('gpsimd', lambda e: e.dma_start(out=DBG['d_yb'], in_=ybuf[:]),
                  reads=['ybuf'], dma=True, slot='dbg')
                E('sync', lambda e: e.dma_start(out=DBG['d_moe'], in_=moesp), extra=ms_evs, dma=True, slot='dbg')
            ytv = ytok.rearrange("p (t d) -> p t d", t=NQT)
            dn_dep = [dn_last_mm]
            step = 0
            for t0 in range(0, NQT, 4):
                for dm in range(KD):
                    pbank = 4 + step % 2
                    step += 1
                    items = [(lambda e, tt=t0 + i, i=i, dm=dm, pbank=pbank: e.matmul(
                        pv(pbank * 512 + 128 * i, 128), ybv[:, dm, 128 * tt:128 * (tt + 1)],
                        identb[:], start=True, stop=True)) for i in range(4)]
                    MMG(items, ['ybuf', 'identb'], [f'bank{pbank}'])
                    E('scalar', lambda e, t0=t0, dm=dm, pbank=pbank: e.copy(
                        out=ytv[:, t0:t0 + 4, 128 * dm:128 * (dm + 1)], in_=pv(pbank * 512, 512)),
                      reads=[f'bank{pbank}'], writes=[f'ytok{t0 + i}' for i in range(4)],
                      extra=dn_dep)
                    dn_dep = ()
            if dbg:
                for tt in range(NQT):
                    E('sync', lambda e, tt=tt: e.dma_start(
                        out=DBG['d_y'][128 * tt:128 * (tt + 1)], in_=ytv[:, tt, :]),
                      reads=[f'ytok{tt}'], dma=True, slot='dbg')

            E('sync', lambda e: e.dma_start(out=gamv, in_=IN['gammab']), dma=True, slot='gam',
              writes=['gamv'], extra=[dn_last_mm])
            E('sync', lambda e: e.dma_start(out=betv, in_=IN['betab']), dma=True, slot='bet',
              writes=['betv'])
            outevs = []

            def t_sA(tt):
                par = tt % 2
                c = LNC[par]
                bn, mvc, rs = c['bn'], c['mv'], c['rstd']
                for s in range(4):
                    E('vector', lambda e, s=s, tt=tt, bn=bn: e.bn_stats(
                        stat[:, bn + 6 * s:bn + 6 * s + 6], ytv[:, tt, 512 * s:512 * (s + 1)]),
                      reads=[f'ytok{tt}'], writes=[f'st_bn{par}'])
                E('vector', lambda e, bn=bn, mvc=mvc: e.bn_aggr(stat[:, mvc:mvc + 2],
                                                                stat[:, bn:bn + 24]),
                  reads=[f'st_bn{par}'], writes=[f'st_mv{par}'])
                E('scalar', lambda e, mvc=mvc, rs=rs: e.activation(
                    out=stat[:, rs:rs + 1], in_=stat[:, mvc + 1:mvc + 2],
                    func=AF.Ln, bias=stat[:, 0:1], scale=1.0),
                  reads=[f'st_mv{par}', 'st_eps'], writes=[f'st_r0{par}'])
                E('scalar', lambda e, rs=rs: e.activation(out=stat[:, rs:rs + 1],
                                                          in_=stat[:, rs:rs + 1],
                                                          func=AF.Exp, scale=-0.5),
                  reads=[f'st_r0{par}'], writes=[f'st_rstd{par}'])

            def t_sB(tt):
                par = tt % 2
                sl = tt % 2
                c = LNC[par]
                mvc, rs = c['mv'], c['rstd']
                E('sync', lambda e, tt=tt, sl=sl: e.dma_start(
                    out=x1r[sl], in_=x1sp[128 * tt:128 * (tt + 1)]),
                  extra=list(x1evs) + [dn_last_mm], dma=True, slot=f'x1r{sl}',
                  writes=[f'x1r{sl}'])
                E('vector', lambda e, tt=tt, mvc=mvc, rs=rs: e.tensor_scalar(
                    out=ytv[:, tt, :], in0=ytv[:, tt, :],
                    scalar1=stat[:, mvc:mvc + 1], scalar2=stat[:, rs:rs + 1],
                    op0=ALU.subtract, op1=ALU.mult),
                  reads=[f'ytok{tt}', f'st_mv{par}', f'st_rstd{par}'], writes=[f'ytok{tt}'])
                E('gpsimd', lambda e, tt=tt: e.tensor_tensor(out=ytv[:, tt, :], in0=ytv[:, tt, :],
                                                             in1=gamv, op=ALU.mult),
                  reads=[f'ytok{tt}', 'gamv'], writes=[f'ytok{tt}'])

            def t_sC(tt):
                sl = tt % 2
                E('vector', lambda e, tt=tt: e.tensor_tensor(out=ytv[:, tt, :], in0=ytv[:, tt, :],
                                                             in1=betv, op=ALU.add),
                  reads=[f'ytok{tt}', 'betv'], writes=[f'ytok{tt}'])
                E('vector', lambda e, tt=tt, sl=sl: e.tensor_tensor(
                    out=outt[sl], in0=ytv[:, tt, :], in1=x1r[sl], op=ALU.add),
                  reads=[f'ytok{tt}', f'x1r{sl}'], writes=[f'outt{sl}'])
                ev = E('sync', lambda e, tt=tt, sl=sl: e.dma_start(
                    out=out[128 * tt:128 * (tt + 1)], in_=outt[sl]),
                  reads=[f'outt{sl}'], dma=True, slot=f'out{tt % 2}')
                outevs.append(ev)

            for i in range(NQT + 2):
                if i < NQT:
                    t_sA(i)
                if 1 <= i <= NQT:
                    t_sB(i - 1)
                if i >= 2:
                    t_sC(i - 2)
            pr.final_wait('sync', outevs)
            pr.emit(block)
    return nc



_CACHED = {}


def _get_nc(dbg=False):
    if dbg not in _CACHED:
        _CACHED[dbg] = build_program(dbg=dbg)
    return _CACHED[dbg]


def kernel(**inputs):
    W = {}
    for k, v in inputs.items():
        if k == 'L0_attn_x':
            x_full = np.asarray(v, np.float32)
        elif k.startswith('L0_attn_'):
            W[k[len('L0_attn_'):]] = np.asarray(v)
        else:
            W[k[len('L0_ffn_'):]] = np.asarray(v)
    shared = host_shared_weights(W)
    in_maps = []
    qposs = []
    for core in range(8):
        d, qpos = host_core_inputs(core, x_full, shared)
        in_maps.append(d)
        qposs.append(qpos)
    nc = _get_nc(dbg=False)
    res = run_bass_kernel_spmd(nc, in_maps, core_ids=list(range(8)))
    outp = np.zeros((B, T, D), np.float32)
    for core in range(8):
        outp[core // 4, qposs[core]] = np.asarray(res.results[core]["out"])
    return outp
